# version 26
# baseline (speedup 1.0000x reference)
import numpy as np
import ml_dtypes
import concourse.bass as bass
import concourse.mybir as mybir
from concourse.bass_utils import run_bass_kernel_spmd

F32 = mybir.dt.float32
BF16 = mybir.dt.bfloat16
AF = mybir.ActivationFunctionType
ALU = mybir.AluOpType
AX = mybir.AxisListType

D = 2048
NK = 16
INTER = 5632
NI = 44
T_OWN = 4096
T_BATCH = 16384
NTO = T_OWN // 128
EPS = 1e-5
RG = [[0, 1, 2, 3], [4, 5, 6, 7]]

BASE = 16384 + 4096


def build_nc():
    nc = bass.Bass("TRN2", target_bir_lowering=False)

    x_h = nc.dram_tensor("x", [T_OWN, D], F32, kind="ExternalInput")
    wqk_h = nc.dram_tensor("wqk", [NK, 128, D], F32, kind="ExternalInput")
    wv_h = nc.dram_tensor("wv", [NK, 128, D], F32, kind="ExternalInput")
    wu_h = nc.dram_tensor("wu", [NK, D, 128], F32, kind="ExternalInput")
    wo_h = nc.dram_tensor("wo", [NK, 128, D], F32, kind="ExternalInput")
    gw_h = nc.dram_tensor("gatew", [NI, D, 128], F32, kind="ExternalInput")
    uw_h = nc.dram_tensor("upw", [NI, D, 128], F32, kind="ExternalInput")
    dw_h = nc.dram_tensor("downw", [NI, 128, D], F32, kind="ExternalInput")
    bqk_h = nc.dram_tensor("bqk", [1, D], F32, kind="ExternalInput")
    bv_h = nc.dram_tensor("bv", [1, D], F32, kind="ExternalInput")
    bo_h = nc.dram_tensor("bo", [1, D], F32, kind="ExternalInput")
    bu_h = nc.dram_tensor("bu", [128, NK], F32, kind="ExternalInput")
    lng_h = nc.dram_tensor("lng", [128, NK], F32, kind="ExternalInput")
    lnb_h = nc.dram_tensor("lnb", [128, NK], F32, kind="ExternalInput")
    gs_h = nc.dram_tensor("gs", [128, NI], F32, kind="ExternalInput")
    us_h = nc.dram_tensor("us", [128, NI], F32, kind="ExternalInput")
    ds_h = nc.dram_tensor("ds", [128, D], F32, kind="ExternalInput")
    id_h = nc.dram_tensor("ident", [128, 128], BF16, kind="ExternalInput")

    kv_cc_in = nc.dram_tensor("kv_cc_in", [128, D], F32, kind="Internal")
    kv_cc_out = nc.dram_tensor("kv_cc_out", [128, D], F32, kind="Internal")
    z_s = nc.dram_tensor("z_s", [NK, 128, T_OWN], BF16, kind="Internal")
    wqk_bf_s = nc.dram_tensor("wqk_bf_s", [NK, 128, D], BF16, kind="Internal")
    wv_bf_s = nc.dram_tensor("wv_bf_s", [NK, 128, D], BF16, kind="Internal")
    dw_bf_s = nc.dram_tensor("dw_bf_s", [NI, 128, D], BF16, kind="Internal")
    q_s = nc.dram_tensor("q_s", [T_OWN, D], BF16, kind="Internal")
    u_s = nc.dram_tensor("u_s", [NK, 128, T_OWN], F32, kind="Internal")
    hdn_s = nc.dram_tensor("hdn_s", [NI, 128, T_OWN], BF16, kind="Internal")
    ffn_s = nc.dram_tensor("ffn_s", [T_OWN, D], F32, kind="Internal")

    out_h = nc.dram_tensor("out", [T_OWN, D], F32, kind="ExternalOutput")

    off = [BASE]
    TOP = 229376 - 32

    def sb(name, shape, dtype, n=1, at=None):
        o = off if at is None else at
        per = int(np.prod(shape[1:])) * mybir.dt.size(dtype)
        per = (per + 31) // 32 * 32
        hs = []
        for i in range(n):
            hs.append(nc.alloc_sbuf_tensor_at(f"{name}{i if n > 1 else ''}",
                                              shape, dtype, offset=o[0]))
            o[0] += per
        assert o[0] <= TOP, f"sbuf overflow at {name}: {o[0]}"
        return hs if n > 1 else hs[0]

    ident = sb("ident_sb", [128, 128], BF16)
    ones_row_bf = sb("ones_row_bf", [1, 128], BF16)
    ones_col_bf = sb("ones_col_bf", [128, 1], BF16)
    ones_row_f = sb("ones_row_f", [1, 128], F32)
    cells = sb("cells", [128, 8], F32)
    bqk_row = sb("bqk_row", [1, D], BF16)
    bv_row = sb("bv_row", [1, D], BF16)
    bu_fm = sb("bu_fm", [128, NK], F32)
    lng_fm = sb("lng_fm", [128, NK], F32)
    lnb_fm = sb("lnb_fm", [128, NK], F32)
    gs_fm = sb("gs_fm", [128, NI], F32)
    us_fm = sb("us_fm", [128, NI], F32)
    kv_sb = sb("kv_sb", [128, D], F32)
    kv_bf = sb("kv_bf", [128, D], BF16)
    smalls = sb("smalls", [128, 96], F32)
    mean_row = sb("mean_row", [1, 512], F32)
    rstd_row = sb("rstd_row", [1, 512], F32)

    PB = off[0]

    XN_OFF = TOP - NK * T_OWN * 2
    xn_own = nc.alloc_sbuf_tensor_at("xn_own", [128, NK * T_OWN], BF16,
                                     offset=XN_OFF)
    xn_v = xn_own.rearrange("p (k t) -> p k t", k=NK)

    def region(tensors, top=None):
        o = [PB]
        out = {}
        for name, shape, dtype, n in tensors:
            out[name] = sb(name, shape, dtype, n, at=o)
        if top is not None:
            assert o[0] <= top, f"region overflow: {o[0]} > {top}"
        return out

    r0 = region([
        ("xbuf", [128, D], F32, 3),
        ("xn_tm", [128, D], BF16, 3),
        ("pc0_f32", [128, D], F32, 1),
        ("pc0_bf", [128, D], BF16, 1),
    ], top=XN_OFF)
    r1 = region([
        ("wq_bf", [128, NK * 512], BF16, 1),
        ("wv_bf", [128, NK * 512], BF16, 1),
        ("wstage", [128, 512], F32, 2),
        ("q_tm", [128, 512], BF16, 2),
        ("v_tm", [128, 512], BF16, 2),
        ("e_buf", [128, 512], F32, 2),
        ("r_buf", [128, 512], F32, 2),
    ], top=XN_OFF)
    row_stage = r0["xbuf"][0][0:1, :]
    r2 = region([
        ("wstg2", [128, D], F32, 2),
        ("gw_bf", [128, D], BF16, 2),
        ("uw_bf", [128, D], BF16, 2),
        ("sg_buf", [128, 512], F32, 2),
        ("u_raw", [128, 512], F32, 2),
        ("hdn_buf", [128, 512], BF16, 2),
        ("u_buf", [128, 512], F32, 2),
        ("pc3_bf", [128, 512], BF16, 1),
    ], top=XN_OFF)
    r4 = region([
        ("dw_bf", [128, NI * 1024], BF16, 1),
        ("hdn_t", [128, NI * 128], BF16, 2),
        ("ffn_buf", [128, 1024], F32, 2),
        ("dstg", [128, 1024], F32, 2),
        ("ds_repl", [128, D], F32, 1),
    ])
    ds_repl = r4["ds_repl"]
    r5 = region([
        ("qtile", [128, D], BF16, 2),
        ("qfm", [128, NK * 256], BF16, 2),
        ("outst", [128, NK * 256], F32, 2),
        ("sq_bf", [128, NK * 256], BF16, 2),
        ("kvscA", [128, NK * 64], F32, 1),
        ("kvscB", [128, NK * 32], F32, 1),
        ("kvs_f", [128, NK], F32, 1),
        ("kvs_bf", [128, NK], BF16, 1),
        ("zfm", [128, NK * 256], BF16, 2),
        ("u_tile", [128, NK * 256], F32, 2),
        ("mean_sb", [128, 256], F32, 2),
        ("rstd_sb", [128, 256], F32, 2),
        ("t1_b", [128, 256], F32, 2),
        ("t2_b", [128, 256], F32, 2),
        ("t3_b", [128, 256], F32, 2),
    ])
    r6 = region([
        ("wo_bf", [128, NK * D], BF16, 1),
        ("wstg6", [128, D], F32, 2),
        ("ztile", [128, NK * 128], BF16, 2),
        ("x_t", [128, D], F32, 2),
        ("ffn_t", [128, D], F32, 2),
        ("xf", [128, D], F32, 2),
        ("outb", [128, D], F32, 2),
        ("bo_row", [1, D], BF16, 1),
    ])
    bo_row = r6["bo_row"]

    pbA = nc.alloc_psum_tensor("pbA", [128, 2048], F32)
    pbt = nc.alloc_psum_tensor("pbt", [128, 2048], BF16)
    pb6 = nc.alloc_psum_tensor("pb6", [128, 512], F32)
    pb7 = nc.alloc_psum_tensor("pb7", [128, 512], F32)

    class Sem:
        def __init__(s, name):
            s.h = nc.alloc_semaphore(name)
            s.n = 0

    def sems(prefix, n):
        return [Sem(f"{prefix}{i}") for i in range(n)]

    sPE, sV, sS, sPool, sBAR = (Sem(x) for x in ("sPE", "sV", "sS", "sPool", "sBAR"))
    sC = Sem("sC")
    sX = sems("sX", 3)
    sWS = sems("sWS", 2)
    sW2 = sems("sW2", 2)
    sDS = sems("sDS", 2)
    sHT = sems("sHT", 2)
    sQT = sems("sQT", 2)
    sUT = sems("sUT", 2)
    sXT = sems("sXT", 2)
    sFT = sems("sFT", 2)
    sSQ = sems("sSQ", 2)
    sSU = sems("sSU", 2)
    sSH = sems("sSH", 2)
    sSF = sems("sSF", 2)
    sSO = sems("sSO", 2)
    sKV = Sem("sKV")
    sCC = Sem("sCC")
    sSZ = sems("sSZ", 2)
    sZT = sems("sZT", 2)
    sPC = Sem("sPC")
    sPCS = Sem("sPCS")
    sPC3 = Sem("sPC3")
    sPS3 = Sem("sPS3")
    LOAD_SEMS = [sC, *sX, *sWS, *sW2, *sDS, *sHT, *sQT, *sUT, *sXT, *sFT, *sZT,
                 sPC, sPC3]
    STORE_SEMS = [*sSQ, *sSU, *sSH, *sSF, *sSO, sKV, *sSZ, sPCS, sPS3]
    bar_n = [0]

    SP, PE, V, S, PL = nc.sync, nc.tensor, nc.vector, nc.scalar, nc.gpsimd

    def ld(sem, dst, src):
        SP.dma_start(dst, src).then_inc(sem.h, 16)
        sem.n += 16
        return sem.n

    def st(sem, dst, src):
        PL.dma_start(dst, src).then_inc(sem.h, 16)
        sem.n += 16
        return sem.n

    def barrier():
        for q in LOAD_SEMS:
            SP.wait_ge(q.h, q.n)
        for q in STORE_SEMS:
            PL.wait_ge(q.h, q.n)
        for eng in (SP, PE, V, S, PL):
            eng.drain()
            eng.nop().then_inc(sBAR.h, 1)
        bar_n[0] += 5
        for eng in (SP, PE, V, S, PL):
            eng.wait_ge(sBAR.h, bar_n[0])

    ld(sC, ident[:], id_h[:])
    ld(sC, bu_fm[:], bu_h[:])
    ld(sC, lng_fm[:], lng_h[:])
    ld(sC, lnb_fm[:], lnb_h[:])
    ld(sC, gs_fm[:], gs_h[:])
    ld(sC, us_fm[:], us_h[:])

    for ap, val in ((ones_row_bf[:], 1.0), (ones_col_bf[:], 1.0), (ones_row_f[:], 1.0),
                    (cells[:, 0:1], EPS), (cells[:, 1:2], 1.0 / D), (cells[:, 2:3], -0.5),
                    (cells[:, 3:4], -1.0),
                    (kv_sb[:], 0.0)):
        PL.memset(ap, val).then_inc(sPool.h, 1)
        sPool.n += 1

    for bh, brow in ((bqk_h, bqk_row), (bv_h, bv_row)):
        n = ld(sC, row_stage[:], bh[:])
        V.wait_ge(sC.h, n)
        V.tensor_copy(brow[:], row_stage[:]).then_inc(sV.h, 1)
        sV.n += 1
        SP.wait_ge(sV.h, sV.n)

    def load_qv_weights(sw):
        c0 = sw * 512
        marks = [0, 0]
        for wi, (wsrc, wbf) in enumerate(((wqk_bf_s, r1["wq_bf"]),
                                          (wv_bf_s, r1["wv_bf"]))):
            dst = wbf.rearrange("p (k c) -> p k c", k=NK)
            src = wsrc.rearrange("k p c -> p k c")[:, :, c0:c0 + 512]
            for p4 in range(4):
                marks[wi] = ld(sWS[wi], dst[:, p4 * 4:(p4 + 1) * 4, :],
                               src[:, p4 * 4:(p4 + 1) * 4, :])
        return marks

    w_ready = [[0, 0]]
    barrier()

    m_xld = [0] * (NTO + 4)
    m_norm = [0] * (NTO + 4)
    m_sq = [0] * (NTO + 4)
    m_tr = [0] * (NTO + 4)
    m_cpx = [0] * (NTO + 4)
    m_projg = [0] * (NTO + 4)
    m_evS = [0] * (NTO + 4)
    m_evV = [0] * (NTO + 4)
    m_kvmm = [0] * (NTO + 4)
    m_kvadd = [0] * (NTO + 4)
    m_xnst = [0] * (NTO + 4)
    m_qst = [0] * (NTO + 4)

    def reset_marks():
        for a_ in (m_xld, m_norm, m_sq, m_tr, m_cpx, m_projg, m_evS, m_evV,
                   m_kvmm, m_kvadd, m_xnst, m_qst):
            for t in range(len(a_)):
                a_[t] = 0

    m_rstd = [0] * (NTO + 4)

    def emit_ln_A(i):
        xb = r0["xbuf"][i % 3]
        lo = (i % 3) * 32
        sm = smalls
        V.wait_ge(sX[i % 3].h, m_xld[i])
        for k in range(4):
            bs_ = V.bn_stats(sm[:, lo + k * 6:lo + (k + 1) * 6],
                             xb[:, k * 512:(k + 1) * 512])
        bs_.then_inc(sV.h, 1)
        sV.n += 1
        V.wait_ge(sV.h, sV.n)
        V.bn_aggr(sm[:, lo + 24:lo + 26], sm[:, lo:lo + 24]).then_inc(sV.h, 1)
        sV.n += 1
        agg_c = sV.n
        S.wait_ge(sV.h, agg_c)
        S.activation(sm[:, lo + 26:lo + 27], sm[:, lo + 25:lo + 26], AF.Ln,
                     bias=cells[:, 0:1]).then_inc(sS.h, 1)
        sS.n += 1
        S.wait_ge(sS.h, sS.n)
        S.activation(sm[:, lo + 27:lo + 28], sm[:, lo + 26:lo + 27], AF.Exp,
                     scale=cells[:, 2:3]).then_inc(sS.h, 1)
        sS.n += 1
        S.wait_ge(sS.h, sS.n)
        S.activation(sm[:, lo + 28:lo + 29], sm[:, lo + 24:lo + 25], AF.Identity,
                     scale=sm[:, lo + 27:lo + 28]).then_inc(sS.h, 1)
        sS.n += 1
        S.wait_ge(sS.h, sS.n)
        S.activation(sm[:, lo + 28:lo + 29], sm[:, lo + 28:lo + 29], AF.Identity,
                     scale=cells[:, 3:4]).then_inc(sS.h, 1)
        sS.n += 1
        m_rstd[i] = sS.n

    def emit_ln_B(i):
        xb = r0["xbuf"][i % 3]
        xt = r0["xn_tm"][i % 3]
        lo = (i % 3) * 32
        sm = smalls
        S.wait_ge(sS.h, m_rstd[i])
        if i >= 3:
            S.wait_ge(sPE.h, m_tr[i - 3])
        S.activation(xt[:], xb[:], AF.Identity, scale=sm[:, lo + 27:lo + 28],
                     bias=sm[:, lo + 28:lo + 29]).then_inc(sS.h, 1)
        sS.n += 1
        m_norm[i] = sS.n

    def emit_transposes(i):
        xt = r0["xn_tm"][i % 3]
        PE.wait_ge(sS.h, m_norm[i])
        if i >= 1:
            PE.wait_ge(sV.h, m_cpx[i - 1])
        for k in range(NK):
            ti_ = PE.transpose(pbt[:, k * 128:(k + 1) * 128],
                               xt[:, k * 128:(k + 1) * 128], ident[:])
        ti_.then_inc(sPE.h, 1)
        sPE.n += 1
        m_tr[i] = sPE.n
        V.wait_ge(sPE.h, m_tr[i])
        V.tensor_copy(xn_v[:, :, i * 128:(i + 1) * 128],
                      pbt.rearrange("p (k t) -> p k t", k=NK)).then_inc(sV.h, 1)
        sV.n += 1
        m_cpx[i] = sV.n

    def emit_tile_proj(i, sw):
        c0 = sw * 512
        qb = r1["q_tm"][i % 2]
        vb = r1["v_tm"][i % 2]
        po = (i % 2) * 512
        qsl = pbA[:, po:po + 512]
        vsl = pbA[:, 1024 + po:1024 + po + 512]
        if i == 0:
            PE.wait_ge(sWS[0].h, w_ready[0][0])
            PE.wait_ge(sWS[1].h, w_ready[0][1])
        if i >= 2:
            PE.wait_ge(sS.h, m_evS[i - 2])
            PE.wait_ge(sV.h, m_evV[i - 2])
        for k in range(NK):
            lhs = xn_v[:, k, i * 128:(i + 1) * 128]
            PE.matmul(qsl, lhs, r1["wq_bf"][:, k * 512:(k + 1) * 512],
                      start=(k == 0), stop=False)
            PE.matmul(vsl, lhs, r1["wv_bf"][:, k * 512:(k + 1) * 512],
                      start=(k == 0), stop=False)
        PE.matmul(qsl, ones_row_bf[:], bqk_row[0:1, c0:c0 + 512],
                  start=False, stop=True)
        mm_ = PE.matmul(vsl, ones_row_bf[:], bv_row[0:1, c0:c0 + 512],
                        start=False, stop=True)
        mm_.then_inc(sPE.h, 1)
        sPE.n += 1
        m_projg[i] = sPE.n
        S.wait_ge(sPE.h, m_projg[i])
        if i >= 2:
            S.wait_ge(sV.h, m_evV[i - 2])
            S.wait_ge(sPE.h, m_kvmm[i - 2])
        S.activation(r1["e_buf"][i % 2][:], qsl, AF.Exp)
        S.activation(vb[:], vsl, AF.Silu).then_inc(sS.h, 1)
        sS.n += 1
        m_evS[i] = sS.n
        V.wait_ge(sS.h, m_evS[i])
        if i >= 2:
            V.wait_ge(sPE.h, m_kvmm[i - 2])
            V.wait_ge(sSQ[i % 2].h, m_qst[i - 2])
        V.tensor_scalar(r1["r_buf"][i % 2][:], qsl, 0.0, 0.0,
                        ALU.max).then_inc(sV.h, 1)
        sV.n += 1
        V.wait_ge(sV.h, sV.n)
        V.scalar_tensor_tensor(qb[:], r1["e_buf"][i % 2][:], -1.0,
                               r1["r_buf"][i % 2][:], ALU.add,
                               ALU.min).then_inc(sV.h, 1)
        sV.n += 1
        m_evV[i] = sV.n

    def emit_kv(i, sw):
        qb = r1["q_tm"][i % 2]
        vb = r1["v_tm"][i % 2]
        kvp = pb6 if i % 2 == 0 else pb7
        PE.wait_ge(sV.h, m_evV[i])
        PE.wait_ge(sS.h, m_evS[i])
        if i >= 2:
            PE.wait_ge(sV.h, m_kvadd[i - 2])
        for j in range(4):
            km_ = PE.matmul(kvp[:, j * 128:(j + 1) * 128],
                            qb[:, j * 128:(j + 1) * 128],
                            vb[:, j * 128:(j + 1) * 128], start=True, stop=True)
        km_.then_inc(sPE.h, 1)
        sPE.n += 1
        m_kvmm[i] = sPE.n
        col0 = sw * 512
        V.wait_ge(sPE.h, m_kvmm[i])
        if i >= 1:
            V.wait_ge(sV.h, m_kvadd[i - 1])
        V.tensor_tensor(kv_sb[:, col0:col0 + 512], kv_sb[:, col0:col0 + 512],
                        kvp[:], ALU.add).then_inc(sV.h, 1)
        sV.n += 1
        m_kvadd[i] = sV.n

    pc0_cast = [0]
    pc0_st = [0]

    def pre_cast_qv(u):
        wsrc = wqk_h if u < NK else wv_h
        wdst = wqk_bf_s if u < NK else wv_bf_s
        k = u % NK
        SP.wait_ge(sV.h, pc0_cast[0])
        n = ld(sPC, r0["pc0_f32"][:], wsrc[k])
        V.wait_ge(sPC.h, n)
        V.wait_ge(sPCS.h, pc0_st[0])
        V.tensor_copy(r0["pc0_bf"][:], r0["pc0_f32"][:]).then_inc(sV.h, 1)
        sV.n += 1
        pc0_cast[0] = sV.n
        PL.wait_ge(sV.h, sV.n)
        pc0_st[0] = st(sPCS, wdst[k], r0["pc0_bf"][:])

    for i0 in range(3):
        m_xld[i0] = ld(sX[i0], r0["xbuf"][i0][:], x_h[i0 * 128:(i0 + 1) * 128, :])
    for i in range(NTO):
        emit_ln_A(i)
        if i >= 1:
            emit_ln_B(i - 1)
            emit_transposes(i - 1)
            pre_cast_qv(i - 1)
            j = i + 2
            if j < NTO:
                SP.wait_ge(sS.h, m_norm[i - 1])
                m_xld[j] = ld(sX[j % 3], r0["xbuf"][j % 3][:],
                              x_h[j * 128:(j + 1) * 128, :])
    emit_ln_B(NTO - 1)
    emit_transposes(NTO - 1)
    pre_cast_qv(NTO - 1)
    barrier()

    for sw in range(4):
        w_ready[0] = load_qv_weights(sw)
        reset_marks()
        for i in range(NTO):
            emit_tile_proj(i, sw)
            if i >= 1:
                emit_kv(i - 1, sw)
            if i > 0:
                PL.wait_ge(sV.h, m_evV[i - 1])
                m_qst[i - 1] = st(sSQ[(i - 1) % 2],
                                  q_s[(i - 1) * 128:i * 128, sw * 512:(sw + 1) * 512],
                                  r1["q_tm"][(i - 1) % 2][:])
        emit_kv(NTO - 1, sw)
        PL.wait_ge(sV.h, m_evV[NTO - 1])
        m_qst[NTO - 1] = st(sSQ[(NTO - 1) % 2],
                            q_s[(NTO - 1) * 128:NTO * 128, sw * 512:(sw + 1) * 512],
                            r1["q_tm"][(NTO - 1) % 2][:])
        barrier()

    st(sKV, kv_cc_in[:], kv_sb[:])
    PL.wait_ge(sKV.h, sKV.n)
    PL.collective_compute(
        "AllReduce", ALU.add, replica_groups=RG,
        ins=[kv_cc_in.ap().opt()], outs=[kv_cc_out.ap().opt()],
    ).then_inc(sCC.h)
    sCC.n = 1

    TG2 = T_OWN // 512
    wu_pe_done = [0, 0]
    wu_cast = [0, 0]
    u_evac = {}
    u_store = {}
    w2_free = [0]
    stg2_free = [0, 0]
    for m in range(NK):
        j = m % 2
        SP.wait_ge(sV.h, stg2_free[j])
        if m >= 2:
            SP.wait_ge(sPE.h, wu_pe_done[j])
        n = ld(sW2[j], r2["wstg2"][j].rearrange("p (c m) -> p c m", c=NK),
               wu_h[m].rearrange("(c p) m -> p c m", p=128))
        V.wait_ge(sW2[j].h, n)
        V.tensor_copy(r2["uw_bf"][j][:], r2["wstg2"][j][:]).then_inc(sV.h, 1)
        sV.n += 1
        wu_cast[j] = sV.n
        stg2_free[j] = sV.n
        PE.wait_ge(sV.h, wu_cast[j])
        for tg in range(TG2):
            u_idx = m * TG2 + tg
            psl = pbA[:, (u_idx % 2) * 512:(u_idx % 2) * 512 + 512]
            prev = u_idx - 2
            if prev >= 0:
                PE.wait_ge(sS.h, u_evac.get(prev, 0))
            for k in range(NK):
                um_ = PE.matmul(psl, r2["uw_bf"][j][:, k * 128:(k + 1) * 128],
                                xn_own[:, k * T_OWN + tg * 512:k * T_OWN + tg * 512 + 512],
                                start=(k == 0), stop=(k == NK - 1))
            um_.then_inc(sPE.h, 1)
            sPE.n += 1
            if tg == TG2 - 1:
                wu_pe_done[j] = sPE.n
            S.wait_ge(sPE.h, sPE.n)
            if prev >= 0:
                S.wait_ge(sSU[u_idx % 2].h, u_store.get(prev, 0))
            S.activation(r2["u_buf"][u_idx % 2][:], psl, AF.Silu,
                         bias=bu_fm[:, m:m + 1]).then_inc(sS.h, 1)
            sS.n += 1
            u_evac[u_idx] = sS.n
            PL.wait_ge(sS.h, sS.n)
            u_store[u_idx] = st(
                sSU[u_idx % 2],
                u_s.rearrange("m p t -> p m t")[:, m, tg * 512:(tg + 1) * 512],
                r2["u_buf"][u_idx % 2][:])

    barrier()

    g_evac = {}
    h_evac = {}
    h_store = {}
    w_pe_done = [0, 0]
    stg_free = [0, 0]
    cast_done = [0] * (NI + 1)
    gw_n = [0] * (NI + 1)
    uw_n = [0] * (NI + 1)

    def p3_ld_gw(i):
        j = i % 2
        SP.wait_ge(sV.h, stg_free[j])
        gw_n[i] = ld(sW2[j], r2["wstg2"][j].rearrange("p (c m) -> p c m", c=NK),
                     gw_h[i].rearrange("(c p) m -> p c m", p=128))

    def p3_cast_gw(i):
        j = i % 2
        V.wait_ge(sW2[j].h, gw_n[i])
        if i >= 2:
            V.wait_ge(sPE.h, w_pe_done[j])
        V.tensor_copy(r2["gw_bf"][j][:], r2["wstg2"][j][:]).then_inc(sV.h, 1)
        sV.n += 1

    def p3_ld_uw(i):
        j = i % 2
        SP.wait_ge(sV.h, sV.n)
        uw_n[i] = ld(sW2[j], r2["wstg2"][j].rearrange("p (c m) -> p c m", c=NK),
                     uw_h[i].rearrange("(c p) m -> p c m", p=128))

    def p3_cast_uw(i):
        j = i % 2
        V.wait_ge(sW2[j].h, uw_n[i])
        V.tensor_copy(r2["uw_bf"][j][:], r2["wstg2"][j][:]).then_inc(sV.h, 1)
        sV.n += 1
        stg_free[j] = sV.n
        cast_done[i] = sV.n

    pc3_cast = [0]
    pc3_st = [0]

    def pre_cast_dw(u):
        i_dw, q = u // 4, u % 4
        SP.wait_ge(sV.h, pc3_cast[0])
        n = ld(sPC3, r2["u_buf"][0][:], dw_h[i_dw, :, q * 512:(q + 1) * 512])
        V.wait_ge(sPC3.h, n)
        V.wait_ge(sPS3.h, pc3_st[0])
        V.tensor_copy(r2["pc3_bf"][:], r2["u_buf"][0][:]).then_inc(sV.h, 1)
        sV.n += 1
        pc3_cast[0] = sV.n
        PL.wait_ge(sV.h, sV.n)
        pc3_st[0] = st(sPS3, dw_bf_s[i_dw, :, q * 512:(q + 1) * 512],
                       r2["pc3_bf"][:])

    p3_ld_gw(0)
    p3_cast_gw(0)
    p3_ld_uw(0)
    p3_cast_uw(0)
    for i in range(NI):
        j = i % 2
        PE.wait_ge(sV.h, cast_done[i])
        for tg in range(TG2):
            if i + 1 < NI:
                if tg == 0:
                    p3_ld_gw(i + 1)
                elif tg == 1:
                    p3_cast_gw(i + 1)
                elif tg == 2:
                    p3_ld_uw(i + 1)
                elif tg == 3:
                    p3_cast_uw(i + 1)
            if tg >= 4:
                pre_cast_dw(i * 4 + (tg - 4))
            f_idx = i * TG2 + tg
            po = (f_idx % 2) * 512
            gsl = pbA[:, po:po + 512]
            usl = pbA[:, 1024 + po:1024 + po + 512]
            prev = f_idx - 2
            if prev >= 0:
                PE.wait_ge(sS.h, g_evac.get(prev, 0))
            for k in range(NK):
                rhs = xn_own[:, k * T_OWN + tg * 512:k * T_OWN + tg * 512 + 512]
                PE.matmul(gsl, r2["gw_bf"][j][:, k * 128:(k + 1) * 128], rhs,
                          start=(k == 0), stop=(k == NK - 1))
                fm_ = PE.matmul(usl, r2["uw_bf"][j][:, k * 128:(k + 1) * 128], rhs,
                                start=(k == 0), stop=(k == NK - 1))
            fm_.then_inc(sPE.h, 1)
            sPE.n += 1
            if tg == TG2 - 1:
                w_pe_done[j] = sPE.n
            S.wait_ge(sPE.h, sPE.n)
            if prev >= 0:
                S.wait_ge(sV.h, h_evac.get(prev, 0))
            S.activation(r2["sg_buf"][f_idx % 2][:], gsl, AF.Silu,
                         scale=gs_fm[:, i:i + 1])
            S.activation(r2["u_raw"][f_idx % 2][:], usl,
                         AF.Identity).then_inc(sS.h, 1)
            sS.n += 1
            g_evac[f_idx] = sS.n
            V.wait_ge(sS.h, sS.n)
            if prev >= 0:
                V.wait_ge(sSH[f_idx % 2].h, h_store.get(prev, 0))
            V.scalar_tensor_tensor(r2["hdn_buf"][f_idx % 2][:],
                                   r2["u_raw"][f_idx % 2][:], us_fm[:, i:i + 1],
                                   r2["sg_buf"][f_idx % 2][:], ALU.mult,
                                   ALU.mult).then_inc(sV.h, 1)
            sV.n += 1
            h_evac[f_idx] = sV.n
            PL.wait_ge(sV.h, sV.n)
            h_store[f_idx] = st(
                sSH[f_idx % 2],
                hdn_s.rearrange("i p t -> p i t")[:, i, tg * 512:(tg + 1) * 512],
                r2["hdn_buf"][f_idx % 2][:])

    barrier()

    n_ds = ld(sC, ds_repl[:], ds_h[:])
    V.wait_ge(sC.h, n_ds)
    dwv = dw_bf_s.rearrange("i p c -> p i c")
    for half in range(2):
        dst_dw = r4["dw_bf"].rearrange("p (i c) -> p i c", i=NI)
        n_dw = 0
        for g in range(11):
            i0, i1 = g * 4, min((g + 1) * 4, NI)
            n_dw = ld(sDS[g % 2], dst_dw[:, i0:i1, :],
                      dwv[:, i0:i1, half * 1024:(half + 1) * 1024])
        PE.wait_ge(sDS[0].h, sDS[0].n)
        PE.wait_ge(sDS[1].h, sDS[1].n)

        p4_mm = {}
        f_evac = {}
        f_store = {}
        for t in range(NTO):
            j = t % 2
            if t >= 2:
                SP.wait_ge(sPE.h, p4_mm.get(t - 2, 0))
            nlast = 0
            hsrc = hdn_s.rearrange("i p t -> p i t")[:, :, t * 128:(t + 1) * 128]
            hdst = r4["hdn_t"][j].rearrange("p (i t) -> p i t", i=NI)
            for piece in range(11):
                i0, i1 = piece * 4, min((piece + 1) * 4, NI)
                nlast = ld(sHT[j], hdst[:, i0:i1, :], hsrc[:, i0:i1, :])
            PE.wait_ge(sHT[j].h, nlast)
            b0 = (t % 2) * 1024
            if t >= 2:
                PE.wait_ge(sV.h, f_evac.get(t - 2, 0))
            for i in range(NI):
                lhs = r4["hdn_t"][j][:, i * 128:(i + 1) * 128]
                PE.matmul(pbA[:, b0:b0 + 512], lhs,
                          r4["dw_bf"][:, i * 1024:i * 1024 + 512],
                          start=(i == 0), stop=(i == NI - 1))
                dm_ = PE.matmul(pbA[:, b0 + 512:b0 + 1024], lhs,
                                r4["dw_bf"][:, i * 1024 + 512:i * 1024 + 1024],
                                start=(i == 0), stop=(i == NI - 1))
            dm_.then_inc(sPE.h, 1)
            sPE.n += 1
            p4_mm[t] = sPE.n
            V.wait_ge(sPE.h, sPE.n)
            V.wait_ge(sSF[j].h, f_store.get(t - 2, 0))
            V.tensor_tensor(r4["ffn_buf"][j][:], pbA[:, b0:b0 + 1024],
                            ds_repl[:, half * 1024:(half + 1) * 1024],
                            ALU.mult).then_inc(sV.h, 1)
            sV.n += 1
            f_evac[t] = sV.n
            PL.wait_ge(sV.h, sV.n)
            f_store[t] = st(sSF[j],
                            ffn_s[t * 128:(t + 1) * 128, half * 1024:(half + 1) * 1024],
                            r4["ffn_buf"][j][:])
        barrier()

    SP.wait_ge(sCC.h, 1)
    n_kv = ld(sC, kv_sb[:], kv_cc_out[:])
    S.wait_ge(sC.h, n_kv)
    S.activation(kv_bf[:], kv_sb[:], AF.Identity).then_inc(sS.h, 1)
    sS.n += 1

    V.wait_ge(sC.h, n_kv)
    kv3 = kv_sb.rearrange("p (m e) -> p m e", m=NK)
    scA = r5["kvscA"].rearrange("p (m e) -> p m e", m=NK)
    scB = r5["kvscB"].rearrange("p (m e) -> p m e", m=NK)
    tree = [(scA, kv3, kv3, 64), (scB, scA, scA, 32), (scA, scB, scB, 16),
            (scB, scA, scA, 8), (scA, scB, scB, 4), (scB, scA, scA, 2),
            (r5["kvs_f"].rearrange("p (m o) -> p m o", o=1), scB, scB, 1)]
    for dst, s0, s1, w in tree:
        V.tensor_tensor(dst[:, :, 0:w], s0[:, :, 0:w], s1[:, :, w:2 * w],
                        ALU.add).then_inc(sV.h, 1)
        sV.n += 1
        V.wait_ge(sV.h, sV.n)
    S.wait_ge(sV.h, sV.n)
    S.activation(r5["kvs_bf"][:], r5["kvs_f"][:], AF.Identity).then_inc(sS.h, 1)
    sS.n += 1
    kvs_ready = sS.n

    NTG = T_OWN // 256
    uv_h = u_s.rearrange("m p t -> p m t")
    zv_h = z_s.rearrange("k p t -> p k t")
    trp = [0, 0]
    pbt_ev = [0]
    outmm_last = [0] * (NTG + 2)
    stats_mm = [0] * (NTG + 2)
    mean_mm = [0] * (NTG + 2)
    mbc_mm = [0] * (NTG + 2)
    mean_ev = [0] * (NTG + 2)
    mr_mark = [0] * (NTG + 2)
    rr_mark = [0] * (NTG + 2)
    bc7_mm = [0] * (NTG + 2)
    bc_ev = [0] * (NTG + 2)
    t1_last = [0] * (NTG + 2)
    t2_last = [0] * (NTG + 2)
    z_g = [0] * (NTG + 2)
    z_st = [0, 0]
    u_ld = [0, 0]
    q_ld = [0, 0]
    oevS = [0] * 4
    oevV = [0] * 4
    tt1, tt2, tt3, ttz = {}, {}, {}, {}

    evac_arr = [0] * (NTG + 2)

    def p5a_A1(tg):
        p = tg % 2
        t0 = tg * 256
        p = tg % 2
        t0 = tg * 256
        pc = p * 256
        for ti in range(2):
            SP.wait_ge(sPE.h, trp[ti])
            q_ld[ti] = ld(sQT[ti], r5["qtile"][ti][:],
                          q_s[(tg * 2 + ti) * 128:(tg * 2 + ti + 1) * 128, :])
        if tg >= 2:
            SP.wait_ge(sV.h, z_g[tg - 2])
        dst_u = r5["u_tile"][p].rearrange("p (m t) -> p m t", m=NK)
        for p4 in range(4):
            n_u = ld(sUT[p], dst_u[:, p4 * 4:(p4 + 1) * 4, :],
                     uv_h[:, p4 * 4:(p4 + 1) * 4, t0:t0 + 256])
        u_ld[p] = n_u
        for ti in range(2):
            PE.wait_ge(sQT[ti].h, q_ld[ti])
            PE.wait_ge(sS.h, pbt_ev[0])
            for k in range(NK):
                tq_ = PE.transpose(pbt[:, k * 128:(k + 1) * 128],
                                   r5["qtile"][ti][:, k * 128:(k + 1) * 128],
                                   ident[:])
            tq_.then_inc(sPE.h, 1)
            sPE.n += 1
            trp[ti] = sPE.n
            S.wait_ge(sPE.h, trp[ti])
            if ti == 0 and tg >= 2:
                S.wait_ge(sPE.h, outmm_last[tg - 2])
            S.activation(
                r5["qfm"][p].rearrange("p (m t) -> p m t", m=NK)[:, :, ti * 128:(ti + 1) * 128],
                pbt.rearrange("p (m t) -> p m t", m=NK)[:, :, 0:128],
                AF.Identity).then_inc(sS.h, 1)
            sS.n += 1
            pbt_ev[0] = sS.n
        qfm_ready = sS.n
        PE.wait_ge(sS.h, qfm_ready)
        if tg == 0:
            PE.wait_ge(sS.h, kvs_ready)
        if tg >= 1:
            PE.wait_ge(sS.h, mean_ev[tg - 1])
            PE.wait_ge(sV.h, mr_mark[tg - 1])
        for m in range(NK):
            mn_ = PE.matmul(pb6[0:1, pc:pc + 256], r5["kvs_bf"][:, m:m + 1],
                            r5["qfm"][p][:, m * 256:(m + 1) * 256],
                            start=(m == 0), stop=(m == NK - 1))
        mn_.then_inc(sPE.h, 1)
        sPE.n += 1
        mean_mm[tg] = sPE.n
        mr = mean_row[0:1, pc:pc + 256]
        V.wait_ge(sPE.h, mean_mm[tg])
        if tg >= 2:
            V.wait_ge(sPE.h, mbc_mm[tg - 2])
        V.tensor_scalar_mul(mr, pb6[0:1, pc:pc + 256], 1.0 / D).then_inc(sV.h, 1)
        sV.n += 1
        mr_mark[tg] = sV.n

    def p5a_A2(tg):
        p = tg % 2
        t0 = tg * 256
        pc = p * 256
        mr = mean_row[0:1, pc:pc + 256]
        for m in range(NK):
            slot = p * 2 + m % 2
            psl = pbA[:, slot * 512:slot * 512 + 256]
            PE.wait_ge(sS.h, oevS[slot])
            PE.matmul(psl, kv_bf[:, m * 128:(m + 1) * 128],
                      r5["qfm"][p][:, m * 256:(m + 1) * 256],
                      start=True, stop=True).then_inc(sPE.h, 1)
            sPE.n += 1
            S.wait_ge(sPE.h, sPE.n)
            if m == 0 and tg >= 2:
                S.wait_ge(sPE.h, stats_mm[tg - 2])
                S.wait_ge(sV.h, z_g[tg - 2])
            S.activation(r5["outst"][p][:, m * 256:(m + 1) * 256], psl,
                         AF.Identity)
            S.activation(r5["sq_bf"][p][:, m * 256:(m + 1) * 256], psl,
                         AF.Square).then_inc(sS.h, 1)
            sS.n += 1
            oevS[slot] = sS.n
        outmm_last[tg] = sPE.n
        evac_arr[tg] = sS.n
        PE.wait_ge(sV.h, mr_mark[tg])
        PE.matmul(pb6[:, pc:pc + 256], ones_row_f[:], mr,
                  start=True, stop=True).then_inc(sPE.h, 1)
        sPE.n += 1
        mbc_mm[tg] = sPE.n
        S.wait_ge(sPE.h, mbc_mm[tg])
        if tg >= 2:
            S.wait_ge(sV.h, z_g[tg - 2])
        S.activation(r5["mean_sb"][p][:], pb6[:, pc:pc + 256],
                     AF.Identity).then_inc(sS.h, 1)
        sS.n += 1
        mean_ev[tg] = sS.n


    def p5a_B1(tg):
        p = tg % 2
        t0 = tg * 256
        pc = p * 256
        PE.wait_ge(sS.h, evac_arr[tg])
        if tg >= 1:
            PE.wait_ge(sS.h, bc_ev[tg - 1])
            PE.wait_ge(sV.h, rr_mark[tg - 1])
        for m in range(NK):
            sm_ = PE.matmul(pb7[0:1, pc:pc + 256], ones_col_bf[:],
                            r5["sq_bf"][p][:, m * 256:(m + 1) * 256],
                            start=(m == 0), stop=(m == NK - 1))
        sm_.then_inc(sPE.h, 1)
        sPE.n += 1
        stats_mm[tg] = sPE.n
        mr = mean_row[0:1, pc:pc + 256]
        rr = rstd_row[0:1, pc:pc + 256]
        V.wait_ge(sPE.h, stats_mm[tg])
        V.wait_ge(sV.h, mr_mark[tg])
        if tg >= 2:
            V.wait_ge(sPE.h, bc7_mm[tg - 2])
        V.tensor_tensor(rr, mr, mr, ALU.mult).then_inc(sV.h, 1)
        sV.n += 1
        V.wait_ge(sV.h, sV.n)
        V.scalar_tensor_tensor(rr, pb7[0:1, pc:pc + 256], 1.0 / D, rr,
                               ALU.mult, ALU.subtract).then_inc(sV.h, 1)
        sV.n += 1
        rr_mark[tg] = sV.n
        S.wait_ge(sV.h, sV.n)
        S.activation(rr, rr, AF.Ln, bias=cells[0:1, 0:1]).then_inc(sS.h, 1)
        sS.n += 1
        S.wait_ge(sS.h, sS.n)
        S.activation(rr, rr, AF.Exp, scale=cells[0:1, 2:3]).then_inc(sS.h, 1)
        sS.n += 1
        rows_c = sS.n
        PE.wait_ge(sS.h, rows_c)
        PE.matmul(pb7[:, pc:pc + 256], ones_row_f[:], rr,
                  start=True, stop=True).then_inc(sPE.h, 1)
        sPE.n += 1
        bc7_mm[tg] = sPE.n
        S.wait_ge(sPE.h, bc7_mm[tg])
        if tg >= 2:
            S.wait_ge(sV.h, z_g[tg - 2])
        S.activation(r5["rstd_sb"][p][:], pb7[:, pc:pc + 256],
                     AF.Identity).then_inc(sS.h, 1)
        sS.n += 1
        bc_ev[tg] = sS.n

    def p5a_B2(tg):
        p = tg % 2
        t0 = tg * 256
        pc = p * 256
        for mp in range(NK // 2):
            ms = (2 * mp, 2 * mp + 1)
            marks = {}
            if mp == 0:
                V.wait_ge(sS.h, bc_ev[tg])
                V.wait_ge(sUT[p].h, u_ld[p])
                V.wait_ge(sSZ[p].h, z_st[p])
            for m in ms:
                V.tensor_tensor(r5["t1_b"][m % 2][:],
                                r5["outst"][p][:, m * 256:(m + 1) * 256],
                                r5["mean_sb"][p][:],
                                ALU.subtract).then_inc(sV.h, 1)
                sV.n += 1
                marks[("t1", m)] = sV.n
            for m in ms:
                V.wait_ge(sV.h, marks[("t1", m)])
                V.scalar_tensor_tensor(r5["t2_b"][m % 2][:], r5["t1_b"][m % 2][:],
                                       lng_fm[:, m:m + 1], r5["rstd_sb"][p][:],
                                       ALU.mult, ALU.mult).then_inc(sV.h, 1)
                sV.n += 1
                marks[("t2", m)] = sV.n
            for m in ms:
                V.wait_ge(sV.h, marks[("t2", m)])
                V.scalar_tensor_tensor(r5["zfm"][p][:, m * 256:(m + 1) * 256],
                                       r5["t2_b"][m % 2][:],
                                       lnb_fm[:, m:m + 1],
                                       r5["u_tile"][p][:, m * 256:(m + 1) * 256],
                                       ALU.add, ALU.mult).then_inc(sV.h, 1)
                sV.n += 1
        t1_last[tg] = sV.n
        t2_last[tg] = sV.n
        z_g[tg] = sV.n
        PL.wait_ge(sV.h, z_g[tg])
        src_z = r5["zfm"][p].rearrange("p (k t) -> p k t", k=NK)
        for p4 in range(4):
            n_z = st(sSZ[p], zv_h[:, p4 * 4:(p4 + 1) * 4, t0:t0 + 256],
                     src_z[:, p4 * 4:(p4 + 1) * 4, :])
        z_st[p] = n_z


    p5a_A1(0)
    p5a_A2(0)
    for tg in range(1, NTG):
        p5a_A1(tg)
        p5a_B1(tg - 1)
        p5a_A2(tg)
        p5a_B2(tg - 1)
    p5a_B1(NTG - 1)
    p5a_B2(NTG - 1)
    barrier()

    n_bo = ld(sC, r6["wstg6"][0][0:1, :], bo_h[:])
    V.wait_ge(sC.h, n_bo)
    V.tensor_copy(bo_row[:], r6["wstg6"][0][0:1, :]).then_inc(sV.h, 1)
    sV.n += 1
    wo_cast = [sV.n, 0]
    wo_k = [0] * NK
    for k in range(NK):
        j = k % 2
        SP.wait_ge(sV.h, wo_cast[j])
        n = ld(sWS[j], r6["wstg6"][j][:], wo_h[k])
        V.wait_ge(sWS[j].h, n)
        V.tensor_copy(r6["wo_bf"][:, k * D:(k + 1) * D],
                      r6["wstg6"][j][:]).then_inc(sV.h, 1)
        sV.n += 1
        wo_cast[j] = sV.n
        wo_k[k] = sV.n

    ot_store = [0, 0]
    xf_done = [0, 0]
    xf_rd = [0, 0]
    z_rd = [0, 0]
    half_ev_V = [0, 0]
    for t in range(NTO):
        j = t % 2
        if t >= 2:
            SP.wait_ge(sPool.h, xf_done[j])
        n_x = ld(sXT[j], r6["x_t"][j][:], x_h[t * 128:(t + 1) * 128, :])
        n_f = ld(sFT[j], r6["ffn_t"][j][:], ffn_s[t * 128:(t + 1) * 128, :])
        if t >= 2:
            SP.wait_ge(sPE.h, z_rd[j])
        zt = r6["ztile"][j].rearrange("p (k t) -> p k t", k=NK)
        for p4 in range(4):
            n_z = ld(sZT[j], zt[:, p4 * 4:(p4 + 1) * 4, :],
                     zv_h[:, p4 * 4:(p4 + 1) * 4, t * 128:(t + 1) * 128])
        PL.wait_ge(sXT[j].h, n_x)
        PL.wait_ge(sFT[j].h, n_f)
        if t >= 2:
            PL.wait_ge(sV.h, xf_rd[j])
        PL.tensor_tensor(r6["xf"][j][:], r6["x_t"][j][:], r6["ffn_t"][j][:],
                         ALU.add).then_inc(sPool.h, 1)
        sPool.n += 1
        xf_done[j] = sPool.n
        PE.wait_ge(sZT[j].h, n_z)
        for gp in range(2):
            u = (t * 2 + gp) % 2
            b0 = u * 1024
            if t * 2 + gp >= 2:
                PE.wait_ge(sV.h, half_ev_V[u])
            for k in range(NK):
                if t == 0 and gp == 0:
                    PE.wait_ge(sV.h, wo_k[k])
                lhs = zt[:, k, :]
                PE.matmul(pbA[:, b0:b0 + 512], lhs,
                          r6["wo_bf"][:, k * D + gp * 1024:k * D + gp * 1024 + 512],
                          start=(k == 0), stop=False)
                PE.matmul(pbA[:, b0 + 512:b0 + 1024], lhs,
                          r6["wo_bf"][:, k * D + gp * 1024 + 512:k * D + gp * 1024 + 1024],
                          start=(k == 0), stop=False)
            PE.matmul(pbA[:, b0:b0 + 512], ones_row_bf[:],
                      bo_row[0:1, gp * 1024:gp * 1024 + 512],
                      start=False, stop=True)
            wm_ = PE.matmul(pbA[:, b0 + 512:b0 + 1024], ones_row_bf[:],
                            bo_row[0:1, gp * 1024 + 512:gp * 1024 + 1024],
                            start=False, stop=True)
            wm_.then_inc(sPE.h, 1)
            sPE.n += 1
            if gp == 1:
                z_rd[j] = sPE.n
            V.wait_ge(sPE.h, sPE.n)
            if gp == 0:
                V.wait_ge(sPool.h, xf_done[j])
                V.wait_ge(sSO[j].h, ot_store[j])
            V.tensor_tensor(r6["outb"][j][:, gp * 1024:(gp + 1) * 1024],
                            pbA[:, b0:b0 + 1024],
                            r6["xf"][j][:, gp * 1024:(gp + 1) * 1024],
                            ALU.add).then_inc(sV.h, 1)
            sV.n += 1
            half_ev_V[u] = sV.n
            if gp == 1:
                xf_rd[j] = sV.n
        PL.wait_ge(sV.h, xf_rd[j])
        ot_store[j] = st(sSO[j], out_h[t * 128:(t + 1) * 128, :],
                         r6["outb"][j][:])

    for q in LOAD_SEMS:
        SP.wait_ge(q.h, q.n)
    for q in STORE_SEMS:
        PL.wait_ge(q.h, q.n)
    SP.wait_ge(sSO[0].h, sSO[0].n)
    SP.wait_ge(sSO[1].h, sSO[1].n)
    return nc


def kernel(L0_attn_x, L0_attn_Wqk, L0_attn_bqk, L0_attn_Wv, L0_attn_bv,
           L0_attn_Wu, L0_attn_bu, L0_attn_Wo, L0_attn_bo, L0_attn_ln_g,
           L0_attn_ln_b, L0_ffn_gate_w, L0_ffn_up_w, L0_ffn_down_w,
           L0_ffn_gate_s, L0_ffn_up_s, L0_ffn_down_s):
    f32 = np.float32
    x = np.asarray(L0_attn_x, dtype=f32).reshape(2 * T_BATCH, D)

    def fm_vec(v, nk):
        return np.ascontiguousarray(np.asarray(v, f32).reshape(nk, 128).T)

    common = {
        "wqk": np.ascontiguousarray(np.asarray(L0_attn_Wqk, f32).reshape(NK, 128, D)),
        "wv": np.ascontiguousarray(np.asarray(L0_attn_Wv, f32).reshape(NK, 128, D)),
        "wu": np.ascontiguousarray(
            np.asarray(L0_attn_Wu, f32).reshape(D, NK, 128).transpose(1, 0, 2)),
        "wo": np.ascontiguousarray(np.asarray(L0_attn_Wo, f32).reshape(NK, 128, D)),
        "gatew": np.ascontiguousarray(
            np.asarray(L0_ffn_gate_w, f32).T.reshape(D, NI, 128).transpose(1, 0, 2)),
        "upw": np.ascontiguousarray(
            np.asarray(L0_ffn_up_w, f32).T.reshape(D, NI, 128).transpose(1, 0, 2)),
        "downw": np.ascontiguousarray(
            np.asarray(L0_ffn_down_w, f32).T.reshape(NI, 128, D)),
        "bqk": np.asarray(L0_attn_bqk, f32).reshape(1, D),
        "bv": np.asarray(L0_attn_bv, f32).reshape(1, D),
        "bo": np.asarray(L0_attn_bo, f32).reshape(1, D),
        "bu": fm_vec(L0_attn_bu, NK),
        "lng": fm_vec(L0_attn_ln_g, NK),
        "lnb": fm_vec(L0_attn_ln_b, NK),
        "gs": fm_vec(L0_ffn_gate_s, NI),
        "us": fm_vec(L0_ffn_up_s, NI),
        "ds": np.ascontiguousarray(
            np.tile(np.asarray(L0_ffn_down_s, f32).reshape(1, D), (128, 1))),
        "ident": np.eye(128, dtype=ml_dtypes.bfloat16),
    }

    in_maps = []
    for c in range(8):
        b = c // 4
        o = (c % 4) * T_OWN
        x_core = np.ascontiguousarray(x[b * T_BATCH + o:b * T_BATCH + o + T_OWN])
        m = dict(common)
        m["x"] = x_core
        in_maps.append(m)

    nc = build_nc()
    res = run_bass_kernel_spmd(nc, in_maps, core_ids=list(range(8)))

    out = np.empty((2 * T_BATCH, D), dtype=np.float32)
    for c in range(8):
        b = c // 4
        o = (c % 4) * T_OWN
        out[b * T_BATCH + o:b * T_BATCH + o + T_OWN] = res.results[c]["out"]
    return out.reshape(2, T_BATCH, D)


# version 27
# speedup vs baseline: 1.0098x; 1.0098x over previous
import numpy as np
import ml_dtypes
import concourse.bass as bass
import concourse.mybir as mybir
from concourse.bass_utils import run_bass_kernel_spmd

F32 = mybir.dt.float32
BF16 = mybir.dt.bfloat16
AF = mybir.ActivationFunctionType
ALU = mybir.AluOpType
AX = mybir.AxisListType

D = 2048
NK = 16
INTER = 5632
NI = 44
T_OWN = 4096
T_BATCH = 16384
NTO = T_OWN // 128
EPS = 1e-5
RG = [[0, 1, 2, 3], [4, 5, 6, 7]]

BASE = 16384 + 4096


def build_nc():
    nc = bass.Bass("TRN2", target_bir_lowering=False)

    x_h = nc.dram_tensor("x", [T_OWN, D], F32, kind="ExternalInput")
    wqk_h = nc.dram_tensor("wqk", [NK, 128, D], F32, kind="ExternalInput")
    wv_h = nc.dram_tensor("wv", [NK, 128, D], F32, kind="ExternalInput")
    wu_h = nc.dram_tensor("wu", [NK, D, 128], F32, kind="ExternalInput")
    wo_h = nc.dram_tensor("wo", [NK, 128, D], F32, kind="ExternalInput")
    gw_h = nc.dram_tensor("gatew", [NI, D, 128], F32, kind="ExternalInput")
    uw_h = nc.dram_tensor("upw", [NI, D, 128], F32, kind="ExternalInput")
    dw_h = nc.dram_tensor("downw", [NI, 128, D], F32, kind="ExternalInput")
    bqk_h = nc.dram_tensor("bqk", [1, D], F32, kind="ExternalInput")
    bv_h = nc.dram_tensor("bv", [1, D], F32, kind="ExternalInput")
    bo_h = nc.dram_tensor("bo", [1, D], F32, kind="ExternalInput")
    bu_h = nc.dram_tensor("bu", [128, NK], F32, kind="ExternalInput")
    lng_h = nc.dram_tensor("lng", [128, NK], F32, kind="ExternalInput")
    lnb_h = nc.dram_tensor("lnb", [128, NK], F32, kind="ExternalInput")
    gs_h = nc.dram_tensor("gs", [128, NI], F32, kind="ExternalInput")
    us_h = nc.dram_tensor("us", [128, NI], F32, kind="ExternalInput")
    ds_h = nc.dram_tensor("ds", [128, D], F32, kind="ExternalInput")
    id_h = nc.dram_tensor("ident", [128, 128], BF16, kind="ExternalInput")

    kv_cc_in = nc.dram_tensor("kv_cc_in", [128, D], F32, kind="Internal")
    kv_cc_out = nc.dram_tensor("kv_cc_out", [128, D], F32, kind="Internal")
    z_s = nc.dram_tensor("z_s", [NK, 128, T_OWN], BF16, kind="Internal")
    wqk_bf_s = nc.dram_tensor("wqk_bf_s", [NK, 128, D], BF16, kind="Internal")
    wv_bf_s = nc.dram_tensor("wv_bf_s", [NK, 128, D], BF16, kind="Internal")
    dw_bf_s = nc.dram_tensor("dw_bf_s", [NI, 128, D], BF16, kind="Internal")
    q_s = nc.dram_tensor("q_s", [T_OWN, D], BF16, kind="Internal")
    u_s = nc.dram_tensor("u_s", [NK, 128, T_OWN], F32, kind="Internal")
    hdn_s = nc.dram_tensor("hdn_s", [NI, 128, T_OWN], BF16, kind="Internal")
    ffn_s = nc.dram_tensor("ffn_s", [T_OWN, D], F32, kind="Internal")

    out_h = nc.dram_tensor("out", [T_OWN, D], F32, kind="ExternalOutput")

    off = [BASE]
    TOP = 229376 - 32

    def sb(name, shape, dtype, n=1, at=None):
        o = off if at is None else at
        per = int(np.prod(shape[1:])) * mybir.dt.size(dtype)
        per = (per + 31) // 32 * 32
        hs = []
        for i in range(n):
            hs.append(nc.alloc_sbuf_tensor_at(f"{name}{i if n > 1 else ''}",
                                              shape, dtype, offset=o[0]))
            o[0] += per
        assert o[0] <= TOP, f"sbuf overflow at {name}: {o[0]}"
        return hs if n > 1 else hs[0]

    ident = sb("ident_sb", [128, 128], BF16)
    ones_row_bf = sb("ones_row_bf", [1, 128], BF16)
    ones_col_bf = sb("ones_col_bf", [128, 1], BF16)
    ones_row_f = sb("ones_row_f", [1, 128], F32)
    cells = sb("cells", [128, 8], F32)
    bqk_row = sb("bqk_row", [1, D], BF16)
    bv_row = sb("bv_row", [1, D], BF16)
    bu_fm = sb("bu_fm", [128, NK], F32)
    lng_fm = sb("lng_fm", [128, NK], F32)
    lnb_fm = sb("lnb_fm", [128, NK], F32)
    gs_fm = sb("gs_fm", [128, NI], F32)
    us_fm = sb("us_fm", [128, NI], F32)
    kv_sb = sb("kv_sb", [128, D], F32)
    kv_bf = sb("kv_bf", [128, D], BF16)
    smalls = sb("smalls", [128, 96], F32)
    mean_row = sb("mean_row", [1, 512], F32)
    rstd_row = sb("rstd_row", [1, 512], F32)

    PB = off[0]

    XN_OFF = TOP - NK * T_OWN * 2
    xn_own = nc.alloc_sbuf_tensor_at("xn_own", [128, NK * T_OWN], BF16,
                                     offset=XN_OFF)
    xn_v = xn_own.rearrange("p (k t) -> p k t", k=NK)

    def region(tensors, top=None):
        o = [PB]
        out = {}
        for name, shape, dtype, n in tensors:
            out[name] = sb(name, shape, dtype, n, at=o)
        if top is not None:
            assert o[0] <= top, f"region overflow: {o[0]} > {top}"
        return out

    r0 = region([
        ("xbuf", [128, D], F32, 3),
        ("xn_tm", [128, D], BF16, 3),
        ("pc0_f32", [128, D], F32, 1),
        ("pc0_bf", [128, D], BF16, 1),
    ], top=XN_OFF)
    r1 = region([
        ("wq_bf", [128, NK * 512], BF16, 1),
        ("wv_bf", [128, NK * 512], BF16, 1),
        ("wstage", [128, 512], F32, 2),
        ("q_tm", [128, 512], BF16, 2),
        ("v_tm", [128, 512], BF16, 2),
        ("e_buf", [128, 512], F32, 2),
        ("r_buf", [128, 512], F32, 2),
    ], top=XN_OFF)
    row_stage = r0["xbuf"][0][0:1, :]
    r2 = region([
        ("wstg2", [128, D], F32, 2),
        ("gw_bf", [128, D], BF16, 2),
        ("uw_bf", [128, D], BF16, 2),
        ("sg_buf", [128, 512], F32, 2),
        ("u_raw", [128, 512], F32, 2),
        ("hdn_buf", [128, 512], BF16, 2),
        ("u_buf", [128, 512], F32, 2),
        ("pc3_bf", [128, 512], BF16, 1),
    ], top=XN_OFF)
    r4 = region([
        ("dw_bf", [128, NI * 1024], BF16, 1),
        ("hdn_t", [128, NI * 128], BF16, 2),
        ("ffn_buf", [128, 1024], F32, 2),
        ("dstg", [128, 1024], F32, 2),
        ("ds_repl", [128, D], F32, 1),
    ])
    ds_repl = r4["ds_repl"]
    r5 = region([
        ("qtile", [128, D], BF16, 2),
        ("qfm", [128, NK * 256], BF16, 2),
        ("outst", [128, NK * 256], F32, 2),
        ("sq_bf", [128, NK * 256], BF16, 2),
        ("kvscA", [128, NK * 64], F32, 1),
        ("kvscB", [128, NK * 32], F32, 1),
        ("kvs_f", [128, NK], F32, 1),
        ("kvs_bf", [128, NK], BF16, 1),
        ("zfm", [128, NK * 256], BF16, 2),
        ("u_tile", [128, NK * 256], F32, 2),
        ("mean_sb", [128, 256], F32, 2),
        ("rstd_sb", [128, 256], F32, 2),
        ("t1_b", [128, 256], F32, 2),
        ("t2_b", [128, 256], F32, 2),
        ("t3_b", [128, 256], F32, 2),
    ])
    r6 = region([
        ("wo_bf", [128, NK * D], BF16, 1),
        ("wstg6", [128, D], F32, 2),
        ("ztile", [128, NK * 128], BF16, 2),
        ("x_t", [128, D], F32, 2),
        ("ffn_t", [128, D], F32, 2),
        ("xf", [128, D], F32, 2),
        ("outb", [128, D], F32, 2),
        ("bo_row", [1, D], BF16, 1),
    ])
    bo_row = r6["bo_row"]

    pbA = nc.alloc_psum_tensor("pbA", [128, 2048], F32)
    pbt = nc.alloc_psum_tensor("pbt", [128, 2048], BF16)
    pb6 = nc.alloc_psum_tensor("pb6", [128, 512], F32)
    pb7 = nc.alloc_psum_tensor("pb7", [128, 512], F32)

    class Sem:
        def __init__(s, name):
            s.h = nc.alloc_semaphore(name)
            s.n = 0

    def sems(prefix, n):
        return [Sem(f"{prefix}{i}") for i in range(n)]

    sPE, sV, sS, sPool, sBAR = (Sem(x) for x in ("sPE", "sV", "sS", "sPool", "sBAR"))
    sC = Sem("sC")
    sX = sems("sX", 3)
    sWS = sems("sWS", 2)
    sW2 = sems("sW2", 2)
    sDS = sems("sDS", 2)
    sHT = sems("sHT", 2)
    sQT = sems("sQT", 2)
    sUT = sems("sUT", 2)
    sXT = sems("sXT", 2)
    sFT = sems("sFT", 2)
    sSQ = sems("sSQ", 2)
    sSU = sems("sSU", 2)
    sSH = sems("sSH", 2)
    sSF = sems("sSF", 2)
    sSO = sems("sSO", 2)
    sKV = Sem("sKV")
    sCC = Sem("sCC")
    sSZ = sems("sSZ", 2)
    sZT = sems("sZT", 2)
    sPC = Sem("sPC")
    sPCS = Sem("sPCS")
    sPC3 = Sem("sPC3")
    sPS3 = Sem("sPS3")
    LOAD_SEMS = [sC, *sX, *sWS, *sW2, *sDS, *sHT, *sQT, *sUT, *sXT, *sFT, *sZT,
                 sPC, sPC3]
    STORE_SEMS = [*sSQ, *sSU, *sSH, *sSF, *sSO, sKV, *sSZ, sPCS, sPS3]
    bar_n = [0]

    SP, PE, V, S, PL = nc.sync, nc.tensor, nc.vector, nc.scalar, nc.gpsimd

    def ld(sem, dst, src):
        SP.dma_start(dst, src).then_inc(sem.h, 16)
        sem.n += 16
        return sem.n

    def st(sem, dst, src):
        PL.dma_start(dst, src).then_inc(sem.h, 16)
        sem.n += 16
        return sem.n

    def barrier():
        for q in LOAD_SEMS:
            SP.wait_ge(q.h, q.n)
        for q in STORE_SEMS:
            PL.wait_ge(q.h, q.n)
        for eng in (SP, PE, V, S, PL):
            eng.drain()
            eng.nop().then_inc(sBAR.h, 1)
        bar_n[0] += 5
        for eng in (SP, PE, V, S, PL):
            eng.wait_ge(sBAR.h, bar_n[0])

    ld(sC, ident[:], id_h[:])
    ld(sC, bu_fm[:], bu_h[:])
    ld(sC, lng_fm[:], lng_h[:])
    ld(sC, lnb_fm[:], lnb_h[:])
    ld(sC, gs_fm[:], gs_h[:])
    ld(sC, us_fm[:], us_h[:])

    for ap, val in ((ones_row_bf[:], 1.0), (ones_col_bf[:], 1.0), (ones_row_f[:], 1.0),
                    (cells[:, 0:1], EPS), (cells[:, 1:2], 1.0 / D), (cells[:, 2:3], -0.5),
                    (cells[:, 3:4], -1.0),
                    (kv_sb[:], 0.0)):
        PL.memset(ap, val).then_inc(sPool.h, 1)
        sPool.n += 1

    for bh, brow in ((bqk_h, bqk_row), (bv_h, bv_row)):
        n = ld(sC, row_stage[:], bh[:])
        V.wait_ge(sC.h, n)
        V.tensor_copy(brow[:], row_stage[:]).then_inc(sV.h, 1)
        sV.n += 1
        SP.wait_ge(sV.h, sV.n)

    def load_qv_weights(sw):
        c0 = sw * 512
        marks = [0, 0]
        for wi, (wsrc, wbf) in enumerate(((wqk_bf_s, r1["wq_bf"]),
                                          (wv_bf_s, r1["wv_bf"]))):
            dst = wbf.rearrange("p (k c) -> p k c", k=NK)
            src = wsrc.rearrange("k p c -> p k c")[:, :, c0:c0 + 512]
            for p4 in range(4):
                marks[wi] = ld(sWS[wi], dst[:, p4 * 4:(p4 + 1) * 4, :],
                               src[:, p4 * 4:(p4 + 1) * 4, :])
        return marks

    w_ready = [[0, 0]]
    barrier()

    m_xld = [0] * (NTO + 4)
    m_norm = [0] * (NTO + 4)
    m_sq = [0] * (NTO + 4)
    m_tr = [0] * (NTO + 4)
    m_cpx = [0] * (NTO + 4)
    m_projg = [0] * (NTO + 4)
    m_evS = [0] * (NTO + 4)
    m_evV = [0] * (NTO + 4)
    m_kvmm = [0] * (NTO + 4)
    m_kvadd = [0] * (NTO + 4)
    m_xnst = [0] * (NTO + 4)
    m_qst = [0] * (NTO + 4)

    def reset_marks():
        for a_ in (m_xld, m_norm, m_sq, m_tr, m_cpx, m_projg, m_evS, m_evV,
                   m_kvmm, m_kvadd, m_xnst, m_qst):
            for t in range(len(a_)):
                a_[t] = 0

    m_rstd = [0] * (NTO + 4)

    def emit_ln_A(i):
        xb = r0["xbuf"][i % 3]
        lo = (i % 3) * 32
        sm = smalls
        V.wait_ge(sX[i % 3].h, m_xld[i])
        if i >= 3:
            V.wait_ge(sS.h, m_rstd[i - 3])
        for k in range(4):
            bs_ = V.bn_stats(sm[:, lo + k * 6:lo + (k + 1) * 6],
                             xb[:, k * 512:(k + 1) * 512])
        bs_.then_inc(sV.h, 1)
        sV.n += 1
        V.wait_ge(sV.h, sV.n)
        V.bn_aggr(sm[:, lo + 24:lo + 26], sm[:, lo:lo + 24]).then_inc(sV.h, 1)
        sV.n += 1
        agg_c = sV.n
        S.wait_ge(sV.h, agg_c)
        S.activation(sm[:, lo + 26:lo + 27], sm[:, lo + 25:lo + 26], AF.Ln,
                     bias=cells[:, 0:1]).then_inc(sS.h, 1)
        sS.n += 1
        S.wait_ge(sS.h, sS.n)
        S.activation(sm[:, lo + 27:lo + 28], sm[:, lo + 26:lo + 27], AF.Exp,
                     scale=cells[:, 2:3]).then_inc(sS.h, 1)
        sS.n += 1
        S.wait_ge(sS.h, sS.n)
        S.activation(sm[:, lo + 28:lo + 29], sm[:, lo + 24:lo + 25], AF.Identity,
                     scale=sm[:, lo + 27:lo + 28]).then_inc(sS.h, 1)
        sS.n += 1
        S.wait_ge(sS.h, sS.n)
        S.activation(sm[:, lo + 28:lo + 29], sm[:, lo + 28:lo + 29], AF.Identity,
                     scale=cells[:, 3:4]).then_inc(sS.h, 1)
        sS.n += 1
        m_rstd[i] = sS.n

    def emit_ln_B(i):
        xb = r0["xbuf"][i % 3]
        xt = r0["xn_tm"][i % 3]
        lo = (i % 3) * 32
        sm = smalls
        S.wait_ge(sS.h, m_rstd[i])
        if i >= 3:
            S.wait_ge(sPE.h, m_tr[i - 3])
        S.activation(xt[:], xb[:], AF.Identity, scale=sm[:, lo + 27:lo + 28],
                     bias=sm[:, lo + 28:lo + 29]).then_inc(sS.h, 1)
        sS.n += 1
        m_norm[i] = sS.n

    def emit_transposes(i):
        xt = r0["xn_tm"][i % 3]
        PE.wait_ge(sS.h, m_norm[i])
        if i >= 1:
            PE.wait_ge(sV.h, m_cpx[i - 1])
        for k in range(NK):
            ti_ = PE.transpose(pbt[:, k * 128:(k + 1) * 128],
                               xt[:, k * 128:(k + 1) * 128], ident[:])
        ti_.then_inc(sPE.h, 1)
        sPE.n += 1
        m_tr[i] = sPE.n
        V.wait_ge(sPE.h, m_tr[i])
        V.tensor_copy(xn_v[:, :, i * 128:(i + 1) * 128],
                      pbt.rearrange("p (k t) -> p k t", k=NK)).then_inc(sV.h, 1)
        sV.n += 1
        m_cpx[i] = sV.n

    def emit_tile_proj(i, sw):
        c0 = sw * 512
        qb = r1["q_tm"][i % 2]
        vb = r1["v_tm"][i % 2]
        po = (i % 2) * 512
        qsl = pbA[:, po:po + 512]
        vsl = pbA[:, 1024 + po:1024 + po + 512]
        if i == 0:
            PE.wait_ge(sWS[0].h, w_ready[0][0])
            PE.wait_ge(sWS[1].h, w_ready[0][1])
        if i >= 2:
            PE.wait_ge(sS.h, m_evS[i - 2])
            PE.wait_ge(sV.h, m_evV[i - 2])
        for k in range(NK):
            lhs = xn_v[:, k, i * 128:(i + 1) * 128]
            PE.matmul(qsl, lhs, r1["wq_bf"][:, k * 512:(k + 1) * 512],
                      start=(k == 0), stop=False)
            PE.matmul(vsl, lhs, r1["wv_bf"][:, k * 512:(k + 1) * 512],
                      start=(k == 0), stop=False)
        PE.matmul(qsl, ones_row_bf[:], bqk_row[0:1, c0:c0 + 512],
                  start=False, stop=True)
        mm_ = PE.matmul(vsl, ones_row_bf[:], bv_row[0:1, c0:c0 + 512],
                        start=False, stop=True)
        mm_.then_inc(sPE.h, 1)
        sPE.n += 1
        m_projg[i] = sPE.n
        S.wait_ge(sPE.h, m_projg[i])
        if i >= 2:
            S.wait_ge(sV.h, m_evV[i - 2])
            S.wait_ge(sPE.h, m_kvmm[i - 2])
        S.activation(r1["e_buf"][i % 2][:], qsl, AF.Exp)
        S.activation(vb[:], vsl, AF.Silu).then_inc(sS.h, 1)
        sS.n += 1
        m_evS[i] = sS.n
        V.wait_ge(sS.h, m_evS[i])
        if i >= 2:
            V.wait_ge(sPE.h, m_kvmm[i - 2])
            V.wait_ge(sSQ[i % 2].h, m_qst[i - 2])
        V.tensor_scalar(r1["r_buf"][i % 2][:], qsl, 0.0, 0.0,
                        ALU.max).then_inc(sV.h, 1)
        sV.n += 1
        V.wait_ge(sV.h, sV.n)
        V.scalar_tensor_tensor(qb[:], r1["e_buf"][i % 2][:], -1.0,
                               r1["r_buf"][i % 2][:], ALU.add,
                               ALU.min).then_inc(sV.h, 1)
        sV.n += 1
        m_evV[i] = sV.n

    def emit_kv(i, sw):
        qb = r1["q_tm"][i % 2]
        vb = r1["v_tm"][i % 2]
        kvp = pb6 if i % 2 == 0 else pb7
        PE.wait_ge(sV.h, m_evV[i])
        PE.wait_ge(sS.h, m_evS[i])
        if i >= 2:
            PE.wait_ge(sV.h, m_kvadd[i - 2])
        for j in range(4):
            km_ = PE.matmul(kvp[:, j * 128:(j + 1) * 128],
                            qb[:, j * 128:(j + 1) * 128],
                            vb[:, j * 128:(j + 1) * 128], start=True, stop=True)
        km_.then_inc(sPE.h, 1)
        sPE.n += 1
        m_kvmm[i] = sPE.n
        col0 = sw * 512
        V.wait_ge(sPE.h, m_kvmm[i])
        if i >= 1:
            V.wait_ge(sV.h, m_kvadd[i - 1])
        V.tensor_tensor(kv_sb[:, col0:col0 + 512], kv_sb[:, col0:col0 + 512],
                        kvp[:], ALU.add).then_inc(sV.h, 1)
        sV.n += 1
        m_kvadd[i] = sV.n

    pc0_cast = [0]
    pc0_st = [0]

    def pre_cast_qv(u):
        wsrc = wqk_h if u < NK else wv_h
        wdst = wqk_bf_s if u < NK else wv_bf_s
        k = u % NK
        SP.wait_ge(sV.h, pc0_cast[0])
        n = ld(sPC, r0["pc0_f32"][:], wsrc[k])
        V.wait_ge(sPC.h, n)
        V.wait_ge(sPCS.h, pc0_st[0])
        V.tensor_copy(r0["pc0_bf"][:], r0["pc0_f32"][:]).then_inc(sV.h, 1)
        sV.n += 1
        pc0_cast[0] = sV.n
        PL.wait_ge(sV.h, sV.n)
        pc0_st[0] = st(sPCS, wdst[k], r0["pc0_bf"][:])

    for i0 in range(3):
        m_xld[i0] = ld(sX[i0], r0["xbuf"][i0][:], x_h[i0 * 128:(i0 + 1) * 128, :])
    for i in range(NTO):
        emit_ln_A(i)
        if i >= 1:
            emit_ln_B(i - 1)
            emit_transposes(i - 1)
            pre_cast_qv(i - 1)
            j = i + 2
            if j < NTO:
                SP.wait_ge(sS.h, m_norm[i - 1])
                m_xld[j] = ld(sX[j % 3], r0["xbuf"][j % 3][:],
                              x_h[j * 128:(j + 1) * 128, :])
    emit_ln_B(NTO - 1)
    emit_transposes(NTO - 1)
    pre_cast_qv(NTO - 1)
    barrier()

    for sw in range(4):
        w_ready[0] = load_qv_weights(sw)
        reset_marks()
        for i in range(NTO):
            emit_tile_proj(i, sw)
            if i >= 1:
                emit_kv(i - 1, sw)
            if i > 0:
                PL.wait_ge(sV.h, m_evV[i - 1])
                m_qst[i - 1] = st(sSQ[(i - 1) % 2],
                                  q_s[(i - 1) * 128:i * 128, sw * 512:(sw + 1) * 512],
                                  r1["q_tm"][(i - 1) % 2][:])
        emit_kv(NTO - 1, sw)
        PL.wait_ge(sV.h, m_evV[NTO - 1])
        m_qst[NTO - 1] = st(sSQ[(NTO - 1) % 2],
                            q_s[(NTO - 1) * 128:NTO * 128, sw * 512:(sw + 1) * 512],
                            r1["q_tm"][(NTO - 1) % 2][:])
        barrier()

    st(sKV, kv_cc_in[:], kv_sb[:])
    PL.wait_ge(sKV.h, sKV.n)
    PL.collective_compute(
        "AllReduce", ALU.add, replica_groups=RG,
        ins=[kv_cc_in.ap().opt()], outs=[kv_cc_out.ap().opt()],
    ).then_inc(sCC.h)
    sCC.n = 1

    TG2 = T_OWN // 512
    wu_pe_done = [0, 0]
    wu_cast = [0, 0]
    u_evac = {}
    u_store = {}
    w2_free = [0]
    stg2_free = [0, 0]
    for m in range(NK):
        j = m % 2
        SP.wait_ge(sV.h, stg2_free[j])
        if m >= 2:
            SP.wait_ge(sPE.h, wu_pe_done[j])
        n = ld(sW2[j], r2["wstg2"][j].rearrange("p (c m) -> p c m", c=NK),
               wu_h[m].rearrange("(c p) m -> p c m", p=128))
        V.wait_ge(sW2[j].h, n)
        V.tensor_copy(r2["uw_bf"][j][:], r2["wstg2"][j][:]).then_inc(sV.h, 1)
        sV.n += 1
        wu_cast[j] = sV.n
        stg2_free[j] = sV.n
        PE.wait_ge(sV.h, wu_cast[j])
        for tg in range(TG2):
            u_idx = m * TG2 + tg
            psl = pbA[:, (u_idx % 2) * 512:(u_idx % 2) * 512 + 512]
            prev = u_idx - 2
            if prev >= 0:
                PE.wait_ge(sS.h, u_evac.get(prev, 0))
            for k in range(NK):
                um_ = PE.matmul(psl, r2["uw_bf"][j][:, k * 128:(k + 1) * 128],
                                xn_own[:, k * T_OWN + tg * 512:k * T_OWN + tg * 512 + 512],
                                start=(k == 0), stop=(k == NK - 1))
            um_.then_inc(sPE.h, 1)
            sPE.n += 1
            if tg == TG2 - 1:
                wu_pe_done[j] = sPE.n
            S.wait_ge(sPE.h, sPE.n)
            if prev >= 0:
                S.wait_ge(sSU[u_idx % 2].h, u_store.get(prev, 0))
            S.activation(r2["u_buf"][u_idx % 2][:], psl, AF.Silu,
                         bias=bu_fm[:, m:m + 1]).then_inc(sS.h, 1)
            sS.n += 1
            u_evac[u_idx] = sS.n
            PL.wait_ge(sS.h, sS.n)
            u_store[u_idx] = st(
                sSU[u_idx % 2],
                u_s.rearrange("m p t -> p m t")[:, m, tg * 512:(tg + 1) * 512],
                r2["u_buf"][u_idx % 2][:])

    barrier()

    g_evac = {}
    h_evac = {}
    h_store = {}
    w_pe_done = [0, 0]
    stg_free = [0, 0]
    cast_done = [0] * (NI + 1)
    gw_n = [0] * (NI + 1)
    uw_n = [0] * (NI + 1)

    def p3_ld_gw(i):
        j = i % 2
        SP.wait_ge(sV.h, stg_free[j])
        gw_n[i] = ld(sW2[j], r2["wstg2"][j].rearrange("p (c m) -> p c m", c=NK),
                     gw_h[i].rearrange("(c p) m -> p c m", p=128))

    def p3_cast_gw(i):
        j = i % 2
        V.wait_ge(sW2[j].h, gw_n[i])
        if i >= 2:
            V.wait_ge(sPE.h, w_pe_done[j])
        V.tensor_copy(r2["gw_bf"][j][:], r2["wstg2"][j][:]).then_inc(sV.h, 1)
        sV.n += 1

    def p3_ld_uw(i):
        j = i % 2
        SP.wait_ge(sV.h, sV.n)
        uw_n[i] = ld(sW2[j], r2["wstg2"][j].rearrange("p (c m) -> p c m", c=NK),
                     uw_h[i].rearrange("(c p) m -> p c m", p=128))

    def p3_cast_uw(i):
        j = i % 2
        V.wait_ge(sW2[j].h, uw_n[i])
        V.tensor_copy(r2["uw_bf"][j][:], r2["wstg2"][j][:]).then_inc(sV.h, 1)
        sV.n += 1
        stg_free[j] = sV.n
        cast_done[i] = sV.n

    pc3_cast = [0]
    pc3_st = [0]

    def pre_cast_dw(u):
        i_dw, q = u // 4, u % 4
        SP.wait_ge(sV.h, pc3_cast[0])
        n = ld(sPC3, r2["u_buf"][0][:], dw_h[i_dw, :, q * 512:(q + 1) * 512])
        V.wait_ge(sPC3.h, n)
        V.wait_ge(sPS3.h, pc3_st[0])
        V.tensor_copy(r2["pc3_bf"][:], r2["u_buf"][0][:]).then_inc(sV.h, 1)
        sV.n += 1
        pc3_cast[0] = sV.n
        PL.wait_ge(sV.h, sV.n)
        pc3_st[0] = st(sPS3, dw_bf_s[i_dw, :, q * 512:(q + 1) * 512],
                       r2["pc3_bf"][:])

    p3_ld_gw(0)
    p3_cast_gw(0)
    p3_ld_uw(0)
    p3_cast_uw(0)
    for i in range(NI):
        j = i % 2
        PE.wait_ge(sV.h, cast_done[i])
        for tg in range(TG2):
            if i + 1 < NI:
                if tg == 0:
                    p3_ld_gw(i + 1)
                elif tg == 1:
                    p3_cast_gw(i + 1)
                elif tg == 2:
                    p3_ld_uw(i + 1)
                elif tg == 3:
                    p3_cast_uw(i + 1)
            if tg >= 4:
                pre_cast_dw(i * 4 + (tg - 4))
            f_idx = i * TG2 + tg
            po = (f_idx % 2) * 512
            gsl = pbA[:, po:po + 512]
            usl = pbA[:, 1024 + po:1024 + po + 512]
            prev = f_idx - 2
            if prev >= 0:
                PE.wait_ge(sS.h, g_evac.get(prev, 0))
            for k in range(NK):
                rhs = xn_own[:, k * T_OWN + tg * 512:k * T_OWN + tg * 512 + 512]
                PE.matmul(gsl, r2["gw_bf"][j][:, k * 128:(k + 1) * 128], rhs,
                          start=(k == 0), stop=(k == NK - 1))
                fm_ = PE.matmul(usl, r2["uw_bf"][j][:, k * 128:(k + 1) * 128], rhs,
                                start=(k == 0), stop=(k == NK - 1))
            fm_.then_inc(sPE.h, 1)
            sPE.n += 1
            if tg == TG2 - 1:
                w_pe_done[j] = sPE.n
            S.wait_ge(sPE.h, sPE.n)
            if prev >= 0:
                S.wait_ge(sV.h, h_evac.get(prev, 0))
            S.activation(r2["sg_buf"][f_idx % 2][:], gsl, AF.Silu,
                         scale=gs_fm[:, i:i + 1])
            S.activation(r2["u_raw"][f_idx % 2][:], usl,
                         AF.Identity).then_inc(sS.h, 1)
            sS.n += 1
            g_evac[f_idx] = sS.n
            V.wait_ge(sS.h, sS.n)
            if prev >= 0:
                V.wait_ge(sSH[f_idx % 2].h, h_store.get(prev, 0))
            V.scalar_tensor_tensor(r2["hdn_buf"][f_idx % 2][:],
                                   r2["u_raw"][f_idx % 2][:], us_fm[:, i:i + 1],
                                   r2["sg_buf"][f_idx % 2][:], ALU.mult,
                                   ALU.mult).then_inc(sV.h, 1)
            sV.n += 1
            h_evac[f_idx] = sV.n
            PL.wait_ge(sV.h, sV.n)
            h_store[f_idx] = st(
                sSH[f_idx % 2],
                hdn_s.rearrange("i p t -> p i t")[:, i, tg * 512:(tg + 1) * 512],
                r2["hdn_buf"][f_idx % 2][:])

    barrier()

    n_ds = ld(sC, ds_repl[:], ds_h[:])
    V.wait_ge(sC.h, n_ds)
    dwv = dw_bf_s.rearrange("i p c -> p i c")
    for half in range(2):
        dst_dw = r4["dw_bf"].rearrange("p (i c) -> p i c", i=NI)
        n_dw = 0
        for g in range(11):
            i0, i1 = g * 4, min((g + 1) * 4, NI)
            n_dw = ld(sDS[g % 2], dst_dw[:, i0:i1, :],
                      dwv[:, i0:i1, half * 1024:(half + 1) * 1024])
        PE.wait_ge(sDS[0].h, sDS[0].n)
        PE.wait_ge(sDS[1].h, sDS[1].n)

        p4_mm = {}
        f_evac = {}
        f_store = {}
        for t in range(NTO):
            j = t % 2
            if t >= 2:
                SP.wait_ge(sPE.h, p4_mm.get(t - 2, 0))
            nlast = 0
            hsrc = hdn_s.rearrange("i p t -> p i t")[:, :, t * 128:(t + 1) * 128]
            hdst = r4["hdn_t"][j].rearrange("p (i t) -> p i t", i=NI)
            for piece in range(11):
                i0, i1 = piece * 4, min((piece + 1) * 4, NI)
                nlast = ld(sHT[j], hdst[:, i0:i1, :], hsrc[:, i0:i1, :])
            PE.wait_ge(sHT[j].h, nlast)
            b0 = (t % 2) * 1024
            if t >= 2:
                PE.wait_ge(sV.h, f_evac.get(t - 2, 0))
            for i in range(NI):
                lhs = r4["hdn_t"][j][:, i * 128:(i + 1) * 128]
                PE.matmul(pbA[:, b0:b0 + 512], lhs,
                          r4["dw_bf"][:, i * 1024:i * 1024 + 512],
                          start=(i == 0), stop=(i == NI - 1))
                dm_ = PE.matmul(pbA[:, b0 + 512:b0 + 1024], lhs,
                                r4["dw_bf"][:, i * 1024 + 512:i * 1024 + 1024],
                                start=(i == 0), stop=(i == NI - 1))
            dm_.then_inc(sPE.h, 1)
            sPE.n += 1
            p4_mm[t] = sPE.n
            V.wait_ge(sPE.h, sPE.n)
            V.wait_ge(sSF[j].h, f_store.get(t - 2, 0))
            V.tensor_tensor(r4["ffn_buf"][j][:], pbA[:, b0:b0 + 1024],
                            ds_repl[:, half * 1024:(half + 1) * 1024],
                            ALU.mult).then_inc(sV.h, 1)
            sV.n += 1
            f_evac[t] = sV.n
            PL.wait_ge(sV.h, sV.n)
            f_store[t] = st(sSF[j],
                            ffn_s[t * 128:(t + 1) * 128, half * 1024:(half + 1) * 1024],
                            r4["ffn_buf"][j][:])
        barrier()

    SP.wait_ge(sCC.h, 1)
    n_kv = ld(sC, kv_sb[:], kv_cc_out[:])
    S.wait_ge(sC.h, n_kv)
    S.activation(kv_bf[:], kv_sb[:], AF.Identity).then_inc(sS.h, 1)
    sS.n += 1

    V.wait_ge(sC.h, n_kv)
    kv3 = kv_sb.rearrange("p (m e) -> p m e", m=NK)
    scA = r5["kvscA"].rearrange("p (m e) -> p m e", m=NK)
    scB = r5["kvscB"].rearrange("p (m e) -> p m e", m=NK)
    tree = [(scA, kv3, kv3, 64), (scB, scA, scA, 32), (scA, scB, scB, 16),
            (scB, scA, scA, 8), (scA, scB, scB, 4), (scB, scA, scA, 2),
            (r5["kvs_f"].rearrange("p (m o) -> p m o", o=1), scB, scB, 1)]
    for dst, s0, s1, w in tree:
        V.tensor_tensor(dst[:, :, 0:w], s0[:, :, 0:w], s1[:, :, w:2 * w],
                        ALU.add).then_inc(sV.h, 1)
        sV.n += 1
        V.wait_ge(sV.h, sV.n)
    S.wait_ge(sV.h, sV.n)
    S.activation(r5["kvs_bf"][:], r5["kvs_f"][:], AF.Identity).then_inc(sS.h, 1)
    sS.n += 1
    kvs_ready = sS.n

    NTG = T_OWN // 256
    uv_h = u_s.rearrange("m p t -> p m t")
    zv_h = z_s.rearrange("k p t -> p k t")
    trp = [0, 0]
    pbt_ev = [0]
    outmm_last = [0] * (NTG + 2)
    stats_mm = [0] * (NTG + 2)
    mean_mm = [0] * (NTG + 2)
    mbc_mm = [0] * (NTG + 2)
    mean_ev = [0] * (NTG + 2)
    mr_mark = [0] * (NTG + 2)
    rr_mark = [0] * (NTG + 2)
    bc7_mm = [0] * (NTG + 2)
    bc_ev = [0] * (NTG + 2)
    t1_last = [0] * (NTG + 2)
    t2_last = [0] * (NTG + 2)
    z_g = [0] * (NTG + 2)
    z_st = [0, 0]
    u_ld = [0, 0]
    q_ld = [0, 0]
    oevS = [0] * 4
    oevV = [0] * 4
    tt1, tt2, tt3, ttz = {}, {}, {}, {}

    evac_arr = [0] * (NTG + 2)

    def p5a_A1(tg):
        p = tg % 2
        t0 = tg * 256
        p = tg % 2
        t0 = tg * 256
        pc = p * 256
        for ti in range(2):
            SP.wait_ge(sPE.h, trp[ti])
            q_ld[ti] = ld(sQT[ti], r5["qtile"][ti][:],
                          q_s[(tg * 2 + ti) * 128:(tg * 2 + ti + 1) * 128, :])
        if tg >= 2:
            SP.wait_ge(sV.h, z_g[tg - 2])
        dst_u = r5["u_tile"][p].rearrange("p (m t) -> p m t", m=NK)
        for p4 in range(4):
            n_u = ld(sUT[p], dst_u[:, p4 * 4:(p4 + 1) * 4, :],
                     uv_h[:, p4 * 4:(p4 + 1) * 4, t0:t0 + 256])
        u_ld[p] = n_u
        for ti in range(2):
            PE.wait_ge(sQT[ti].h, q_ld[ti])
            PE.wait_ge(sS.h, pbt_ev[0])
            for k in range(NK):
                tq_ = PE.transpose(pbt[:, k * 128:(k + 1) * 128],
                                   r5["qtile"][ti][:, k * 128:(k + 1) * 128],
                                   ident[:])
            tq_.then_inc(sPE.h, 1)
            sPE.n += 1
            trp[ti] = sPE.n
            S.wait_ge(sPE.h, trp[ti])
            if ti == 0 and tg >= 2:
                S.wait_ge(sPE.h, outmm_last[tg - 2])
            S.activation(
                r5["qfm"][p].rearrange("p (m t) -> p m t", m=NK)[:, :, ti * 128:(ti + 1) * 128],
                pbt.rearrange("p (m t) -> p m t", m=NK)[:, :, 0:128],
                AF.Identity).then_inc(sS.h, 1)
            sS.n += 1
            pbt_ev[0] = sS.n
        qfm_ready = sS.n
        PE.wait_ge(sS.h, qfm_ready)
        if tg == 0:
            PE.wait_ge(sS.h, kvs_ready)
        if tg >= 1:
            PE.wait_ge(sS.h, mean_ev[tg - 1])
            PE.wait_ge(sV.h, mr_mark[tg - 1])
        for m in range(NK):
            mn_ = PE.matmul(pb6[0:1, pc:pc + 256], r5["kvs_bf"][:, m:m + 1],
                            r5["qfm"][p][:, m * 256:(m + 1) * 256],
                            start=(m == 0), stop=(m == NK - 1))
        mn_.then_inc(sPE.h, 1)
        sPE.n += 1
        mean_mm[tg] = sPE.n
        mr = mean_row[0:1, pc:pc + 256]
        V.wait_ge(sPE.h, mean_mm[tg])
        if tg >= 2:
            V.wait_ge(sPE.h, mbc_mm[tg - 2])
        V.tensor_scalar_mul(mr, pb6[0:1, pc:pc + 256], 1.0 / D).then_inc(sV.h, 1)
        sV.n += 1
        mr_mark[tg] = sV.n

    def p5a_A2(tg):
        p = tg % 2
        t0 = tg * 256
        pc = p * 256
        mr = mean_row[0:1, pc:pc + 256]
        for m in range(NK):
            slot = p * 2 + m % 2
            psl = pbA[:, slot * 512:slot * 512 + 256]
            PE.wait_ge(sS.h, oevS[slot])
            PE.matmul(psl, kv_bf[:, m * 128:(m + 1) * 128],
                      r5["qfm"][p][:, m * 256:(m + 1) * 256],
                      start=True, stop=True).then_inc(sPE.h, 1)
            sPE.n += 1
            S.wait_ge(sPE.h, sPE.n)
            if m == 0 and tg >= 2:
                S.wait_ge(sPE.h, stats_mm[tg - 2])
                S.wait_ge(sV.h, z_g[tg - 2])
            S.activation(r5["outst"][p][:, m * 256:(m + 1) * 256], psl,
                         AF.Identity)
            S.activation(r5["sq_bf"][p][:, m * 256:(m + 1) * 256], psl,
                         AF.Square).then_inc(sS.h, 1)
            sS.n += 1
            oevS[slot] = sS.n
        outmm_last[tg] = sPE.n
        evac_arr[tg] = sS.n
        PE.wait_ge(sV.h, mr_mark[tg])
        PE.matmul(pb6[:, pc:pc + 256], ones_row_f[:], mr,
                  start=True, stop=True).then_inc(sPE.h, 1)
        sPE.n += 1
        mbc_mm[tg] = sPE.n
        S.wait_ge(sPE.h, mbc_mm[tg])
        if tg >= 2:
            S.wait_ge(sV.h, z_g[tg - 2])
        S.activation(r5["mean_sb"][p][:], pb6[:, pc:pc + 256],
                     AF.Identity).then_inc(sS.h, 1)
        sS.n += 1
        mean_ev[tg] = sS.n


    def p5a_B1(tg):
        p = tg % 2
        t0 = tg * 256
        pc = p * 256
        PE.wait_ge(sS.h, evac_arr[tg])
        if tg >= 1:
            PE.wait_ge(sS.h, bc_ev[tg - 1])
            PE.wait_ge(sV.h, rr_mark[tg - 1])
        for m in range(NK):
            sm_ = PE.matmul(pb7[0:1, pc:pc + 256], ones_col_bf[:],
                            r5["sq_bf"][p][:, m * 256:(m + 1) * 256],
                            start=(m == 0), stop=(m == NK - 1))
        sm_.then_inc(sPE.h, 1)
        sPE.n += 1
        stats_mm[tg] = sPE.n
        mr = mean_row[0:1, pc:pc + 256]
        rr = rstd_row[0:1, pc:pc + 256]
        V.wait_ge(sPE.h, stats_mm[tg])
        V.wait_ge(sV.h, mr_mark[tg])
        if tg >= 2:
            V.wait_ge(sPE.h, bc7_mm[tg - 2])
        V.tensor_tensor(rr, mr, mr, ALU.mult).then_inc(sV.h, 1)
        sV.n += 1
        V.wait_ge(sV.h, sV.n)
        V.scalar_tensor_tensor(rr, pb7[0:1, pc:pc + 256], 1.0 / D, rr,
                               ALU.mult, ALU.subtract).then_inc(sV.h, 1)
        sV.n += 1
        rr_mark[tg] = sV.n
        S.wait_ge(sV.h, sV.n)
        S.activation(rr, rr, AF.Ln, bias=cells[0:1, 0:1]).then_inc(sS.h, 1)
        sS.n += 1
        S.wait_ge(sS.h, sS.n)
        S.activation(rr, rr, AF.Exp, scale=cells[0:1, 2:3]).then_inc(sS.h, 1)
        sS.n += 1
        rows_c = sS.n
        PE.wait_ge(sS.h, rows_c)
        PE.matmul(pb7[:, pc:pc + 256], ones_row_f[:], rr,
                  start=True, stop=True).then_inc(sPE.h, 1)
        sPE.n += 1
        bc7_mm[tg] = sPE.n
        S.wait_ge(sPE.h, bc7_mm[tg])
        if tg >= 2:
            S.wait_ge(sV.h, z_g[tg - 2])
        S.activation(r5["rstd_sb"][p][:], pb7[:, pc:pc + 256],
                     AF.Identity).then_inc(sS.h, 1)
        sS.n += 1
        bc_ev[tg] = sS.n

    def p5a_B2(tg):
        p = tg % 2
        t0 = tg * 256
        pc = p * 256
        for mp in range(NK // 2):
            ms = (2 * mp, 2 * mp + 1)
            marks = {}
            if mp == 0:
                V.wait_ge(sS.h, bc_ev[tg])
                V.wait_ge(sUT[p].h, u_ld[p])
                V.wait_ge(sSZ[p].h, z_st[p])
            for m in ms:
                V.tensor_tensor(r5["t1_b"][m % 2][:],
                                r5["outst"][p][:, m * 256:(m + 1) * 256],
                                r5["mean_sb"][p][:],
                                ALU.subtract).then_inc(sV.h, 1)
                sV.n += 1
                marks[("t1", m)] = sV.n
            for m in ms:
                V.wait_ge(sV.h, marks[("t1", m)])
                V.scalar_tensor_tensor(r5["t2_b"][m % 2][:], r5["t1_b"][m % 2][:],
                                       lng_fm[:, m:m + 1], r5["rstd_sb"][p][:],
                                       ALU.mult, ALU.mult).then_inc(sV.h, 1)
                sV.n += 1
                marks[("t2", m)] = sV.n
            for m in ms:
                V.wait_ge(sV.h, marks[("t2", m)])
                V.scalar_tensor_tensor(r5["zfm"][p][:, m * 256:(m + 1) * 256],
                                       r5["t2_b"][m % 2][:],
                                       lnb_fm[:, m:m + 1],
                                       r5["u_tile"][p][:, m * 256:(m + 1) * 256],
                                       ALU.add, ALU.mult).then_inc(sV.h, 1)
                sV.n += 1
        t1_last[tg] = sV.n
        t2_last[tg] = sV.n
        z_g[tg] = sV.n
        PL.wait_ge(sV.h, z_g[tg])
        src_z = r5["zfm"][p].rearrange("p (k t) -> p k t", k=NK)
        for p4 in range(4):
            n_z = st(sSZ[p], zv_h[:, p4 * 4:(p4 + 1) * 4, t0:t0 + 256],
                     src_z[:, p4 * 4:(p4 + 1) * 4, :])
        z_st[p] = n_z


    p5a_A1(0)
    p5a_A2(0)
    for tg in range(1, NTG):
        p5a_A1(tg)
        p5a_B1(tg - 1)
        p5a_A2(tg)
        p5a_B2(tg - 1)
    p5a_B1(NTG - 1)
    p5a_B2(NTG - 1)
    barrier()

    n_bo = ld(sC, r6["wstg6"][0][0:1, :], bo_h[:])
    V.wait_ge(sC.h, n_bo)
    V.tensor_copy(bo_row[:], r6["wstg6"][0][0:1, :]).then_inc(sV.h, 1)
    sV.n += 1
    wo_cast = [sV.n, 0]
    wo_k = [0] * NK
    for k in range(NK):
        j = k % 2
        SP.wait_ge(sV.h, wo_cast[j])
        n = ld(sWS[j], r6["wstg6"][j][:], wo_h[k])
        V.wait_ge(sWS[j].h, n)
        V.tensor_copy(r6["wo_bf"][:, k * D:(k + 1) * D],
                      r6["wstg6"][j][:]).then_inc(sV.h, 1)
        sV.n += 1
        wo_cast[j] = sV.n
        wo_k[k] = sV.n

    ot_store = [0, 0]
    xf_done = [0, 0]
    xf_rd = [0, 0]
    z_rd = [0, 0]
    half_ev_V = [0, 0]
    for t in range(NTO):
        j = t % 2
        if t >= 2:
            SP.wait_ge(sPool.h, xf_done[j])
        n_x = ld(sXT[j], r6["x_t"][j][:], x_h[t * 128:(t + 1) * 128, :])
        n_f = ld(sFT[j], r6["ffn_t"][j][:], ffn_s[t * 128:(t + 1) * 128, :])
        if t >= 2:
            SP.wait_ge(sPE.h, z_rd[j])
        zt = r6["ztile"][j].rearrange("p (k t) -> p k t", k=NK)
        for p4 in range(4):
            n_z = ld(sZT[j], zt[:, p4 * 4:(p4 + 1) * 4, :],
                     zv_h[:, p4 * 4:(p4 + 1) * 4, t * 128:(t + 1) * 128])
        PL.wait_ge(sXT[j].h, n_x)
        PL.wait_ge(sFT[j].h, n_f)
        if t >= 2:
            PL.wait_ge(sV.h, xf_rd[j])
        PL.tensor_tensor(r6["xf"][j][:], r6["x_t"][j][:], r6["ffn_t"][j][:],
                         ALU.add).then_inc(sPool.h, 1)
        sPool.n += 1
        xf_done[j] = sPool.n
        PE.wait_ge(sZT[j].h, n_z)
        for gp in range(2):
            u = (t * 2 + gp) % 2
            b0 = u * 1024
            if t * 2 + gp >= 2:
                PE.wait_ge(sV.h, half_ev_V[u])
            for k in range(NK):
                if t == 0 and gp == 0:
                    PE.wait_ge(sV.h, wo_k[k])
                lhs = zt[:, k, :]
                PE.matmul(pbA[:, b0:b0 + 512], lhs,
                          r6["wo_bf"][:, k * D + gp * 1024:k * D + gp * 1024 + 512],
                          start=(k == 0), stop=False)
                PE.matmul(pbA[:, b0 + 512:b0 + 1024], lhs,
                          r6["wo_bf"][:, k * D + gp * 1024 + 512:k * D + gp * 1024 + 1024],
                          start=(k == 0), stop=False)
            PE.matmul(pbA[:, b0:b0 + 512], ones_row_bf[:],
                      bo_row[0:1, gp * 1024:gp * 1024 + 512],
                      start=False, stop=True)
            wm_ = PE.matmul(pbA[:, b0 + 512:b0 + 1024], ones_row_bf[:],
                            bo_row[0:1, gp * 1024 + 512:gp * 1024 + 1024],
                            start=False, stop=True)
            wm_.then_inc(sPE.h, 1)
            sPE.n += 1
            if gp == 1:
                z_rd[j] = sPE.n
            V.wait_ge(sPE.h, sPE.n)
            if gp == 0:
                V.wait_ge(sPool.h, xf_done[j])
                V.wait_ge(sSO[j].h, ot_store[j])
            V.tensor_tensor(r6["outb"][j][:, gp * 1024:(gp + 1) * 1024],
                            pbA[:, b0:b0 + 1024],
                            r6["xf"][j][:, gp * 1024:(gp + 1) * 1024],
                            ALU.add).then_inc(sV.h, 1)
            sV.n += 1
            half_ev_V[u] = sV.n
            if gp == 1:
                xf_rd[j] = sV.n
        PL.wait_ge(sV.h, xf_rd[j])
        ot_store[j] = st(sSO[j], out_h[t * 128:(t + 1) * 128, :],
                         r6["outb"][j][:])

    for q in LOAD_SEMS:
        SP.wait_ge(q.h, q.n)
    for q in STORE_SEMS:
        PL.wait_ge(q.h, q.n)
    SP.wait_ge(sSO[0].h, sSO[0].n)
    SP.wait_ge(sSO[1].h, sSO[1].n)
    return nc


def kernel(L0_attn_x, L0_attn_Wqk, L0_attn_bqk, L0_attn_Wv, L0_attn_bv,
           L0_attn_Wu, L0_attn_bu, L0_attn_Wo, L0_attn_bo, L0_attn_ln_g,
           L0_attn_ln_b, L0_ffn_gate_w, L0_ffn_up_w, L0_ffn_down_w,
           L0_ffn_gate_s, L0_ffn_up_s, L0_ffn_down_s):
    f32 = np.float32
    x = np.asarray(L0_attn_x, dtype=f32).reshape(2 * T_BATCH, D)

    def fm_vec(v, nk):
        return np.ascontiguousarray(np.asarray(v, f32).reshape(nk, 128).T)

    common = {
        "wqk": np.ascontiguousarray(np.asarray(L0_attn_Wqk, f32).reshape(NK, 128, D)),
        "wv": np.ascontiguousarray(np.asarray(L0_attn_Wv, f32).reshape(NK, 128, D)),
        "wu": np.ascontiguousarray(
            np.asarray(L0_attn_Wu, f32).reshape(D, NK, 128).transpose(1, 0, 2)),
        "wo": np.ascontiguousarray(np.asarray(L0_attn_Wo, f32).reshape(NK, 128, D)),
        "gatew": np.ascontiguousarray(
            np.asarray(L0_ffn_gate_w, f32).T.reshape(D, NI, 128).transpose(1, 0, 2)),
        "upw": np.ascontiguousarray(
            np.asarray(L0_ffn_up_w, f32).T.reshape(D, NI, 128).transpose(1, 0, 2)),
        "downw": np.ascontiguousarray(
            np.asarray(L0_ffn_down_w, f32).T.reshape(NI, 128, D)),
        "bqk": np.asarray(L0_attn_bqk, f32).reshape(1, D),
        "bv": np.asarray(L0_attn_bv, f32).reshape(1, D),
        "bo": np.asarray(L0_attn_bo, f32).reshape(1, D),
        "bu": fm_vec(L0_attn_bu, NK),
        "lng": fm_vec(L0_attn_ln_g, NK),
        "lnb": fm_vec(L0_attn_ln_b, NK),
        "gs": fm_vec(L0_ffn_gate_s, NI),
        "us": fm_vec(L0_ffn_up_s, NI),
        "ds": np.ascontiguousarray(
            np.tile(np.asarray(L0_ffn_down_s, f32).reshape(1, D), (128, 1))),
        "ident": np.eye(128, dtype=ml_dtypes.bfloat16),
    }

    in_maps = []
    for c in range(8):
        b = c // 4
        o = (c % 4) * T_OWN
        x_core = np.ascontiguousarray(x[b * T_BATCH + o:b * T_BATCH + o + T_OWN])
        m = dict(common)
        m["x"] = x_core
        in_maps.append(m)

    nc = build_nc()
    res = run_bass_kernel_spmd(nc, in_maps, core_ids=list(range(8)))

    out = np.empty((2 * T_BATCH, D), dtype=np.float32)
    for c in range(8):
        b = c // 4
        o = (c % 4) * T_OWN
        out[b * T_BATCH + o:b * T_BATCH + o + T_OWN] = res.results[c]["out"]
    return out.reshape(2, T_BATCH, D)


# version 28 (final)
# speedup vs baseline: 1.0125x; 1.0026x over previous
import numpy as np
import ml_dtypes
import concourse.bass as bass
import concourse.mybir as mybir
from concourse.bass_utils import run_bass_kernel_spmd

F32 = mybir.dt.float32
BF16 = mybir.dt.bfloat16
AF = mybir.ActivationFunctionType
ALU = mybir.AluOpType
AX = mybir.AxisListType

D = 2048
NK = 16
INTER = 5632
NI = 44
T_OWN = 4096
T_BATCH = 16384
NTO = T_OWN // 128
EPS = 1e-5
RG = [[0, 1, 2, 3], [4, 5, 6, 7]]

BASE = 16384 + 4096


def build_nc():
    nc = bass.Bass("TRN2", target_bir_lowering=False)

    x_h = nc.dram_tensor("x", [T_OWN, D], F32, kind="ExternalInput")
    wqk_h = nc.dram_tensor("wqk", [NK, 128, D], F32, kind="ExternalInput")
    wv_h = nc.dram_tensor("wv", [NK, 128, D], F32, kind="ExternalInput")
    wu_h = nc.dram_tensor("wu", [NK, D, 128], F32, kind="ExternalInput")
    wo_h = nc.dram_tensor("wo", [NK, 128, D], F32, kind="ExternalInput")
    gw_h = nc.dram_tensor("gatew", [NI, D, 128], F32, kind="ExternalInput")
    uw_h = nc.dram_tensor("upw", [NI, D, 128], F32, kind="ExternalInput")
    dw_h = nc.dram_tensor("downw", [NI, 128, D], F32, kind="ExternalInput")
    bqk_h = nc.dram_tensor("bqk", [1, D], F32, kind="ExternalInput")
    bv_h = nc.dram_tensor("bv", [1, D], F32, kind="ExternalInput")
    bo_h = nc.dram_tensor("bo", [1, D], F32, kind="ExternalInput")
    bu_h = nc.dram_tensor("bu", [128, NK], F32, kind="ExternalInput")
    lng_h = nc.dram_tensor("lng", [128, NK], F32, kind="ExternalInput")
    lnb_h = nc.dram_tensor("lnb", [128, NK], F32, kind="ExternalInput")
    gs_h = nc.dram_tensor("gs", [128, NI], F32, kind="ExternalInput")
    us_h = nc.dram_tensor("us", [128, NI], F32, kind="ExternalInput")
    ds_h = nc.dram_tensor("ds", [128, D], F32, kind="ExternalInput")
    id_h = nc.dram_tensor("ident", [128, 128], BF16, kind="ExternalInput")

    kv_cc_in = nc.dram_tensor("kv_cc_in", [128, D], F32, kind="Internal")
    kv_cc_out = nc.dram_tensor("kv_cc_out", [128, D], F32, kind="Internal")
    z_s = nc.dram_tensor("z_s", [NK, 128, T_OWN], BF16, kind="Internal")
    wqk_bf_s = nc.dram_tensor("wqk_bf_s", [NK, 128, D], BF16, kind="Internal")
    wv_bf_s = nc.dram_tensor("wv_bf_s", [NK, 128, D], BF16, kind="Internal")
    dw_bf_s = nc.dram_tensor("dw_bf_s", [NI, 128, D], BF16, kind="Internal")
    q_s = nc.dram_tensor("q_s", [T_OWN, D], BF16, kind="Internal")
    u_s = nc.dram_tensor("u_s", [NK, 128, T_OWN], F32, kind="Internal")
    hdn_s = nc.dram_tensor("hdn_s", [NI, 128, T_OWN], BF16, kind="Internal")
    ffn_s = nc.dram_tensor("ffn_s", [T_OWN, D], F32, kind="Internal")

    out_h = nc.dram_tensor("out", [T_OWN, D], F32, kind="ExternalOutput")

    off = [BASE]
    TOP = 229376 - 32

    def sb(name, shape, dtype, n=1, at=None):
        o = off if at is None else at
        per = int(np.prod(shape[1:])) * mybir.dt.size(dtype)
        per = (per + 31) // 32 * 32
        hs = []
        for i in range(n):
            hs.append(nc.alloc_sbuf_tensor_at(f"{name}{i if n > 1 else ''}",
                                              shape, dtype, offset=o[0]))
            o[0] += per
        assert o[0] <= TOP, f"sbuf overflow at {name}: {o[0]}"
        return hs if n > 1 else hs[0]

    ident = sb("ident_sb", [128, 128], BF16)
    ones_row_bf = sb("ones_row_bf", [1, 128], BF16)
    ones_col_bf = sb("ones_col_bf", [128, 1], BF16)
    ones_row_f = sb("ones_row_f", [1, 128], F32)
    cells = sb("cells", [128, 8], F32)
    bqk_row = sb("bqk_row", [1, D], BF16)
    bv_row = sb("bv_row", [1, D], BF16)
    bu_fm = sb("bu_fm", [128, NK], F32)
    lng_fm = sb("lng_fm", [128, NK], F32)
    lnb_fm = sb("lnb_fm", [128, NK], F32)
    gs_fm = sb("gs_fm", [128, NI], F32)
    us_fm = sb("us_fm", [128, NI], F32)
    kv_sb = sb("kv_sb", [128, D], F32)
    kv_bf = sb("kv_bf", [128, D], BF16)
    smalls = sb("smalls", [128, 96], F32)
    mean_row = sb("mean_row", [1, 512], F32)
    rstd_row = sb("rstd_row", [1, 512], F32)

    PB = off[0]

    XN_OFF = TOP - NK * T_OWN * 2
    xn_own = nc.alloc_sbuf_tensor_at("xn_own", [128, NK * T_OWN], BF16,
                                     offset=XN_OFF)
    xn_v = xn_own.rearrange("p (k t) -> p k t", k=NK)

    def region(tensors, top=None):
        o = [PB]
        out = {}
        for name, shape, dtype, n in tensors:
            out[name] = sb(name, shape, dtype, n, at=o)
        if top is not None:
            assert o[0] <= top, f"region overflow: {o[0]} > {top}"
        return out

    r0 = region([
        ("xbuf", [128, D], F32, 3),
        ("xn_tm", [128, D], BF16, 3),
        ("pc0_f32", [128, D], F32, 1),
        ("pc0_bf", [128, D], BF16, 1),
    ], top=XN_OFF)
    r1 = region([
        ("wq_bf", [128, NK * 512], BF16, 1),
        ("wv_bf", [128, NK * 512], BF16, 1),
        ("wstage", [128, 512], F32, 2),
        ("q_tm", [128, 512], BF16, 2),
        ("v_tm", [128, 512], BF16, 2),
        ("e_buf", [128, 512], F32, 2),
        ("r_buf", [128, 512], F32, 2),
    ], top=XN_OFF)
    row_stage = r0["xbuf"][0][0:1, :]
    r2 = region([
        ("wstg2", [128, D], F32, 2),
        ("gw_bf", [128, D], BF16, 2),
        ("uw_bf", [128, D], BF16, 2),
        ("sg_buf", [128, 512], F32, 2),
        ("u_raw", [128, 512], F32, 2),
        ("hdn_buf", [128, 512], BF16, 2),
        ("u_buf", [128, 512], F32, 2),
        ("pc3_bf", [128, 512], BF16, 1),
    ], top=XN_OFF)
    r4 = region([
        ("dw_bf", [128, NI * 1024], BF16, 1),
        ("hdn_t", [128, NI * 128], BF16, 2),
        ("ffn_buf", [128, 1024], F32, 2),
        ("dstg", [128, 1024], F32, 2),
        ("ds_repl", [128, D], F32, 1),
    ])
    ds_repl = r4["ds_repl"]
    r5 = region([
        ("qtile", [128, D], BF16, 2),
        ("qfm", [128, NK * 256], BF16, 2),
        ("outst", [128, NK * 256], F32, 2),
        ("sq_bf", [128, NK * 256], BF16, 2),
        ("kvscA", [128, NK * 64], F32, 1),
        ("kvscB", [128, NK * 32], F32, 1),
        ("kvs_f", [128, NK], F32, 1),
        ("kvs_bf", [128, NK], BF16, 1),
        ("zfm", [128, NK * 256], BF16, 2),
        ("u_tile", [128, NK * 256], F32, 2),
        ("mean_sb", [128, 256], F32, 2),
        ("rstd_sb", [128, 256], F32, 2),
        ("t1_b", [128, 256], F32, 2),
        ("t2_b", [128, 256], F32, 2),
        ("t3_b", [128, 256], F32, 2),
    ])
    r6 = region([
        ("wo_bf", [128, NK * D], BF16, 1),
        ("wstg6", [128, D], F32, 2),
        ("ztile", [128, NK * 128], BF16, 2),
        ("x_t", [128, D], F32, 2),
        ("ffn_t", [128, D], F32, 2),
        ("xf", [128, D], F32, 2),
        ("outb", [128, D], F32, 2),
        ("bo_row", [1, D], BF16, 1),
    ])
    bo_row = r6["bo_row"]

    pbA = nc.alloc_psum_tensor("pbA", [128, 2048], F32)
    pbt = nc.alloc_psum_tensor("pbt", [128, 2048], BF16)
    pb6 = nc.alloc_psum_tensor("pb6", [128, 512], F32)
    pb7 = nc.alloc_psum_tensor("pb7", [128, 512], F32)

    class Sem:
        def __init__(s, name):
            s.h = nc.alloc_semaphore(name)
            s.n = 0

    def sems(prefix, n):
        return [Sem(f"{prefix}{i}") for i in range(n)]

    sPE, sV, sS, sPool, sBAR = (Sem(x) for x in ("sPE", "sV", "sS", "sPool", "sBAR"))
    sC = Sem("sC")
    sX = sems("sX", 3)
    sWS = sems("sWS", 2)
    sW2 = sems("sW2", 2)
    sDS = sems("sDS", 2)
    sHT = sems("sHT", 2)
    sQT = sems("sQT", 2)
    sUT = sems("sUT", 2)
    sXT = sems("sXT", 2)
    sFT = sems("sFT", 2)
    sSQ = sems("sSQ", 2)
    sSU = sems("sSU", 2)
    sSH = sems("sSH", 2)
    sSF = sems("sSF", 2)
    sSO = sems("sSO", 2)
    sKV = Sem("sKV")
    sCC = Sem("sCC")
    sSZ = sems("sSZ", 2)
    sZT = sems("sZT", 2)
    sPC = Sem("sPC")
    sPCS = Sem("sPCS")
    sPC3 = Sem("sPC3")
    sPS3 = Sem("sPS3")
    LOAD_SEMS = [sC, *sX, *sWS, *sW2, *sDS, *sHT, *sQT, *sUT, *sXT, *sFT, *sZT,
                 sPC, sPC3]
    STORE_SEMS = [*sSQ, *sSU, *sSH, *sSF, *sSO, sKV, *sSZ, sPCS, sPS3]
    bar_n = [0]

    SP, PE, V, S, PL = nc.sync, nc.tensor, nc.vector, nc.scalar, nc.gpsimd

    def ld(sem, dst, src):
        SP.dma_start(dst, src).then_inc(sem.h, 16)
        sem.n += 16
        return sem.n

    def st(sem, dst, src):
        PL.dma_start(dst, src).then_inc(sem.h, 16)
        sem.n += 16
        return sem.n

    def barrier():
        for q in LOAD_SEMS:
            SP.wait_ge(q.h, q.n)
        for q in STORE_SEMS:
            PL.wait_ge(q.h, q.n)
        for eng in (SP, PE, V, S, PL):
            eng.drain()
            eng.nop().then_inc(sBAR.h, 1)
        bar_n[0] += 5
        for eng in (SP, PE, V, S, PL):
            eng.wait_ge(sBAR.h, bar_n[0])

    ld(sC, ident[:], id_h[:])
    ld(sC, bu_fm[:], bu_h[:])
    ld(sC, lng_fm[:], lng_h[:])
    ld(sC, lnb_fm[:], lnb_h[:])
    ld(sC, gs_fm[:], gs_h[:])
    ld(sC, us_fm[:], us_h[:])

    for ap, val in ((ones_row_bf[:], 1.0), (ones_col_bf[:], 1.0), (ones_row_f[:], 1.0),
                    (cells[:, 0:1], EPS), (cells[:, 1:2], 1.0 / D), (cells[:, 2:3], -0.5),
                    (cells[:, 3:4], -1.0),
                    (kv_sb[:], 0.0)):
        PL.memset(ap, val).then_inc(sPool.h, 1)
        sPool.n += 1

    for bh, brow in ((bqk_h, bqk_row), (bv_h, bv_row)):
        n = ld(sC, row_stage[:], bh[:])
        V.wait_ge(sC.h, n)
        V.tensor_copy(brow[:], row_stage[:]).then_inc(sV.h, 1)
        sV.n += 1
        SP.wait_ge(sV.h, sV.n)

    def load_qv_weights(sw):
        c0 = sw * 512
        marks = [0, 0]
        for wi, (wsrc, wbf) in enumerate(((wqk_bf_s, r1["wq_bf"]),
                                          (wv_bf_s, r1["wv_bf"]))):
            dst = wbf.rearrange("p (k c) -> p k c", k=NK)
            src = wsrc.rearrange("k p c -> p k c")[:, :, c0:c0 + 512]
            for p4 in range(4):
                marks[wi] = ld(sWS[wi], dst[:, p4 * 4:(p4 + 1) * 4, :],
                               src[:, p4 * 4:(p4 + 1) * 4, :])
        return marks

    w_ready = [[0, 0]]
    barrier()

    m_xld = [0] * (NTO + 4)
    m_norm = [0] * (NTO + 4)
    m_sq = [0] * (NTO + 4)
    m_tr = [0] * (NTO + 4)
    m_cpx = [0] * (NTO + 4)
    m_projg = [0] * (NTO + 4)
    m_evS = [0] * (NTO + 4)
    m_evV = [0] * (NTO + 4)
    m_kvmm = [0] * (NTO + 4)
    m_kvadd = [0] * (NTO + 4)
    m_xnst = [0] * (NTO + 4)
    m_qst = [0] * (NTO + 4)

    def reset_marks():
        for a_ in (m_xld, m_norm, m_sq, m_tr, m_cpx, m_projg, m_evS, m_evV,
                   m_kvmm, m_kvadd, m_xnst, m_qst):
            for t in range(len(a_)):
                a_[t] = 0

    m_rstd = [0] * (NTO + 4)

    def emit_ln_A(i):
        xb = r0["xbuf"][i % 3]
        lo = (i % 3) * 32
        sm = smalls
        V.wait_ge(sX[i % 3].h, m_xld[i])
        if i >= 3:
            V.wait_ge(sS.h, m_rstd[i - 3])
        for k in range(4):
            bs_ = V.bn_stats(sm[:, lo + k * 6:lo + (k + 1) * 6],
                             xb[:, k * 512:(k + 1) * 512])
        bs_.then_inc(sV.h, 1)
        sV.n += 1
        V.wait_ge(sV.h, sV.n)
        V.bn_aggr(sm[:, lo + 24:lo + 26], sm[:, lo:lo + 24]).then_inc(sV.h, 1)
        sV.n += 1
        agg_c = sV.n
        S.wait_ge(sV.h, agg_c)
        S.activation(sm[:, lo + 26:lo + 27], sm[:, lo + 25:lo + 26], AF.Ln,
                     bias=cells[:, 0:1]).then_inc(sS.h, 1)
        sS.n += 1
        S.wait_ge(sS.h, sS.n)
        S.activation(sm[:, lo + 27:lo + 28], sm[:, lo + 26:lo + 27], AF.Exp,
                     scale=cells[:, 2:3]).then_inc(sS.h, 1)
        sS.n += 1
        S.wait_ge(sS.h, sS.n)
        S.activation(sm[:, lo + 28:lo + 29], sm[:, lo + 24:lo + 25], AF.Identity,
                     scale=sm[:, lo + 27:lo + 28]).then_inc(sS.h, 1)
        sS.n += 1
        S.wait_ge(sS.h, sS.n)
        S.activation(sm[:, lo + 28:lo + 29], sm[:, lo + 28:lo + 29], AF.Identity,
                     scale=cells[:, 3:4]).then_inc(sS.h, 1)
        sS.n += 1
        m_rstd[i] = sS.n

    def emit_ln_B(i):
        xb = r0["xbuf"][i % 3]
        xt = r0["xn_tm"][i % 3]
        lo = (i % 3) * 32
        sm = smalls
        S.wait_ge(sS.h, m_rstd[i])
        if i >= 3:
            S.wait_ge(sPE.h, m_tr[i - 3])
        S.activation(xt[:], xb[:], AF.Identity, scale=sm[:, lo + 27:lo + 28],
                     bias=sm[:, lo + 28:lo + 29]).then_inc(sS.h, 1)
        sS.n += 1
        m_norm[i] = sS.n

    def emit_transposes(i):
        xt = r0["xn_tm"][i % 3]
        PE.wait_ge(sS.h, m_norm[i])
        if i >= 1:
            PE.wait_ge(sV.h, m_cpx[i - 1])
        for k in range(NK):
            ti_ = PE.transpose(pbt[:, k * 128:(k + 1) * 128],
                               xt[:, k * 128:(k + 1) * 128], ident[:])
        ti_.then_inc(sPE.h, 1)
        sPE.n += 1
        m_tr[i] = sPE.n
        V.wait_ge(sPE.h, m_tr[i])
        V.tensor_copy(xn_v[:, :, i * 128:(i + 1) * 128],
                      pbt.rearrange("p (k t) -> p k t", k=NK)).then_inc(sV.h, 1)
        sV.n += 1
        m_cpx[i] = sV.n

    def emit_tile_proj(i, sw):
        c0 = sw * 512
        qb = r1["q_tm"][i % 2]
        vb = r1["v_tm"][i % 2]
        po = (i % 2) * 512
        qsl = pbA[:, po:po + 512]
        vsl = pbA[:, 1024 + po:1024 + po + 512]
        if i == 0:
            PE.wait_ge(sWS[0].h, w_ready[0][0])
            PE.wait_ge(sWS[1].h, w_ready[0][1])
        if i >= 2:
            PE.wait_ge(sS.h, m_evS[i - 2])
            PE.wait_ge(sV.h, m_evV[i - 2])
        for k in range(NK):
            lhs = xn_v[:, k, i * 128:(i + 1) * 128]
            PE.matmul(qsl, lhs, r1["wq_bf"][:, k * 512:(k + 1) * 512],
                      start=(k == 0), stop=False)
            PE.matmul(vsl, lhs, r1["wv_bf"][:, k * 512:(k + 1) * 512],
                      start=(k == 0), stop=False)
        PE.matmul(qsl, ones_row_bf[:], bqk_row[0:1, c0:c0 + 512],
                  start=False, stop=True)
        mm_ = PE.matmul(vsl, ones_row_bf[:], bv_row[0:1, c0:c0 + 512],
                        start=False, stop=True)
        mm_.then_inc(sPE.h, 1)
        sPE.n += 1
        m_projg[i] = sPE.n
        S.wait_ge(sPE.h, m_projg[i])
        if i >= 2:
            S.wait_ge(sV.h, m_evV[i - 2])
            S.wait_ge(sPE.h, m_kvmm[i - 2])
        S.activation(r1["e_buf"][i % 2][:], qsl, AF.Exp)
        S.activation(vb[:], vsl, AF.Silu).then_inc(sS.h, 1)
        sS.n += 1
        m_evS[i] = sS.n
        V.wait_ge(sS.h, m_evS[i])
        if i >= 2:
            V.wait_ge(sPE.h, m_kvmm[i - 2])
            V.wait_ge(sSQ[i % 2].h, m_qst[i - 2])
        V.tensor_scalar(r1["r_buf"][i % 2][:], qsl, 0.0, 0.0,
                        ALU.max).then_inc(sV.h, 1)
        sV.n += 1
        V.wait_ge(sV.h, sV.n)
        V.scalar_tensor_tensor(qb[:], r1["e_buf"][i % 2][:], -1.0,
                               r1["r_buf"][i % 2][:], ALU.add,
                               ALU.min).then_inc(sV.h, 1)
        sV.n += 1
        m_evV[i] = sV.n

    def emit_kv(i, sw):
        qb = r1["q_tm"][i % 2]
        vb = r1["v_tm"][i % 2]
        kvp = pb6 if i % 2 == 0 else pb7
        PE.wait_ge(sV.h, m_evV[i])
        PE.wait_ge(sS.h, m_evS[i])
        if i >= 2:
            PE.wait_ge(sV.h, m_kvadd[i - 2])
        for j in range(4):
            km_ = PE.matmul(kvp[:, j * 128:(j + 1) * 128],
                            qb[:, j * 128:(j + 1) * 128],
                            vb[:, j * 128:(j + 1) * 128], start=True, stop=True)
        km_.then_inc(sPE.h, 1)
        sPE.n += 1
        m_kvmm[i] = sPE.n
        col0 = sw * 512
        V.wait_ge(sPE.h, m_kvmm[i])
        if i >= 1:
            V.wait_ge(sV.h, m_kvadd[i - 1])
        V.tensor_tensor(kv_sb[:, col0:col0 + 512], kv_sb[:, col0:col0 + 512],
                        kvp[:], ALU.add).then_inc(sV.h, 1)
        sV.n += 1
        m_kvadd[i] = sV.n

    pc0_cast = [0]
    pc0_st = [0]

    def pre_cast_qv(u):
        wsrc = wqk_h if u < NK else wv_h
        wdst = wqk_bf_s if u < NK else wv_bf_s
        k = u % NK
        SP.wait_ge(sV.h, pc0_cast[0])
        n = ld(sPC, r0["pc0_f32"][:], wsrc[k])
        V.wait_ge(sPC.h, n)
        V.wait_ge(sPCS.h, pc0_st[0])
        V.tensor_copy(r0["pc0_bf"][:], r0["pc0_f32"][:]).then_inc(sV.h, 1)
        sV.n += 1
        pc0_cast[0] = sV.n
        PL.wait_ge(sV.h, sV.n)
        pc0_st[0] = st(sPCS, wdst[k], r0["pc0_bf"][:])

    for i0 in range(3):
        m_xld[i0] = ld(sX[i0], r0["xbuf"][i0][:], x_h[i0 * 128:(i0 + 1) * 128, :])
    for i in range(NTO):
        emit_ln_A(i)
        if i >= 1:
            emit_ln_B(i - 1)
            emit_transposes(i - 1)
            pre_cast_qv(i - 1)
            j = i + 2
            if j < NTO:
                SP.wait_ge(sS.h, m_norm[i - 1])
                m_xld[j] = ld(sX[j % 3], r0["xbuf"][j % 3][:],
                              x_h[j * 128:(j + 1) * 128, :])
    emit_ln_B(NTO - 1)
    emit_transposes(NTO - 1)
    pre_cast_qv(NTO - 1)
    barrier()

    for sw in range(4):
        w_ready[0] = load_qv_weights(sw)
        reset_marks()
        for i in range(NTO):
            emit_tile_proj(i, sw)
            if i >= 1:
                emit_kv(i - 1, sw)
            if i > 0:
                PL.wait_ge(sV.h, m_evV[i - 1])
                m_qst[i - 1] = st(sSQ[(i - 1) % 2],
                                  q_s[(i - 1) * 128:i * 128, sw * 512:(sw + 1) * 512],
                                  r1["q_tm"][(i - 1) % 2][:])
        emit_kv(NTO - 1, sw)
        PL.wait_ge(sV.h, m_evV[NTO - 1])
        m_qst[NTO - 1] = st(sSQ[(NTO - 1) % 2],
                            q_s[(NTO - 1) * 128:NTO * 128, sw * 512:(sw + 1) * 512],
                            r1["q_tm"][(NTO - 1) % 2][:])
        barrier()

    st(sKV, kv_cc_in[:], kv_sb[:])
    PL.wait_ge(sKV.h, sKV.n)
    PL.collective_compute(
        "AllReduce", ALU.add, replica_groups=RG,
        ins=[kv_cc_in.ap().opt()], outs=[kv_cc_out.ap().opt()],
    ).then_inc(sCC.h)
    sCC.n = 1

    TG2 = T_OWN // 512
    wu_pe_done = [0, 0]
    wu_cast = [0, 0]
    u_evac = {}
    u_store = {}
    w2_free = [0]
    stg2_free = [0, 0]
    for m in range(NK):
        j = m % 2
        SP.wait_ge(sV.h, stg2_free[j])
        if m >= 2:
            SP.wait_ge(sPE.h, wu_pe_done[j])
        n = ld(sW2[j], r2["wstg2"][j].rearrange("p (c m) -> p c m", c=NK),
               wu_h[m].rearrange("(c p) m -> p c m", p=128))
        V.wait_ge(sW2[j].h, n)
        V.tensor_copy(r2["uw_bf"][j][:], r2["wstg2"][j][:]).then_inc(sV.h, 1)
        sV.n += 1
        wu_cast[j] = sV.n
        stg2_free[j] = sV.n
        PE.wait_ge(sV.h, wu_cast[j])
        for tg in range(TG2):
            u_idx = m * TG2 + tg
            psl = pbA[:, (u_idx % 2) * 512:(u_idx % 2) * 512 + 512]
            prev = u_idx - 2
            if prev >= 0:
                PE.wait_ge(sS.h, u_evac.get(prev, 0))
            for k in range(NK):
                um_ = PE.matmul(psl, r2["uw_bf"][j][:, k * 128:(k + 1) * 128],
                                xn_own[:, k * T_OWN + tg * 512:k * T_OWN + tg * 512 + 512],
                                start=(k == 0), stop=(k == NK - 1))
            um_.then_inc(sPE.h, 1)
            sPE.n += 1
            if tg == TG2 - 1:
                wu_pe_done[j] = sPE.n
            S.wait_ge(sPE.h, sPE.n)
            if prev >= 0:
                S.wait_ge(sSU[u_idx % 2].h, u_store.get(prev, 0))
            S.activation(r2["u_buf"][u_idx % 2][:], psl, AF.Silu,
                         bias=bu_fm[:, m:m + 1]).then_inc(sS.h, 1)
            sS.n += 1
            u_evac[u_idx] = sS.n
            PL.wait_ge(sS.h, sS.n)
            u_store[u_idx] = st(
                sSU[u_idx % 2],
                u_s.rearrange("m p t -> p m t")[:, m, tg * 512:(tg + 1) * 512],
                r2["u_buf"][u_idx % 2][:])

    barrier()

    g_evac = {}
    h_evac = {}
    h_store = {}
    w_pe_done = [0, 0]
    stg_free = [0, 0]
    cast_done = [0] * (NI + 1)
    gw_n = [0] * (NI + 1)
    uw_n = [0] * (NI + 1)

    def p3_ld_gw(i):
        j = i % 2
        SP.wait_ge(sV.h, stg_free[j])
        gw_n[i] = ld(sW2[j], r2["wstg2"][j].rearrange("p (c m) -> p c m", c=NK),
                     gw_h[i].rearrange("(c p) m -> p c m", p=128))

    def p3_cast_gw(i):
        j = i % 2
        V.wait_ge(sW2[j].h, gw_n[i])
        if i >= 2:
            V.wait_ge(sPE.h, w_pe_done[j])
        V.tensor_copy(r2["gw_bf"][j][:], r2["wstg2"][j][:]).then_inc(sV.h, 1)
        sV.n += 1

    def p3_ld_uw(i):
        j = i % 2
        SP.wait_ge(sV.h, sV.n)
        uw_n[i] = ld(sW2[j], r2["wstg2"][j].rearrange("p (c m) -> p c m", c=NK),
                     uw_h[i].rearrange("(c p) m -> p c m", p=128))

    def p3_cast_uw(i):
        j = i % 2
        V.wait_ge(sW2[j].h, uw_n[i])
        V.tensor_copy(r2["uw_bf"][j][:], r2["wstg2"][j][:]).then_inc(sV.h, 1)
        sV.n += 1
        stg_free[j] = sV.n
        cast_done[i] = sV.n

    pc3_cast = [0]
    pc3_st = [0]

    def pre_cast_dw(u):
        i_dw, q = u // 4, u % 4
        SP.wait_ge(sV.h, pc3_cast[0])
        n = ld(sPC3, r2["u_buf"][0][:], dw_h[i_dw, :, q * 512:(q + 1) * 512])
        V.wait_ge(sPC3.h, n)
        V.wait_ge(sPS3.h, pc3_st[0])
        V.tensor_copy(r2["pc3_bf"][:], r2["u_buf"][0][:]).then_inc(sV.h, 1)
        sV.n += 1
        pc3_cast[0] = sV.n
        PL.wait_ge(sV.h, sV.n)
        pc3_st[0] = st(sPS3, dw_bf_s[i_dw, :, q * 512:(q + 1) * 512],
                       r2["pc3_bf"][:])

    p3_ld_gw(0)
    p3_cast_gw(0)
    p3_ld_uw(0)
    p3_cast_uw(0)
    for i in range(NI):
        j = i % 2
        PE.wait_ge(sV.h, cast_done[i])
        for tg in range(TG2):
            if i + 1 < NI:
                if tg == 0:
                    p3_ld_gw(i + 1)
                elif tg == 1:
                    p3_cast_gw(i + 1)
                elif tg == 2:
                    p3_ld_uw(i + 1)
                elif tg == 3:
                    p3_cast_uw(i + 1)
            if tg >= 4:
                pre_cast_dw(i * 4 + (tg - 4))
            f_idx = i * TG2 + tg
            po = (f_idx % 2) * 512
            gsl = pbA[:, po:po + 512]
            usl = pbA[:, 1024 + po:1024 + po + 512]
            prev = f_idx - 2
            if prev >= 0:
                PE.wait_ge(sS.h, g_evac.get(prev, 0))
            for k in range(NK):
                rhs = xn_own[:, k * T_OWN + tg * 512:k * T_OWN + tg * 512 + 512]
                PE.matmul(gsl, r2["gw_bf"][j][:, k * 128:(k + 1) * 128], rhs,
                          start=(k == 0), stop=(k == NK - 1))
                fm_ = PE.matmul(usl, r2["uw_bf"][j][:, k * 128:(k + 1) * 128], rhs,
                                start=(k == 0), stop=(k == NK - 1))
            fm_.then_inc(sPE.h, 1)
            sPE.n += 1
            if tg == TG2 - 1:
                w_pe_done[j] = sPE.n
            S.wait_ge(sPE.h, sPE.n)
            if prev >= 0:
                S.wait_ge(sV.h, h_evac.get(prev, 0))
            S.activation(r2["sg_buf"][f_idx % 2][:], gsl, AF.Silu,
                         scale=gs_fm[:, i:i + 1])
            S.activation(r2["u_raw"][f_idx % 2][:], usl,
                         AF.Identity).then_inc(sS.h, 1)
            sS.n += 1
            g_evac[f_idx] = sS.n
            V.wait_ge(sS.h, sS.n)
            if prev >= 0:
                V.wait_ge(sSH[f_idx % 2].h, h_store.get(prev, 0))
            V.scalar_tensor_tensor(r2["hdn_buf"][f_idx % 2][:],
                                   r2["u_raw"][f_idx % 2][:], us_fm[:, i:i + 1],
                                   r2["sg_buf"][f_idx % 2][:], ALU.mult,
                                   ALU.mult).then_inc(sV.h, 1)
            sV.n += 1
            h_evac[f_idx] = sV.n
            PL.wait_ge(sV.h, sV.n)
            h_store[f_idx] = st(
                sSH[f_idx % 2],
                hdn_s.rearrange("i p t -> p i t")[:, i, tg * 512:(tg + 1) * 512],
                r2["hdn_buf"][f_idx % 2][:])

    barrier()

    n_ds = ld(sC, ds_repl[:], ds_h[:])
    V.wait_ge(sC.h, n_ds)
    dwv = dw_bf_s.rearrange("i p c -> p i c")
    for half in range(2):
        dst_dw = r4["dw_bf"].rearrange("p (i c) -> p i c", i=NI)
        n_dw = 0
        for g in range(11):
            i0, i1 = g * 4, min((g + 1) * 4, NI)
            n_dw = ld(sDS[g % 2], dst_dw[:, i0:i1, :],
                      dwv[:, i0:i1, half * 1024:(half + 1) * 1024])
        PE.wait_ge(sDS[0].h, sDS[0].n)
        PE.wait_ge(sDS[1].h, sDS[1].n)

        p4_mm = {}
        f_evac = {}
        f_store = {}
        for t in range(NTO):
            j = t % 2
            if t >= 2:
                SP.wait_ge(sPE.h, p4_mm.get(t - 2, 0))
            nlast = 0
            hsrc = hdn_s.rearrange("i p t -> p i t")[:, :, t * 128:(t + 1) * 128]
            hdst = r4["hdn_t"][j].rearrange("p (i t) -> p i t", i=NI)
            for piece in range(11):
                i0, i1 = piece * 4, min((piece + 1) * 4, NI)
                nlast = ld(sHT[j], hdst[:, i0:i1, :], hsrc[:, i0:i1, :])
            PE.wait_ge(sHT[j].h, nlast)
            b0 = (t % 2) * 1024
            if t >= 2:
                PE.wait_ge(sV.h, f_evac.get(t - 2, 0))
            for i in range(NI):
                lhs = r4["hdn_t"][j][:, i * 128:(i + 1) * 128]
                PE.matmul(pbA[:, b0:b0 + 512], lhs,
                          r4["dw_bf"][:, i * 1024:i * 1024 + 512],
                          start=(i == 0), stop=(i == NI - 1))
                dm_ = PE.matmul(pbA[:, b0 + 512:b0 + 1024], lhs,
                                r4["dw_bf"][:, i * 1024 + 512:i * 1024 + 1024],
                                start=(i == 0), stop=(i == NI - 1))
            dm_.then_inc(sPE.h, 1)
            sPE.n += 1
            p4_mm[t] = sPE.n
            V.wait_ge(sPE.h, sPE.n)
            V.wait_ge(sSF[j].h, f_store.get(t - 2, 0))
            V.tensor_tensor(r4["ffn_buf"][j][:], pbA[:, b0:b0 + 1024],
                            ds_repl[:, half * 1024:(half + 1) * 1024],
                            ALU.mult).then_inc(sV.h, 1)
            sV.n += 1
            f_evac[t] = sV.n
            PL.wait_ge(sV.h, sV.n)
            f_store[t] = st(sSF[j],
                            ffn_s[t * 128:(t + 1) * 128, half * 1024:(half + 1) * 1024],
                            r4["ffn_buf"][j][:])
        barrier()

    SP.wait_ge(sCC.h, 1)
    n_kv = ld(sC, kv_sb[:], kv_cc_out[:])
    S.wait_ge(sC.h, n_kv)
    S.activation(kv_bf[:], kv_sb[:], AF.Identity).then_inc(sS.h, 1)
    sS.n += 1

    V.wait_ge(sC.h, n_kv)
    kv3 = kv_sb.rearrange("p (m e) -> p m e", m=NK)
    scA = r5["kvscA"].rearrange("p (m e) -> p m e", m=NK)
    scB = r5["kvscB"].rearrange("p (m e) -> p m e", m=NK)
    tree = [(scA, kv3, kv3, 64), (scB, scA, scA, 32), (scA, scB, scB, 16),
            (scB, scA, scA, 8), (scA, scB, scB, 4), (scB, scA, scA, 2),
            (r5["kvs_f"].rearrange("p (m o) -> p m o", o=1), scB, scB, 1)]
    for dst, s0, s1, w in tree:
        V.tensor_tensor(dst[:, :, 0:w], s0[:, :, 0:w], s1[:, :, w:2 * w],
                        ALU.add).then_inc(sV.h, 1)
        sV.n += 1
        V.wait_ge(sV.h, sV.n)
    S.wait_ge(sV.h, sV.n)
    S.activation(r5["kvs_bf"][:], r5["kvs_f"][:], AF.Identity).then_inc(sS.h, 1)
    sS.n += 1
    kvs_ready = sS.n

    NTG = T_OWN // 256
    uv_h = u_s.rearrange("m p t -> p m t")
    zv_h = z_s.rearrange("k p t -> p k t")
    trp = [0, 0]
    pbt_ev = [0]
    outmm_last = [0] * (NTG + 2)
    stats_mm = [0] * (NTG + 2)
    mean_mm = [0] * (NTG + 2)
    mbc_mm = [0] * (NTG + 2)
    mean_ev = [0] * (NTG + 2)
    mr_mark = [0] * (NTG + 2)
    rr_mark = [0] * (NTG + 2)
    bc7_mm = [0] * (NTG + 2)
    bc_ev = [0] * (NTG + 2)
    t1_last = [0] * (NTG + 2)
    t2_last = [0] * (NTG + 2)
    z_g = [0] * (NTG + 2)
    z_st = [0, 0]
    u_ld = [0, 0]
    q_ld = [0, 0]
    oevS = [0] * 4
    oevV = [0] * 4
    tt1, tt2, tt3, ttz = {}, {}, {}, {}

    evac_arr = [0] * (NTG + 2)

    def p5a_A1(tg):
        p = tg % 2
        t0 = tg * 256
        p = tg % 2
        t0 = tg * 256
        pc = p * 256
        for ti in range(2):
            SP.wait_ge(sPE.h, trp[ti])
            q_ld[ti] = ld(sQT[ti], r5["qtile"][ti][:],
                          q_s[(tg * 2 + ti) * 128:(tg * 2 + ti + 1) * 128, :])
        if tg >= 2:
            SP.wait_ge(sV.h, z_g[tg - 2])
        dst_u = r5["u_tile"][p].rearrange("p (m t) -> p m t", m=NK)
        for p4 in range(4):
            n_u = ld(sUT[p], dst_u[:, p4 * 4:(p4 + 1) * 4, :],
                     uv_h[:, p4 * 4:(p4 + 1) * 4, t0:t0 + 256])
        u_ld[p] = n_u
        for ti in range(2):
            PE.wait_ge(sQT[ti].h, q_ld[ti])
            PE.wait_ge(sS.h, pbt_ev[0])
            for k in range(NK):
                tq_ = PE.transpose(pbt[:, k * 128:(k + 1) * 128],
                                   r5["qtile"][ti][:, k * 128:(k + 1) * 128],
                                   ident[:])
            tq_.then_inc(sPE.h, 1)
            sPE.n += 1
            trp[ti] = sPE.n
            S.wait_ge(sPE.h, trp[ti])
            if ti == 0 and tg >= 2:
                S.wait_ge(sPE.h, outmm_last[tg - 2])
            S.activation(
                r5["qfm"][p].rearrange("p (m t) -> p m t", m=NK)[:, :, ti * 128:(ti + 1) * 128],
                pbt.rearrange("p (m t) -> p m t", m=NK)[:, :, 0:128],
                AF.Identity).then_inc(sS.h, 1)
            sS.n += 1
            pbt_ev[0] = sS.n
        qfm_ready = sS.n
        PE.wait_ge(sS.h, qfm_ready)
        if tg == 0:
            PE.wait_ge(sS.h, kvs_ready)
        if tg >= 1:
            PE.wait_ge(sS.h, mean_ev[tg - 1])
            PE.wait_ge(sV.h, mr_mark[tg - 1])
        for m in range(NK):
            mn_ = PE.matmul(pb6[0:1, pc:pc + 256], r5["kvs_bf"][:, m:m + 1],
                            r5["qfm"][p][:, m * 256:(m + 1) * 256],
                            start=(m == 0), stop=(m == NK - 1))
        mn_.then_inc(sPE.h, 1)
        sPE.n += 1
        mean_mm[tg] = sPE.n
        mr = mean_row[0:1, pc:pc + 256]
        V.wait_ge(sPE.h, mean_mm[tg])
        if tg >= 2:
            V.wait_ge(sPE.h, mbc_mm[tg - 2])
        V.tensor_scalar_mul(mr, pb6[0:1, pc:pc + 256], 1.0 / D).then_inc(sV.h, 1)
        sV.n += 1
        mr_mark[tg] = sV.n

    def p5a_A2(tg):
        p = tg % 2
        t0 = tg * 256
        pc = p * 256
        mr = mean_row[0:1, pc:pc + 256]
        for mp in range(NK // 2):
            slot = p * 2 + mp % 2
            PE.wait_ge(sS.h, oevS[slot])
            for h in range(2):
                m = mp * 2 + h
                psl = pbA[:, slot * 512 + h * 256:slot * 512 + h * 256 + 256]
                PE.matmul(psl, kv_bf[:, m * 128:(m + 1) * 128],
                          r5["qfm"][p][:, m * 256:(m + 1) * 256],
                          start=True, stop=True).then_inc(sPE.h, 1)
                sPE.n += 1
            S.wait_ge(sPE.h, sPE.n)
            if mp == 0 and tg >= 2:
                S.wait_ge(sPE.h, stats_mm[tg - 2])
                S.wait_ge(sV.h, z_g[tg - 2])
            pp = pbA[:, slot * 512:slot * 512 + 512]
            S.activation(r5["outst"][p][:, mp * 512:(mp + 1) * 512], pp,
                         AF.Identity)
            S.activation(r5["sq_bf"][p][:, mp * 512:(mp + 1) * 512], pp,
                         AF.Square).then_inc(sS.h, 1)
            sS.n += 1
            oevS[slot] = sS.n
        outmm_last[tg] = sPE.n
        evac_arr[tg] = sS.n
        PE.wait_ge(sV.h, mr_mark[tg])
        PE.matmul(pb6[:, pc:pc + 256], ones_row_f[:], mr,
                  start=True, stop=True).then_inc(sPE.h, 1)
        sPE.n += 1
        mbc_mm[tg] = sPE.n
        S.wait_ge(sPE.h, mbc_mm[tg])
        if tg >= 2:
            S.wait_ge(sV.h, z_g[tg - 2])
        S.activation(r5["mean_sb"][p][:], pb6[:, pc:pc + 256],
                     AF.Identity).then_inc(sS.h, 1)
        sS.n += 1
        mean_ev[tg] = sS.n


    def p5a_B1(tg):
        p = tg % 2
        t0 = tg * 256
        pc = p * 256
        PE.wait_ge(sS.h, evac_arr[tg])
        if tg >= 1:
            PE.wait_ge(sS.h, bc_ev[tg - 1])
            PE.wait_ge(sV.h, rr_mark[tg - 1])
        for m in range(NK):
            sm_ = PE.matmul(pb7[0:1, pc:pc + 256], ones_col_bf[:],
                            r5["sq_bf"][p][:, m * 256:(m + 1) * 256],
                            start=(m == 0), stop=(m == NK - 1))
        sm_.then_inc(sPE.h, 1)
        sPE.n += 1
        stats_mm[tg] = sPE.n
        mr = mean_row[0:1, pc:pc + 256]
        rr = rstd_row[0:1, pc:pc + 256]
        V.wait_ge(sPE.h, stats_mm[tg])
        V.wait_ge(sV.h, mr_mark[tg])
        if tg >= 2:
            V.wait_ge(sPE.h, bc7_mm[tg - 2])
        V.tensor_tensor(rr, mr, mr, ALU.mult).then_inc(sV.h, 1)
        sV.n += 1
        V.wait_ge(sV.h, sV.n)
        V.scalar_tensor_tensor(rr, pb7[0:1, pc:pc + 256], 1.0 / D, rr,
                               ALU.mult, ALU.subtract).then_inc(sV.h, 1)
        sV.n += 1
        rr_mark[tg] = sV.n
        S.wait_ge(sV.h, sV.n)
        S.activation(rr, rr, AF.Ln, bias=cells[0:1, 0:1]).then_inc(sS.h, 1)
        sS.n += 1
        S.wait_ge(sS.h, sS.n)
        S.activation(rr, rr, AF.Exp, scale=cells[0:1, 2:3]).then_inc(sS.h, 1)
        sS.n += 1
        rows_c = sS.n
        PE.wait_ge(sS.h, rows_c)
        PE.matmul(pb7[:, pc:pc + 256], ones_row_f[:], rr,
                  start=True, stop=True).then_inc(sPE.h, 1)
        sPE.n += 1
        bc7_mm[tg] = sPE.n
        S.wait_ge(sPE.h, bc7_mm[tg])
        if tg >= 2:
            S.wait_ge(sV.h, z_g[tg - 2])
        S.activation(r5["rstd_sb"][p][:], pb7[:, pc:pc + 256],
                     AF.Identity).then_inc(sS.h, 1)
        sS.n += 1
        bc_ev[tg] = sS.n

    def p5a_B2(tg):
        p = tg % 2
        t0 = tg * 256
        pc = p * 256
        for mp in range(NK // 2):
            ms = (2 * mp, 2 * mp + 1)
            marks = {}
            if mp == 0:
                V.wait_ge(sS.h, bc_ev[tg])
                V.wait_ge(sUT[p].h, u_ld[p])
                V.wait_ge(sSZ[p].h, z_st[p])
            for m in ms:
                V.tensor_tensor(r5["t1_b"][m % 2][:],
                                r5["outst"][p][:, m * 256:(m + 1) * 256],
                                r5["mean_sb"][p][:],
                                ALU.subtract).then_inc(sV.h, 1)
                sV.n += 1
                marks[("t1", m)] = sV.n
            for m in ms:
                V.wait_ge(sV.h, marks[("t1", m)])
                V.scalar_tensor_tensor(r5["t2_b"][m % 2][:], r5["t1_b"][m % 2][:],
                                       lng_fm[:, m:m + 1], r5["rstd_sb"][p][:],
                                       ALU.mult, ALU.mult).then_inc(sV.h, 1)
                sV.n += 1
                marks[("t2", m)] = sV.n
            for m in ms:
                V.wait_ge(sV.h, marks[("t2", m)])
                V.scalar_tensor_tensor(r5["zfm"][p][:, m * 256:(m + 1) * 256],
                                       r5["t2_b"][m % 2][:],
                                       lnb_fm[:, m:m + 1],
                                       r5["u_tile"][p][:, m * 256:(m + 1) * 256],
                                       ALU.add, ALU.mult).then_inc(sV.h, 1)
                sV.n += 1
        t1_last[tg] = sV.n
        t2_last[tg] = sV.n
        z_g[tg] = sV.n
        PL.wait_ge(sV.h, z_g[tg])
        src_z = r5["zfm"][p].rearrange("p (k t) -> p k t", k=NK)
        for p4 in range(4):
            n_z = st(sSZ[p], zv_h[:, p4 * 4:(p4 + 1) * 4, t0:t0 + 256],
                     src_z[:, p4 * 4:(p4 + 1) * 4, :])
        z_st[p] = n_z


    p5a_A1(0)
    p5a_A2(0)
    for tg in range(1, NTG):
        p5a_A1(tg)
        p5a_B1(tg - 1)
        p5a_A2(tg)
        p5a_B2(tg - 1)
    p5a_B1(NTG - 1)
    p5a_B2(NTG - 1)
    barrier()

    n_bo = ld(sC, r6["wstg6"][0][0:1, :], bo_h[:])
    V.wait_ge(sC.h, n_bo)
    V.tensor_copy(bo_row[:], r6["wstg6"][0][0:1, :]).then_inc(sV.h, 1)
    sV.n += 1
    wo_cast = [sV.n, 0]
    wo_k = [0] * NK
    for k in range(NK):
        j = k % 2
        SP.wait_ge(sV.h, wo_cast[j])
        n = ld(sWS[j], r6["wstg6"][j][:], wo_h[k])
        V.wait_ge(sWS[j].h, n)
        V.tensor_copy(r6["wo_bf"][:, k * D:(k + 1) * D],
                      r6["wstg6"][j][:]).then_inc(sV.h, 1)
        sV.n += 1
        wo_cast[j] = sV.n
        wo_k[k] = sV.n

    ot_store = [0, 0]
    xf_done = [0, 0]
    xf_rd = [0, 0]
    z_rd = [0, 0]
    half_ev_V = [0, 0]
    for t in range(NTO):
        j = t % 2
        if t >= 2:
            SP.wait_ge(sPool.h, xf_done[j])
        n_x = ld(sXT[j], r6["x_t"][j][:], x_h[t * 128:(t + 1) * 128, :])
        n_f = ld(sFT[j], r6["ffn_t"][j][:], ffn_s[t * 128:(t + 1) * 128, :])
        if t >= 2:
            SP.wait_ge(sPE.h, z_rd[j])
        zt = r6["ztile"][j].rearrange("p (k t) -> p k t", k=NK)
        for p4 in range(4):
            n_z = ld(sZT[j], zt[:, p4 * 4:(p4 + 1) * 4, :],
                     zv_h[:, p4 * 4:(p4 + 1) * 4, t * 128:(t + 1) * 128])
        PL.wait_ge(sXT[j].h, n_x)
        PL.wait_ge(sFT[j].h, n_f)
        if t >= 2:
            PL.wait_ge(sV.h, xf_rd[j])
        PL.tensor_tensor(r6["xf"][j][:], r6["x_t"][j][:], r6["ffn_t"][j][:],
                         ALU.add).then_inc(sPool.h, 1)
        sPool.n += 1
        xf_done[j] = sPool.n
        PE.wait_ge(sZT[j].h, n_z)
        for gp in range(2):
            u = (t * 2 + gp) % 2
            b0 = u * 1024
            if t * 2 + gp >= 2:
                PE.wait_ge(sV.h, half_ev_V[u])
            for k in range(NK):
                if t == 0 and gp == 0:
                    PE.wait_ge(sV.h, wo_k[k])
                lhs = zt[:, k, :]
                PE.matmul(pbA[:, b0:b0 + 512], lhs,
                          r6["wo_bf"][:, k * D + gp * 1024:k * D + gp * 1024 + 512],
                          start=(k == 0), stop=False)
                PE.matmul(pbA[:, b0 + 512:b0 + 1024], lhs,
                          r6["wo_bf"][:, k * D + gp * 1024 + 512:k * D + gp * 1024 + 1024],
                          start=(k == 0), stop=False)
            PE.matmul(pbA[:, b0:b0 + 512], ones_row_bf[:],
                      bo_row[0:1, gp * 1024:gp * 1024 + 512],
                      start=False, stop=True)
            wm_ = PE.matmul(pbA[:, b0 + 512:b0 + 1024], ones_row_bf[:],
                            bo_row[0:1, gp * 1024 + 512:gp * 1024 + 1024],
                            start=False, stop=True)
            wm_.then_inc(sPE.h, 1)
            sPE.n += 1
            if gp == 1:
                z_rd[j] = sPE.n
            V.wait_ge(sPE.h, sPE.n)
            if gp == 0:
                V.wait_ge(sPool.h, xf_done[j])
                V.wait_ge(sSO[j].h, ot_store[j])
            V.tensor_tensor(r6["outb"][j][:, gp * 1024:(gp + 1) * 1024],
                            pbA[:, b0:b0 + 1024],
                            r6["xf"][j][:, gp * 1024:(gp + 1) * 1024],
                            ALU.add).then_inc(sV.h, 1)
            sV.n += 1
            half_ev_V[u] = sV.n
            if gp == 1:
                xf_rd[j] = sV.n
        PL.wait_ge(sV.h, xf_rd[j])
        ot_store[j] = st(sSO[j], out_h[t * 128:(t + 1) * 128, :],
                         r6["outb"][j][:])

    for q in LOAD_SEMS:
        SP.wait_ge(q.h, q.n)
    for q in STORE_SEMS:
        PL.wait_ge(q.h, q.n)
    SP.wait_ge(sSO[0].h, sSO[0].n)
    SP.wait_ge(sSO[1].h, sSO[1].n)
    return nc


def kernel(L0_attn_x, L0_attn_Wqk, L0_attn_bqk, L0_attn_Wv, L0_attn_bv,
           L0_attn_Wu, L0_attn_bu, L0_attn_Wo, L0_attn_bo, L0_attn_ln_g,
           L0_attn_ln_b, L0_ffn_gate_w, L0_ffn_up_w, L0_ffn_down_w,
           L0_ffn_gate_s, L0_ffn_up_s, L0_ffn_down_s):
    f32 = np.float32
    x = np.asarray(L0_attn_x, dtype=f32).reshape(2 * T_BATCH, D)

    def fm_vec(v, nk):
        return np.ascontiguousarray(np.asarray(v, f32).reshape(nk, 128).T)

    common = {
        "wqk": np.ascontiguousarray(np.asarray(L0_attn_Wqk, f32).reshape(NK, 128, D)),
        "wv": np.ascontiguousarray(np.asarray(L0_attn_Wv, f32).reshape(NK, 128, D)),
        "wu": np.ascontiguousarray(
            np.asarray(L0_attn_Wu, f32).reshape(D, NK, 128).transpose(1, 0, 2)),
        "wo": np.ascontiguousarray(np.asarray(L0_attn_Wo, f32).reshape(NK, 128, D)),
        "gatew": np.ascontiguousarray(
            np.asarray(L0_ffn_gate_w, f32).T.reshape(D, NI, 128).transpose(1, 0, 2)),
        "upw": np.ascontiguousarray(
            np.asarray(L0_ffn_up_w, f32).T.reshape(D, NI, 128).transpose(1, 0, 2)),
        "downw": np.ascontiguousarray(
            np.asarray(L0_ffn_down_w, f32).T.reshape(NI, 128, D)),
        "bqk": np.asarray(L0_attn_bqk, f32).reshape(1, D),
        "bv": np.asarray(L0_attn_bv, f32).reshape(1, D),
        "bo": np.asarray(L0_attn_bo, f32).reshape(1, D),
        "bu": fm_vec(L0_attn_bu, NK),
        "lng": fm_vec(L0_attn_ln_g, NK),
        "lnb": fm_vec(L0_attn_ln_b, NK),
        "gs": fm_vec(L0_ffn_gate_s, NI),
        "us": fm_vec(L0_ffn_up_s, NI),
        "ds": np.ascontiguousarray(
            np.tile(np.asarray(L0_ffn_down_s, f32).reshape(1, D), (128, 1))),
        "ident": np.eye(128, dtype=ml_dtypes.bfloat16),
    }

    in_maps = []
    for c in range(8):
        b = c // 4
        o = (c % 4) * T_OWN
        x_core = np.ascontiguousarray(x[b * T_BATCH + o:b * T_BATCH + o + T_OWN])
        m = dict(common)
        m["x"] = x_core
        in_maps.append(m)

    nc = build_nc()
    res = run_bass_kernel_spmd(nc, in_maps, core_ids=list(range(8)))

    out = np.empty((2 * T_BATCH, D), dtype=np.float32)
    for c in range(8):
        b = c // 4
        o = (c % 4) * T_OWN
        out[b * T_BATCH + o:b * T_BATCH + o + T_OWN] = res.results[c]["out"]
    return out.reshape(2, T_BATCH, D)


# version 29
# speedup vs baseline: 1.0382x; 1.0254x over previous
import numpy as np
import ml_dtypes
import concourse.bass as bass
import concourse.mybir as mybir
from concourse.bass_utils import run_bass_kernel_spmd

F32 = mybir.dt.float32
BF16 = mybir.dt.bfloat16
AF = mybir.ActivationFunctionType
ALU = mybir.AluOpType
AX = mybir.AxisListType

D = 2048
NK = 16
INTER = 5632
NI = 44
T_OWN = 4096
T_BATCH = 16384
NTO = T_OWN // 128
EPS = 1e-5
RG = [[0, 1, 2, 3], [4, 5, 6, 7]]

BASE = 16384 + 4096


def build_nc():
    nc = bass.Bass("TRN2", target_bir_lowering=False)

    x_h = nc.dram_tensor("x", [T_OWN, D], F32, kind="ExternalInput")
    wqk_h = nc.dram_tensor("wqk", [NK, 128, D], F32, kind="ExternalInput")
    wv_h = nc.dram_tensor("wv", [NK, 128, D], F32, kind="ExternalInput")
    wu_h = nc.dram_tensor("wu", [NK, D, 128], F32, kind="ExternalInput")
    wo_h = nc.dram_tensor("wo", [NK, 128, D], F32, kind="ExternalInput")
    gw_h = nc.dram_tensor("gatew", [NI, D, 128], F32, kind="ExternalInput")
    uw_h = nc.dram_tensor("upw", [NI, D, 128], F32, kind="ExternalInput")
    dw_h = nc.dram_tensor("downw", [NI, 128, D], F32, kind="ExternalInput")
    bqk_h = nc.dram_tensor("bqk", [1, D], F32, kind="ExternalInput")
    bv_h = nc.dram_tensor("bv", [1, D], F32, kind="ExternalInput")
    bo_h = nc.dram_tensor("bo", [1, D], F32, kind="ExternalInput")
    bu_h = nc.dram_tensor("bu", [128, NK], F32, kind="ExternalInput")
    lng_h = nc.dram_tensor("lng", [128, NK], F32, kind="ExternalInput")
    lnb_h = nc.dram_tensor("lnb", [128, NK], F32, kind="ExternalInput")
    gs_h = nc.dram_tensor("gs", [128, NI], F32, kind="ExternalInput")
    us_h = nc.dram_tensor("us", [128, NI], F32, kind="ExternalInput")
    ds_h = nc.dram_tensor("ds", [128, D], F32, kind="ExternalInput")
    id_h = nc.dram_tensor("ident", [128, 128], BF16, kind="ExternalInput")

    kv_cc_in = nc.dram_tensor("kv_cc_in", [128, D], F32, kind="Internal")
    kv_cc_out = nc.dram_tensor("kv_cc_out", [128, D], F32, kind="Internal")
    z_s = nc.dram_tensor("z_s", [NK, 128, T_OWN], BF16, kind="Internal")
    wqk_bf_s = nc.dram_tensor("wqk_bf_s", [NK, 128, D], BF16, kind="Internal")
    wv_bf_s = nc.dram_tensor("wv_bf_s", [NK, 128, D], BF16, kind="Internal")
    dw_bf_s = nc.dram_tensor("dw_bf_s", [NI, 128, D], BF16, kind="Internal")
    q_s = nc.dram_tensor("q_s", [T_OWN, D], BF16, kind="Internal")
    u_s = nc.dram_tensor("u_s", [NK, 128, T_OWN], F32, kind="Internal")
    hdn_s = nc.dram_tensor("hdn_s", [NI, 128, T_OWN], BF16, kind="Internal")
    ffn_s = nc.dram_tensor("ffn_s", [T_OWN, D], F32, kind="Internal")

    out_h = nc.dram_tensor("out", [T_OWN, D], F32, kind="ExternalOutput")

    off = [BASE]
    TOP = 229376 - 32

    def sb(name, shape, dtype, n=1, at=None):
        o = off if at is None else at
        per = int(np.prod(shape[1:])) * mybir.dt.size(dtype)
        per = (per + 31) // 32 * 32
        hs = []
        for i in range(n):
            hs.append(nc.alloc_sbuf_tensor_at(f"{name}{i if n > 1 else ''}",
                                              shape, dtype, offset=o[0]))
            o[0] += per
        assert o[0] <= TOP, f"sbuf overflow at {name}: {o[0]}"
        return hs if n > 1 else hs[0]

    ident = sb("ident_sb", [128, 128], BF16)
    ones_row_bf = sb("ones_row_bf", [1, 128], BF16)
    ones_col_bf = sb("ones_col_bf", [128, 1], BF16)
    ones_row_f = sb("ones_row_f", [1, 128], F32)
    cells = sb("cells", [128, 8], F32)
    bqk_row = sb("bqk_row", [1, D], BF16)
    bv_row = sb("bv_row", [1, D], BF16)
    bu_fm = sb("bu_fm", [128, NK], F32)
    lng_fm = sb("lng_fm", [128, NK], F32)
    lnb_fm = sb("lnb_fm", [128, NK], F32)
    gs_fm = sb("gs_fm", [128, NI], F32)
    us_fm = sb("us_fm", [128, NI], F32)
    kv_sb = sb("kv_sb", [128, D], F32)
    kv_bf = sb("kv_bf", [128, D], BF16)
    smalls = sb("smalls", [128, 96], F32)
    mean_row = sb("mean_row", [1, 512], F32)
    rstd_row = sb("rstd_row", [1, 512], F32)

    PB = off[0]

    XN_OFF = TOP - NK * T_OWN * 2
    xn_own = nc.alloc_sbuf_tensor_at("xn_own", [128, NK * T_OWN], BF16,
                                     offset=XN_OFF)
    xn_v = xn_own.rearrange("p (k t) -> p k t", k=NK)

    def region(tensors, top=None):
        o = [PB]
        out = {}
        for name, shape, dtype, n in tensors:
            out[name] = sb(name, shape, dtype, n, at=o)
        if top is not None:
            assert o[0] <= top, f"region overflow: {o[0]} > {top}"
        return out

    r0 = region([
        ("xbuf", [128, D], F32, 3),
        ("xn_tm", [128, D], BF16, 3),
        ("pc0_f32", [128, D], F32, 1),
        ("pc0_bf", [128, D], BF16, 1),
    ], top=XN_OFF)
    r1 = region([
        ("wq_bf", [128, NK * 512], BF16, 1),
        ("wv_bf", [128, NK * 512], BF16, 1),
        ("wstage", [128, 512], F32, 2),
        ("q_tm", [128, 512], BF16, 2),
        ("v_tm", [128, 512], BF16, 2),
        ("e_buf", [128, 512], F32, 2),
        ("r_buf", [128, 512], F32, 2),
    ], top=XN_OFF)
    row_stage = r0["xbuf"][0][0:1, :]
    r2 = region([
        ("wstg2", [128, D], F32, 2),
        ("gw_bf", [128, D], BF16, 2),
        ("uw_bf", [128, D], BF16, 2),
        ("sg_buf", [128, 512], F32, 2),
        ("u_raw", [128, 512], F32, 2),
        ("hdn_buf", [128, 512], BF16, 2),
        ("u_buf", [128, 512], F32, 2),
        ("pc3_bf", [128, 512], BF16, 1),
    ], top=XN_OFF)
    r4 = region([
        ("dw_bf", [128, NI * 1024], BF16, 1),
        ("hdn_t", [128, NI * 128], BF16, 2),
        ("ffn_buf", [128, 1024], F32, 2),
        ("dstg", [128, 1024], F32, 2),
        ("ds_repl", [128, D], F32, 1),
    ])
    ds_repl = r4["ds_repl"]
    r5 = region([
        ("qtile", [128, D], BF16, 2),
        ("qfm", [128, NK * 256], BF16, 2),
        ("outst", [128, NK * 256], F32, 2),
        ("sq_bf", [128, NK * 256], BF16, 2),
        ("kvscA", [128, NK * 64], F32, 1),
        ("kvscB", [128, NK * 32], F32, 1),
        ("kvs_f", [128, NK], F32, 1),
        ("kvs_bf", [128, NK], BF16, 1),
        ("zfm", [128, NK * 256], BF16, 2),
        ("u_tile", [128, NK * 256], F32, 2),
        ("mean_sb", [128, 256], F32, 2),
        ("rstd_sb", [128, 256], F32, 2),
        ("t1_b", [128, 256], F32, 2),
        ("t2_b", [128, 256], F32, 2),
        ("t3_b", [128, 256], F32, 2),
    ])
    r6 = region([
        ("wo_bf", [128, NK * D], BF16, 1),
        ("wstg6", [128, D], F32, 2),
        ("ztile", [128, NK * 128], BF16, 2),
        ("x_t", [128, D], F32, 2),
        ("ffn_t", [128, D], F32, 2),
        ("xf", [128, D], F32, 2),
        ("outb", [128, D], F32, 2),
        ("bo_row", [1, D], BF16, 1),
    ])
    bo_row = r6["bo_row"]

    pbA = nc.alloc_psum_tensor("pbA", [128, 2048], F32)
    pbt = nc.alloc_psum_tensor("pbt", [128, 2048], BF16)
    pb6 = nc.alloc_psum_tensor("pb6", [128, 512], F32)
    pb7 = nc.alloc_psum_tensor("pb7", [128, 512], F32)

    class Sem:
        def __init__(s, name):
            s.h = nc.alloc_semaphore(name)
            s.n = 0

    def sems(prefix, n):
        return [Sem(f"{prefix}{i}") for i in range(n)]

    sPE, sV, sS, sPool, sBAR = (Sem(x) for x in ("sPE", "sV", "sS", "sPool", "sBAR"))
    sC = Sem("sC")
    sX = sems("sX", 3)
    sWS = sems("sWS", 2)
    sW2 = sems("sW2", 2)
    sDS = sems("sDS", 2)
    sHT = sems("sHT", 2)
    sQT = sems("sQT", 2)
    sUT = sems("sUT", 2)
    sXT = sems("sXT", 2)
    sFT = sems("sFT", 2)
    sSQ = sems("sSQ", 2)
    sSU = sems("sSU", 2)
    sSH = sems("sSH", 2)
    sSF = sems("sSF", 2)
    sSO = sems("sSO", 2)
    sKV = Sem("sKV")
    sCC = Sem("sCC")
    sSZ = sems("sSZ", 2)
    sZT = sems("sZT", 2)
    sPC = Sem("sPC")
    sPCS = Sem("sPCS")
    sPC3 = Sem("sPC3")
    sPS3 = Sem("sPS3")
    LOAD_SEMS = [sC, *sX, *sWS, *sW2, *sDS, *sHT, *sQT, *sUT, *sXT, *sFT, *sZT,
                 sPC, sPC3]
    STORE_SEMS = [*sSQ, *sSU, *sSH, *sSF, *sSO, sKV, *sSZ, sPCS, sPS3]
    bar_n = [0]

    SP, PE, V, S, PL = nc.sync, nc.tensor, nc.vector, nc.scalar, nc.gpsimd

    def ld(sem, dst, src):
        SP.dma_start(dst, src).then_inc(sem.h, 16)
        sem.n += 16
        return sem.n

    def st(sem, dst, src):
        PL.dma_start(dst, src).then_inc(sem.h, 16)
        sem.n += 16
        return sem.n

    def barrier():
        for q in LOAD_SEMS:
            SP.wait_ge(q.h, q.n)
        for q in STORE_SEMS:
            PL.wait_ge(q.h, q.n)
        for eng in (SP, PE, V, S, PL):
            eng.drain()
            eng.nop().then_inc(sBAR.h, 1)
        bar_n[0] += 5
        for eng in (SP, PE, V, S, PL):
            eng.wait_ge(sBAR.h, bar_n[0])

    ld(sC, ident[:], id_h[:])
    ld(sC, bu_fm[:], bu_h[:])
    ld(sC, lng_fm[:], lng_h[:])
    ld(sC, lnb_fm[:], lnb_h[:])
    ld(sC, gs_fm[:], gs_h[:])
    ld(sC, us_fm[:], us_h[:])

    for ap, val in ((ones_row_bf[:], 1.0), (ones_col_bf[:], 1.0), (ones_row_f[:], 1.0),
                    (cells[:, 0:1], EPS), (cells[:, 1:2], 1.0 / D), (cells[:, 2:3], -0.5),
                    (cells[:, 3:4], -1.0),
                    (kv_sb[:], 0.0)):
        PL.memset(ap, val).then_inc(sPool.h, 1)
        sPool.n += 1

    for bh, brow in ((bqk_h, bqk_row), (bv_h, bv_row)):
        n = ld(sC, row_stage[:], bh[:])
        V.wait_ge(sC.h, n)
        V.tensor_copy(brow[:], row_stage[:]).then_inc(sV.h, 1)
        sV.n += 1
        SP.wait_ge(sV.h, sV.n)

    def load_qv_weights(sw):
        c0 = sw * 512
        marks = [0, 0]
        for wi, (wsrc, wbf) in enumerate(((wqk_bf_s, r1["wq_bf"]),
                                          (wv_bf_s, r1["wv_bf"]))):
            dst = wbf.rearrange("p (k c) -> p k c", k=NK)
            src = wsrc.rearrange("k p c -> p k c")[:, :, c0:c0 + 512]
            for p4 in range(4):
                marks[wi] = ld(sWS[wi], dst[:, p4 * 4:(p4 + 1) * 4, :],
                               src[:, p4 * 4:(p4 + 1) * 4, :])
        return marks

    w_ready = [[0, 0]]
    barrier()

    m_xld = [0] * (NTO + 4)
    m_norm = [0] * (NTO + 4)
    m_sq = [0] * (NTO + 4)
    m_tr = [0] * (NTO + 4)
    m_cpx = [0] * (NTO + 4)
    m_projg = [0] * (NTO + 4)
    m_evS = [0] * (NTO + 4)
    m_evV = [0] * (NTO + 4)
    m_kvmm = [0] * (NTO + 4)
    m_kvadd = [0] * (NTO + 4)
    m_xnst = [0] * (NTO + 4)
    m_qst = [0] * (NTO + 4)

    def reset_marks():
        for a_ in (m_xld, m_norm, m_sq, m_tr, m_cpx, m_projg, m_evS, m_evV,
                   m_kvmm, m_kvadd, m_xnst, m_qst):
            for t in range(len(a_)):
                a_[t] = 0

    m_rstd = [0] * (NTO + 4)

    def emit_ln_A(i):
        xb = r0["xbuf"][i % 3]
        lo = (i % 3) * 32
        sm = smalls
        V.wait_ge(sX[i % 3].h, m_xld[i])
        if i >= 3:
            V.wait_ge(sS.h, m_rstd[i - 3])
        for k in range(4):
            bs_ = V.bn_stats(sm[:, lo + k * 6:lo + (k + 1) * 6],
                             xb[:, k * 512:(k + 1) * 512])
        bs_.then_inc(sV.h, 1)
        sV.n += 1
        V.wait_ge(sV.h, sV.n)
        V.bn_aggr(sm[:, lo + 24:lo + 26], sm[:, lo:lo + 24]).then_inc(sV.h, 1)
        sV.n += 1
        agg_c = sV.n
        S.wait_ge(sV.h, agg_c)
        S.activation(sm[:, lo + 26:lo + 27], sm[:, lo + 25:lo + 26], AF.Ln,
                     bias=cells[:, 0:1]).then_inc(sS.h, 1)
        sS.n += 1
        S.wait_ge(sS.h, sS.n)
        S.activation(sm[:, lo + 27:lo + 28], sm[:, lo + 26:lo + 27], AF.Exp,
                     scale=cells[:, 2:3]).then_inc(sS.h, 1)
        sS.n += 1
        S.wait_ge(sS.h, sS.n)
        S.activation(sm[:, lo + 28:lo + 29], sm[:, lo + 24:lo + 25], AF.Identity,
                     scale=sm[:, lo + 27:lo + 28]).then_inc(sS.h, 1)
        sS.n += 1
        S.wait_ge(sS.h, sS.n)
        S.activation(sm[:, lo + 28:lo + 29], sm[:, lo + 28:lo + 29], AF.Identity,
                     scale=cells[:, 3:4]).then_inc(sS.h, 1)
        sS.n += 1
        m_rstd[i] = sS.n

    def emit_ln_B(i):
        xb = r0["xbuf"][i % 3]
        xt = r0["xn_tm"][i % 3]
        lo = (i % 3) * 32
        sm = smalls
        S.wait_ge(sS.h, m_rstd[i])
        if i >= 3:
            S.wait_ge(sPE.h, m_tr[i - 3])
        S.activation(xt[:], xb[:], AF.Identity, scale=sm[:, lo + 27:lo + 28],
                     bias=sm[:, lo + 28:lo + 29]).then_inc(sS.h, 1)
        sS.n += 1
        m_norm[i] = sS.n

    def emit_transposes(i):
        xt = r0["xn_tm"][i % 3]
        PE.wait_ge(sS.h, m_norm[i])
        if i >= 1:
            PE.wait_ge(sV.h, m_cpx[i - 1])
        for k in range(NK):
            ti_ = PE.transpose(pbt[:, k * 128:(k + 1) * 128],
                               xt[:, k * 128:(k + 1) * 128], ident[:])
        ti_.then_inc(sPE.h, 1)
        sPE.n += 1
        m_tr[i] = sPE.n
        V.wait_ge(sPE.h, m_tr[i])
        V.tensor_copy(xn_v[:, :, i * 128:(i + 1) * 128],
                      pbt.rearrange("p (k t) -> p k t", k=NK)).then_inc(sV.h, 1)
        sV.n += 1
        m_cpx[i] = sV.n

    def emit_tile_proj(i, sw):
        c0 = sw * 512
        qb = r1["q_tm"][i % 2]
        vb = r1["v_tm"][i % 2]
        po = (i % 2) * 512
        qsl = pbA[:, po:po + 512]
        vsl = pbA[:, 1024 + po:1024 + po + 512]
        if i == 0:
            PE.wait_ge(sWS[0].h, w_ready[0][0])
            PE.wait_ge(sWS[1].h, w_ready[0][1])
        if i >= 2:
            PE.wait_ge(sS.h, m_evS[i - 2])
            PE.wait_ge(sV.h, m_evV[i - 2])
        for k in range(NK):
            lhs = xn_v[:, k, i * 128:(i + 1) * 128]
            PE.matmul(qsl, lhs, r1["wq_bf"][:, k * 512:(k + 1) * 512],
                      start=(k == 0), stop=False)
            PE.matmul(vsl, lhs, r1["wv_bf"][:, k * 512:(k + 1) * 512],
                      start=(k == 0), stop=False)
        PE.matmul(qsl, ones_row_bf[:], bqk_row[0:1, c0:c0 + 512],
                  start=False, stop=True)
        mm_ = PE.matmul(vsl, ones_row_bf[:], bv_row[0:1, c0:c0 + 512],
                        start=False, stop=True)
        mm_.then_inc(sPE.h, 1)
        sPE.n += 1
        m_projg[i] = sPE.n
        S.wait_ge(sPE.h, m_projg[i])
        if i >= 2:
            S.wait_ge(sV.h, m_evV[i - 2])
            S.wait_ge(sPE.h, m_kvmm[i - 2])
        S.activation(r1["e_buf"][i % 2][:], qsl, AF.Exp)
        S.activation(vb[:], vsl, AF.Silu).then_inc(sS.h, 1)
        sS.n += 1
        m_evS[i] = sS.n
        V.wait_ge(sS.h, m_evS[i])
        if i >= 2:
            V.wait_ge(sPE.h, m_kvmm[i - 2])
            V.wait_ge(sSQ[i % 2].h, m_qst[i - 2])
        V.tensor_scalar(r1["r_buf"][i % 2][:], qsl, 0.0, 0.0,
                        ALU.max).then_inc(sV.h, 1)
        sV.n += 1
        V.wait_ge(sV.h, sV.n)
        V.scalar_tensor_tensor(qb[:], r1["e_buf"][i % 2][:], -1.0,
                               r1["r_buf"][i % 2][:], ALU.add,
                               ALU.min).then_inc(sV.h, 1)
        sV.n += 1
        m_evV[i] = sV.n

    def emit_kv(i, sw):
        qb = r1["q_tm"][i % 2]
        vb = r1["v_tm"][i % 2]
        kvp = pb6 if i % 2 == 0 else pb7
        PE.wait_ge(sV.h, m_evV[i])
        PE.wait_ge(sS.h, m_evS[i])
        if i >= 2:
            PE.wait_ge(sV.h, m_kvadd[i - 2])
        for j in range(4):
            km_ = PE.matmul(kvp[:, j * 128:(j + 1) * 128],
                            qb[:, j * 128:(j + 1) * 128],
                            vb[:, j * 128:(j + 1) * 128], start=True, stop=True)
        km_.then_inc(sPE.h, 1)
        sPE.n += 1
        m_kvmm[i] = sPE.n
        col0 = sw * 512
        V.wait_ge(sPE.h, m_kvmm[i])
        if i >= 1:
            V.wait_ge(sV.h, m_kvadd[i - 1])
        V.tensor_tensor(kv_sb[:, col0:col0 + 512], kv_sb[:, col0:col0 + 512],
                        kvp[:], ALU.add).then_inc(sV.h, 1)
        sV.n += 1
        m_kvadd[i] = sV.n

    pc0_cast = [0]
    pc0_st = [0]

    def pre_cast_qv(u):
        wsrc = wqk_h if u < NK else wv_h
        wdst = wqk_bf_s if u < NK else wv_bf_s
        k = u % NK
        SP.wait_ge(sV.h, pc0_cast[0])
        n = ld(sPC, r0["pc0_f32"][:], wsrc[k])
        V.wait_ge(sPC.h, n)
        V.wait_ge(sPCS.h, pc0_st[0])
        V.tensor_copy(r0["pc0_bf"][:], r0["pc0_f32"][:]).then_inc(sV.h, 1)
        sV.n += 1
        pc0_cast[0] = sV.n
        PL.wait_ge(sV.h, sV.n)
        pc0_st[0] = st(sPCS, wdst[k], r0["pc0_bf"][:])

    for i0 in range(3):
        m_xld[i0] = ld(sX[i0], r0["xbuf"][i0][:], x_h[i0 * 128:(i0 + 1) * 128, :])
    for i in range(NTO):
        emit_ln_A(i)
        if i >= 1:
            emit_ln_B(i - 1)
            emit_transposes(i - 1)
            pre_cast_qv(i - 1)
            j = i + 2
            if j < NTO:
                SP.wait_ge(sS.h, m_norm[i - 1])
                m_xld[j] = ld(sX[j % 3], r0["xbuf"][j % 3][:],
                              x_h[j * 128:(j + 1) * 128, :])
    emit_ln_B(NTO - 1)
    emit_transposes(NTO - 1)
    pre_cast_qv(NTO - 1)
    barrier()

    for sw in range(4):
        w_ready[0] = load_qv_weights(sw)
        reset_marks()
        for i in range(NTO):
            emit_tile_proj(i, sw)
            if i >= 1:
                emit_kv(i - 1, sw)
            if i > 0:
                PL.wait_ge(sV.h, m_evV[i - 1])
                m_qst[i - 1] = st(sSQ[(i - 1) % 2],
                                  q_s[(i - 1) * 128:i * 128, sw * 512:(sw + 1) * 512],
                                  r1["q_tm"][(i - 1) % 2][:])
        emit_kv(NTO - 1, sw)
        PL.wait_ge(sV.h, m_evV[NTO - 1])
        m_qst[NTO - 1] = st(sSQ[(NTO - 1) % 2],
                            q_s[(NTO - 1) * 128:NTO * 128, sw * 512:(sw + 1) * 512],
                            r1["q_tm"][(NTO - 1) % 2][:])
        barrier()

    st(sKV, kv_cc_in[:], kv_sb[:])
    PL.wait_ge(sKV.h, sKV.n)
    PL.collective_compute(
        "AllReduce", ALU.add, replica_groups=RG,
        ins=[kv_cc_in.ap().opt()], outs=[kv_cc_out.ap().opt()],
    ).then_inc(sCC.h)
    sCC.n = 1

    TG2 = T_OWN // 512
    wu_pe_done = [0, 0]
    wu_cast = [0, 0]
    u_evac = {}
    u_store = {}
    w2_free = [0]
    stg2_free = [0, 0]
    for m in range(NK):
        j = m % 2
        SP.wait_ge(sV.h, stg2_free[j])
        if m >= 2:
            SP.wait_ge(sPE.h, wu_pe_done[j])
        n = ld(sW2[j], r2["wstg2"][j].rearrange("p (c m) -> p c m", c=NK),
               wu_h[m].rearrange("(c p) m -> p c m", p=128))
        V.wait_ge(sW2[j].h, n)
        V.tensor_copy(r2["uw_bf"][j][:], r2["wstg2"][j][:]).then_inc(sV.h, 1)
        sV.n += 1
        wu_cast[j] = sV.n
        stg2_free[j] = sV.n
        PE.wait_ge(sV.h, wu_cast[j])
        for tg in range(TG2):
            u_idx = m * TG2 + tg
            psl = pbA[:, (u_idx % 2) * 512:(u_idx % 2) * 512 + 512]
            prev = u_idx - 2
            if prev >= 0:
                PE.wait_ge(sS.h, u_evac.get(prev, 0))
            for k in range(NK):
                um_ = PE.matmul(psl, r2["uw_bf"][j][:, k * 128:(k + 1) * 128],
                                xn_own[:, k * T_OWN + tg * 512:k * T_OWN + tg * 512 + 512],
                                start=(k == 0), stop=(k == NK - 1))
            um_.then_inc(sPE.h, 1)
            sPE.n += 1
            if tg == TG2 - 1:
                wu_pe_done[j] = sPE.n
            S.wait_ge(sPE.h, sPE.n)
            if prev >= 0:
                S.wait_ge(sSU[u_idx % 2].h, u_store.get(prev, 0))
            S.activation(r2["u_buf"][u_idx % 2][:], psl, AF.Silu,
                         bias=bu_fm[:, m:m + 1]).then_inc(sS.h, 1)
            sS.n += 1
            u_evac[u_idx] = sS.n
            PL.wait_ge(sS.h, sS.n)
            u_store[u_idx] = st(
                sSU[u_idx % 2],
                u_s.rearrange("m p t -> p m t")[:, m, tg * 512:(tg + 1) * 512],
                r2["u_buf"][u_idx % 2][:])

    barrier()

    g_evac = {}
    h_evac = {}
    h_store = {}
    w_pe_done = [0, 0]
    stg_free = [0, 0]
    cast_done = [0] * (NI + 1)
    gw_n = [0] * (NI + 1)
    uw_n = [0] * (NI + 1)

    def p3_ld_gw(i):
        j = i % 2
        SP.wait_ge(sV.h, stg_free[j])
        gw_n[i] = ld(sW2[j], r2["wstg2"][j].rearrange("p (c m) -> p c m", c=NK),
                     gw_h[i].rearrange("(c p) m -> p c m", p=128))

    def p3_cast_gw(i):
        j = i % 2
        V.wait_ge(sW2[j].h, gw_n[i])
        if i >= 2:
            V.wait_ge(sPE.h, w_pe_done[j])
        V.tensor_copy(r2["gw_bf"][j][:], r2["wstg2"][j][:]).then_inc(sV.h, 1)
        sV.n += 1

    def p3_ld_uw(i):
        j = i % 2
        SP.wait_ge(sV.h, sV.n)
        uw_n[i] = ld(sW2[j], r2["wstg2"][j].rearrange("p (c m) -> p c m", c=NK),
                     uw_h[i].rearrange("(c p) m -> p c m", p=128))

    def p3_cast_uw(i):
        j = i % 2
        V.wait_ge(sW2[j].h, uw_n[i])
        V.tensor_copy(r2["uw_bf"][j][:], r2["wstg2"][j][:]).then_inc(sV.h, 1)
        sV.n += 1
        stg_free[j] = sV.n
        cast_done[i] = sV.n

    pc3_cast = [0]
    pc3_st = [0]

    def pre_cast_dw(u):
        i_dw, q = u // 4, u % 4
        SP.wait_ge(sV.h, pc3_cast[0])
        n = ld(sPC3, r2["u_buf"][0][:], dw_h[i_dw, :, q * 512:(q + 1) * 512])
        V.wait_ge(sPC3.h, n)
        V.wait_ge(sPS3.h, pc3_st[0])
        V.tensor_copy(r2["pc3_bf"][:], r2["u_buf"][0][:]).then_inc(sV.h, 1)
        sV.n += 1
        pc3_cast[0] = sV.n
        PL.wait_ge(sV.h, sV.n)
        pc3_st[0] = st(sPS3, dw_bf_s[i_dw, :, q * 512:(q + 1) * 512],
                       r2["pc3_bf"][:])

    p3_ld_gw(0)
    p3_cast_gw(0)
    p3_ld_uw(0)
    p3_cast_uw(0)
    for i in range(NI):
        j = i % 2
        PE.wait_ge(sV.h, cast_done[i])
        for tg in range(TG2):
            if i + 1 < NI:
                if tg == 0:
                    p3_ld_gw(i + 1)
                elif tg == 1:
                    p3_cast_gw(i + 1)
                elif tg == 2:
                    p3_ld_uw(i + 1)
                elif tg == 3:
                    p3_cast_uw(i + 1)
            if tg >= 4:
                pre_cast_dw(i * 4 + (tg - 4))
            f_idx = i * TG2 + tg
            po = (f_idx % 2) * 512
            gsl = pbA[:, po:po + 512]
            usl = pbA[:, 1024 + po:1024 + po + 512]
            prev = f_idx - 2
            if prev >= 0:
                PE.wait_ge(sS.h, g_evac.get(prev, 0))
            for k in range(NK):
                rhs = xn_own[:, k * T_OWN + tg * 512:k * T_OWN + tg * 512 + 512]
                PE.matmul(gsl, r2["gw_bf"][j][:, k * 128:(k + 1) * 128], rhs,
                          start=(k == 0), stop=(k == NK - 1))
                fm_ = PE.matmul(usl, r2["uw_bf"][j][:, k * 128:(k + 1) * 128], rhs,
                                start=(k == 0), stop=(k == NK - 1))
            fm_.then_inc(sPE.h, 1)
            sPE.n += 1
            if tg == TG2 - 1:
                w_pe_done[j] = sPE.n
            S.wait_ge(sPE.h, sPE.n)
            if prev >= 0:
                S.wait_ge(sV.h, h_evac.get(prev, 0))
            S.activation(r2["sg_buf"][f_idx % 2][:], gsl, AF.Silu,
                         scale=gs_fm[:, i:i + 1])
            S.activation(r2["u_raw"][f_idx % 2][:], usl,
                         AF.Identity).then_inc(sS.h, 1)
            sS.n += 1
            g_evac[f_idx] = sS.n
            V.wait_ge(sS.h, sS.n)
            if prev >= 0:
                V.wait_ge(sSH[f_idx % 2].h, h_store.get(prev, 0))
            V.scalar_tensor_tensor(r2["hdn_buf"][f_idx % 2][:],
                                   r2["u_raw"][f_idx % 2][:], us_fm[:, i:i + 1],
                                   r2["sg_buf"][f_idx % 2][:], ALU.mult,
                                   ALU.mult).then_inc(sV.h, 1)
            sV.n += 1
            h_evac[f_idx] = sV.n
            PL.wait_ge(sV.h, sV.n)
            h_store[f_idx] = st(
                sSH[f_idx % 2],
                hdn_s.rearrange("i p t -> p i t")[:, i, tg * 512:(tg + 1) * 512],
                r2["hdn_buf"][f_idx % 2][:])

    barrier()

    n_ds = ld(sC, ds_repl[:], ds_h[:])
    V.wait_ge(sC.h, n_ds)
    dwv = dw_bf_s.rearrange("i p c -> p i c")
    for half in range(2):
        dst_dw = r4["dw_bf"].rearrange("p (i c) -> p i c", i=NI)
        n_dw = 0
        for g in range(11):
            i0, i1 = g * 4, min((g + 1) * 4, NI)
            n_dw = ld(sDS[g % 2], dst_dw[:, i0:i1, :],
                      dwv[:, i0:i1, half * 1024:(half + 1) * 1024])
        PE.wait_ge(sDS[0].h, sDS[0].n)
        PE.wait_ge(sDS[1].h, sDS[1].n)

        p4_mm = {}
        f_evac = {}
        f_store = {}
        for t in range(NTO):
            j = t % 2
            if t >= 2:
                SP.wait_ge(sPE.h, p4_mm.get(t - 2, 0))
            nlast = 0
            hsrc = hdn_s.rearrange("i p t -> p i t")[:, :, t * 128:(t + 1) * 128]
            hdst = r4["hdn_t"][j].rearrange("p (i t) -> p i t", i=NI)
            for piece in range(11):
                i0, i1 = piece * 4, min((piece + 1) * 4, NI)
                nlast = ld(sHT[j], hdst[:, i0:i1, :], hsrc[:, i0:i1, :])
            PE.wait_ge(sHT[j].h, nlast)
            b0 = (t % 2) * 1024
            if t >= 2:
                PE.wait_ge(sV.h, f_evac.get(t - 2, 0))
            for i in range(NI):
                lhs = r4["hdn_t"][j][:, i * 128:(i + 1) * 128]
                PE.matmul(pbA[:, b0:b0 + 512], lhs,
                          r4["dw_bf"][:, i * 1024:i * 1024 + 512],
                          start=(i == 0), stop=(i == NI - 1))
                dm_ = PE.matmul(pbA[:, b0 + 512:b0 + 1024], lhs,
                                r4["dw_bf"][:, i * 1024 + 512:i * 1024 + 1024],
                                start=(i == 0), stop=(i == NI - 1))
            dm_.then_inc(sPE.h, 1)
            sPE.n += 1
            p4_mm[t] = sPE.n
            V.wait_ge(sPE.h, sPE.n)
            V.wait_ge(sSF[j].h, f_store.get(t - 2, 0))
            V.tensor_tensor(r4["ffn_buf"][j][:], pbA[:, b0:b0 + 1024],
                            ds_repl[:, half * 1024:(half + 1) * 1024],
                            ALU.mult).then_inc(sV.h, 1)
            sV.n += 1
            f_evac[t] = sV.n
            PL.wait_ge(sV.h, sV.n)
            f_store[t] = st(sSF[j],
                            ffn_s[t * 128:(t + 1) * 128, half * 1024:(half + 1) * 1024],
                            r4["ffn_buf"][j][:])
        barrier()

    SP.wait_ge(sCC.h, 1)
    n_kv = ld(sC, kv_sb[:], kv_cc_out[:])
    S.wait_ge(sC.h, n_kv)
    S.activation(kv_bf[:], kv_sb[:], AF.Identity).then_inc(sS.h, 1)
    sS.n += 1

    V.wait_ge(sC.h, n_kv)
    kv3 = kv_sb.rearrange("p (m e) -> p m e", m=NK)
    scA = r5["kvscA"].rearrange("p (m e) -> p m e", m=NK)
    scB = r5["kvscB"].rearrange("p (m e) -> p m e", m=NK)
    tree = [(scA, kv3, kv3, 64), (scB, scA, scA, 32), (scA, scB, scB, 16),
            (scB, scA, scA, 8), (scA, scB, scB, 4), (scB, scA, scA, 2),
            (r5["kvs_f"].rearrange("p (m o) -> p m o", o=1), scB, scB, 1)]
    for dst, s0, s1, w in tree:
        V.tensor_tensor(dst[:, :, 0:w], s0[:, :, 0:w], s1[:, :, w:2 * w],
                        ALU.add).then_inc(sV.h, 1)
        sV.n += 1
        V.wait_ge(sV.h, sV.n)
    S.wait_ge(sV.h, sV.n)
    S.activation(r5["kvs_bf"][:], r5["kvs_f"][:], AF.Identity).then_inc(sS.h, 1)
    sS.n += 1
    kvs_ready = sS.n

    NTG = T_OWN // 256
    uv_h = u_s.rearrange("m p t -> p m t")
    zv_h = z_s.rearrange("k p t -> p k t")
    trp = [0, 0]
    pbt_ev = [0]
    outmm_last = [0] * (NTG + 2)
    stats_mm = [0] * (NTG + 2)
    mean_mm = [0] * (NTG + 2)
    mbc_mm = [0] * (NTG + 2)
    mean_ev = [0] * (NTG + 2)
    mr_mark = [0] * (NTG + 2)
    rr_mark = [0] * (NTG + 2)
    bc7_mm = [0] * (NTG + 2)
    bc_ev = [0] * (NTG + 2)
    t1_last = [0] * (NTG + 2)
    t2_last = [0] * (NTG + 2)
    z_g = [0] * (NTG + 2)
    z_st = [0, 0]
    u_ld = [0, 0]
    q_ld = [0, 0]
    oevS = [0] * 4
    oevV = [0] * 4
    tt1, tt2, tt3, ttz = {}, {}, {}, {}

    evac_arr = [0] * (NTG + 2)

    def p5a_A1(tg):
        p = tg % 2
        t0 = tg * 256
        p = tg % 2
        t0 = tg * 256
        pc = p * 256
        for ti in range(2):
            SP.wait_ge(sPE.h, trp[ti])
            q_ld[ti] = ld(sQT[ti], r5["qtile"][ti][:],
                          q_s[(tg * 2 + ti) * 128:(tg * 2 + ti + 1) * 128, :])
        if tg >= 2:
            SP.wait_ge(sV.h, z_g[tg - 2])
        dst_u = r5["u_tile"][p].rearrange("p (m t) -> p m t", m=NK)
        for p4 in range(4):
            n_u = ld(sUT[p], dst_u[:, p4 * 4:(p4 + 1) * 4, :],
                     uv_h[:, p4 * 4:(p4 + 1) * 4, t0:t0 + 256])
        u_ld[p] = n_u
        for ti in range(2):
            PE.wait_ge(sQT[ti].h, q_ld[ti])
            PE.wait_ge(sS.h, pbt_ev[0])
            for k in range(NK):
                tq_ = PE.transpose(pbt[:, k * 128:(k + 1) * 128],
                                   r5["qtile"][ti][:, k * 128:(k + 1) * 128],
                                   ident[:])
            tq_.then_inc(sPE.h, 1)
            sPE.n += 1
            trp[ti] = sPE.n
            S.wait_ge(sPE.h, trp[ti])
            if ti == 0 and tg >= 2:
                S.wait_ge(sPE.h, outmm_last[tg - 2])
            S.activation(r5["qfm"][p][:, ti * D:(ti + 1) * D], pbt[:],
                         AF.Identity).then_inc(sS.h, 1)
            sS.n += 1
            pbt_ev[0] = sS.n
        qfm_ready = sS.n
        PE.wait_ge(sS.h, qfm_ready)
        if tg == 0:
            PE.wait_ge(sS.h, kvs_ready)
        if tg >= 1:
            PE.wait_ge(sS.h, mean_ev[tg - 1])
            PE.wait_ge(sV.h, mr_mark[tg - 1])
        for ti in range(2):
            for m in range(NK):
                mn_ = PE.matmul(pb6[0:1, pc + ti * 128:pc + ti * 128 + 128],
                                r5["kvs_bf"][:, m:m + 1],
                                r5["qfm"][p][:, ti * D + m * 128:ti * D + (m + 1) * 128],
                                start=(m == 0), stop=(m == NK - 1))
        mn_.then_inc(sPE.h, 1)
        sPE.n += 1
        mean_mm[tg] = sPE.n
        mr = mean_row[0:1, pc:pc + 256]
        V.wait_ge(sPE.h, mean_mm[tg])
        if tg >= 2:
            V.wait_ge(sPE.h, mbc_mm[tg - 2])
        V.tensor_scalar_mul(mr, pb6[0:1, pc:pc + 256], 1.0 / D).then_inc(sV.h, 1)
        sV.n += 1
        mr_mark[tg] = sV.n

    def p5a_A2(tg):
        p = tg % 2
        t0 = tg * 256
        pc = p * 256
        mr = mean_row[0:1, pc:pc + 256]
        for mp in range(NK // 2):
            slot = p * 2 + mp % 2
            PE.wait_ge(sS.h, oevS[slot])
            for h in range(2):
                m = mp * 2 + h
                for ti in range(2):
                    psl = pbA[:, slot * 512 + h * 256 + ti * 128:
                              slot * 512 + h * 256 + ti * 128 + 128]
                    PE.matmul(psl, kv_bf[:, m * 128:(m + 1) * 128],
                              r5["qfm"][p][:, ti * D + m * 128:ti * D + (m + 1) * 128],
                              start=True, stop=True).then_inc(sPE.h, 1)
                    sPE.n += 1
            S.wait_ge(sPE.h, sPE.n)
            if mp == 0 and tg >= 2:
                S.wait_ge(sPE.h, stats_mm[tg - 2])
                S.wait_ge(sV.h, z_g[tg - 2])
            pp = pbA[:, slot * 512:slot * 512 + 512]
            S.activation(r5["outst"][p][:, mp * 512:(mp + 1) * 512], pp,
                         AF.Identity)
            S.activation(r5["sq_bf"][p][:, mp * 512:(mp + 1) * 512], pp,
                         AF.Square).then_inc(sS.h, 1)
            sS.n += 1
            oevS[slot] = sS.n
        outmm_last[tg] = sPE.n
        evac_arr[tg] = sS.n
        PE.wait_ge(sV.h, mr_mark[tg])
        PE.matmul(pb6[:, pc:pc + 256], ones_row_f[:], mr,
                  start=True, stop=True).then_inc(sPE.h, 1)
        sPE.n += 1
        mbc_mm[tg] = sPE.n
        S.wait_ge(sPE.h, mbc_mm[tg])
        if tg >= 2:
            S.wait_ge(sV.h, z_g[tg - 2])
        S.activation(r5["mean_sb"][p][:], pb6[:, pc:pc + 256],
                     AF.Identity).then_inc(sS.h, 1)
        sS.n += 1
        mean_ev[tg] = sS.n


    def p5a_B1(tg):
        p = tg % 2
        t0 = tg * 256
        pc = p * 256
        PE.wait_ge(sS.h, evac_arr[tg])
        if tg >= 1:
            PE.wait_ge(sS.h, bc_ev[tg - 1])
            PE.wait_ge(sV.h, rr_mark[tg - 1])
        for m in range(NK):
            sm_ = PE.matmul(pb7[0:1, pc:pc + 256], ones_col_bf[:],
                            r5["sq_bf"][p][:, m * 256:(m + 1) * 256],
                            start=(m == 0), stop=(m == NK - 1))
        sm_.then_inc(sPE.h, 1)
        sPE.n += 1
        stats_mm[tg] = sPE.n
        mr = mean_row[0:1, pc:pc + 256]
        rr = rstd_row[0:1, pc:pc + 256]
        V.wait_ge(sPE.h, stats_mm[tg])
        V.wait_ge(sV.h, mr_mark[tg])
        if tg >= 2:
            V.wait_ge(sPE.h, bc7_mm[tg - 2])
        V.tensor_tensor(rr, mr, mr, ALU.mult).then_inc(sV.h, 1)
        sV.n += 1
        V.wait_ge(sV.h, sV.n)
        V.scalar_tensor_tensor(rr, pb7[0:1, pc:pc + 256], 1.0 / D, rr,
                               ALU.mult, ALU.subtract).then_inc(sV.h, 1)
        sV.n += 1
        rr_mark[tg] = sV.n
        S.wait_ge(sV.h, sV.n)
        S.activation(rr, rr, AF.Ln, bias=cells[0:1, 0:1]).then_inc(sS.h, 1)
        sS.n += 1
        S.wait_ge(sS.h, sS.n)
        S.activation(rr, rr, AF.Exp, scale=cells[0:1, 2:3]).then_inc(sS.h, 1)
        sS.n += 1
        rows_c = sS.n
        PE.wait_ge(sS.h, rows_c)
        PE.matmul(pb7[:, pc:pc + 256], ones_row_f[:], rr,
                  start=True, stop=True).then_inc(sPE.h, 1)
        sPE.n += 1
        bc7_mm[tg] = sPE.n
        S.wait_ge(sPE.h, bc7_mm[tg])
        if tg >= 2:
            S.wait_ge(sV.h, z_g[tg - 2])
        S.activation(r5["rstd_sb"][p][:], pb7[:, pc:pc + 256],
                     AF.Identity).then_inc(sS.h, 1)
        sS.n += 1
        bc_ev[tg] = sS.n

    def p5a_B2(tg):
        p = tg % 2
        t0 = tg * 256
        pc = p * 256
        for mp in range(NK // 2):
            ms = (2 * mp, 2 * mp + 1)
            marks = {}
            if mp == 0:
                V.wait_ge(sS.h, bc_ev[tg])
                V.wait_ge(sUT[p].h, u_ld[p])
                V.wait_ge(sSZ[p].h, z_st[p])
            for m in ms:
                V.tensor_tensor(r5["t1_b"][m % 2][:],
                                r5["outst"][p][:, m * 256:(m + 1) * 256],
                                r5["mean_sb"][p][:],
                                ALU.subtract).then_inc(sV.h, 1)
                sV.n += 1
                marks[("t1", m)] = sV.n
            for m in ms:
                V.wait_ge(sV.h, marks[("t1", m)])
                V.scalar_tensor_tensor(r5["t2_b"][m % 2][:], r5["t1_b"][m % 2][:],
                                       lng_fm[:, m:m + 1], r5["rstd_sb"][p][:],
                                       ALU.mult, ALU.mult).then_inc(sV.h, 1)
                sV.n += 1
                marks[("t2", m)] = sV.n
            for m in ms:
                V.wait_ge(sV.h, marks[("t2", m)])
                V.scalar_tensor_tensor(r5["zfm"][p][:, m * 256:(m + 1) * 256],
                                       r5["t2_b"][m % 2][:],
                                       lnb_fm[:, m:m + 1],
                                       r5["u_tile"][p][:, m * 256:(m + 1) * 256],
                                       ALU.add, ALU.mult).then_inc(sV.h, 1)
                sV.n += 1
        t1_last[tg] = sV.n
        t2_last[tg] = sV.n
        z_g[tg] = sV.n
        PL.wait_ge(sV.h, z_g[tg])
        src_z = r5["zfm"][p].rearrange("p (k t) -> p k t", k=NK)
        for p4 in range(4):
            n_z = st(sSZ[p], zv_h[:, p4 * 4:(p4 + 1) * 4, t0:t0 + 256],
                     src_z[:, p4 * 4:(p4 + 1) * 4, :])
        z_st[p] = n_z


    p5a_A1(0)
    p5a_A2(0)
    for tg in range(1, NTG):
        p5a_A1(tg)
        p5a_B1(tg - 1)
        p5a_A2(tg)
        p5a_B2(tg - 1)
    p5a_B1(NTG - 1)
    p5a_B2(NTG - 1)
    barrier()

    n_bo = ld(sC, r6["wstg6"][0][0:1, :], bo_h[:])
    V.wait_ge(sC.h, n_bo)
    V.tensor_copy(bo_row[:], r6["wstg6"][0][0:1, :]).then_inc(sV.h, 1)
    sV.n += 1
    wo_cast = [sV.n, 0]
    wo_k = [0] * NK
    for k in range(NK):
        j = k % 2
        SP.wait_ge(sV.h, wo_cast[j])
        n = ld(sWS[j], r6["wstg6"][j][:], wo_h[k])
        V.wait_ge(sWS[j].h, n)
        V.tensor_copy(r6["wo_bf"][:, k * D:(k + 1) * D],
                      r6["wstg6"][j][:]).then_inc(sV.h, 1)
        sV.n += 1
        wo_cast[j] = sV.n
        wo_k[k] = sV.n

    ot_store = [0, 0]
    xf_done = [0, 0]
    xf_rd = [0, 0]
    z_rd = [0, 0]
    half_ev_V = [0, 0]
    for t in range(NTO):
        j = t % 2
        if t >= 2:
            SP.wait_ge(sPool.h, xf_done[j])
        n_x = ld(sXT[j], r6["x_t"][j][:], x_h[t * 128:(t + 1) * 128, :])
        n_f = ld(sFT[j], r6["ffn_t"][j][:], ffn_s[t * 128:(t + 1) * 128, :])
        if t >= 2:
            SP.wait_ge(sPE.h, z_rd[j])
        zt = r6["ztile"][j].rearrange("p (k t) -> p k t", k=NK)
        for p4 in range(4):
            n_z = ld(sZT[j], zt[:, p4 * 4:(p4 + 1) * 4, :],
                     zv_h[:, p4 * 4:(p4 + 1) * 4, t * 128:(t + 1) * 128])
        PL.wait_ge(sXT[j].h, n_x)
        PL.wait_ge(sFT[j].h, n_f)
        if t >= 2:
            PL.wait_ge(sV.h, xf_rd[j])
        PL.tensor_tensor(r6["xf"][j][:], r6["x_t"][j][:], r6["ffn_t"][j][:],
                         ALU.add).then_inc(sPool.h, 1)
        sPool.n += 1
        xf_done[j] = sPool.n
        PE.wait_ge(sZT[j].h, n_z)
        for gp in range(2):
            u = (t * 2 + gp) % 2
            b0 = u * 1024
            if t * 2 + gp >= 2:
                PE.wait_ge(sV.h, half_ev_V[u])
            for k in range(NK):
                if t == 0 and gp == 0:
                    PE.wait_ge(sV.h, wo_k[k])
                lhs = zt[:, k, :]
                PE.matmul(pbA[:, b0:b0 + 512], lhs,
                          r6["wo_bf"][:, k * D + gp * 1024:k * D + gp * 1024 + 512],
                          start=(k == 0), stop=False)
                PE.matmul(pbA[:, b0 + 512:b0 + 1024], lhs,
                          r6["wo_bf"][:, k * D + gp * 1024 + 512:k * D + gp * 1024 + 1024],
                          start=(k == 0), stop=False)
            PE.matmul(pbA[:, b0:b0 + 512], ones_row_bf[:],
                      bo_row[0:1, gp * 1024:gp * 1024 + 512],
                      start=False, stop=True)
            wm_ = PE.matmul(pbA[:, b0 + 512:b0 + 1024], ones_row_bf[:],
                            bo_row[0:1, gp * 1024 + 512:gp * 1024 + 1024],
                            start=False, stop=True)
            wm_.then_inc(sPE.h, 1)
            sPE.n += 1
            if gp == 1:
                z_rd[j] = sPE.n
            V.wait_ge(sPE.h, sPE.n)
            if gp == 0:
                V.wait_ge(sPool.h, xf_done[j])
                V.wait_ge(sSO[j].h, ot_store[j])
            V.tensor_tensor(r6["outb"][j][:, gp * 1024:(gp + 1) * 1024],
                            pbA[:, b0:b0 + 1024],
                            r6["xf"][j][:, gp * 1024:(gp + 1) * 1024],
                            ALU.add).then_inc(sV.h, 1)
            sV.n += 1
            half_ev_V[u] = sV.n
            if gp == 1:
                xf_rd[j] = sV.n
        PL.wait_ge(sV.h, xf_rd[j])
        ot_store[j] = st(sSO[j], out_h[t * 128:(t + 1) * 128, :],
                         r6["outb"][j][:])

    for q in LOAD_SEMS:
        SP.wait_ge(q.h, q.n)
    for q in STORE_SEMS:
        PL.wait_ge(q.h, q.n)
    SP.wait_ge(sSO[0].h, sSO[0].n)
    SP.wait_ge(sSO[1].h, sSO[1].n)
    return nc


def kernel(L0_attn_x, L0_attn_Wqk, L0_attn_bqk, L0_attn_Wv, L0_attn_bv,
           L0_attn_Wu, L0_attn_bu, L0_attn_Wo, L0_attn_bo, L0_attn_ln_g,
           L0_attn_ln_b, L0_ffn_gate_w, L0_ffn_up_w, L0_ffn_down_w,
           L0_ffn_gate_s, L0_ffn_up_s, L0_ffn_down_s):
    f32 = np.float32
    x = np.asarray(L0_attn_x, dtype=f32).reshape(2 * T_BATCH, D)

    def fm_vec(v, nk):
        return np.ascontiguousarray(np.asarray(v, f32).reshape(nk, 128).T)

    common = {
        "wqk": np.ascontiguousarray(np.asarray(L0_attn_Wqk, f32).reshape(NK, 128, D)),
        "wv": np.ascontiguousarray(np.asarray(L0_attn_Wv, f32).reshape(NK, 128, D)),
        "wu": np.ascontiguousarray(
            np.asarray(L0_attn_Wu, f32).reshape(D, NK, 128).transpose(1, 0, 2)),
        "wo": np.ascontiguousarray(np.asarray(L0_attn_Wo, f32).reshape(NK, 128, D)),
        "gatew": np.ascontiguousarray(
            np.asarray(L0_ffn_gate_w, f32).T.reshape(D, NI, 128).transpose(1, 0, 2)),
        "upw": np.ascontiguousarray(
            np.asarray(L0_ffn_up_w, f32).T.reshape(D, NI, 128).transpose(1, 0, 2)),
        "downw": np.ascontiguousarray(
            np.asarray(L0_ffn_down_w, f32).T.reshape(NI, 128, D)),
        "bqk": np.asarray(L0_attn_bqk, f32).reshape(1, D),
        "bv": np.asarray(L0_attn_bv, f32).reshape(1, D),
        "bo": np.asarray(L0_attn_bo, f32).reshape(1, D),
        "bu": fm_vec(L0_attn_bu, NK),
        "lng": fm_vec(L0_attn_ln_g, NK),
        "lnb": fm_vec(L0_attn_ln_b, NK),
        "gs": fm_vec(L0_ffn_gate_s, NI),
        "us": fm_vec(L0_ffn_up_s, NI),
        "ds": np.ascontiguousarray(
            np.tile(np.asarray(L0_ffn_down_s, f32).reshape(1, D), (128, 1))),
        "ident": np.eye(128, dtype=ml_dtypes.bfloat16),
    }

    in_maps = []
    for c in range(8):
        b = c // 4
        o = (c % 4) * T_OWN
        x_core = np.ascontiguousarray(x[b * T_BATCH + o:b * T_BATCH + o + T_OWN])
        m = dict(common)
        m["x"] = x_core
        in_maps.append(m)

    nc = build_nc()
    res = run_bass_kernel_spmd(nc, in_maps, core_ids=list(range(8)))

    out = np.empty((2 * T_BATCH, D), dtype=np.float32)
    for c in range(8):
        b = c // 4
        o = (c % 4) * T_OWN
        out[b * T_BATCH + o:b * T_BATCH + o + T_OWN] = res.results[c]["out"]
    return out.reshape(2, T_BATCH, D)
